# version 02
# baseline (speedup 1.0000x reference)
import numpy as np
import ml_dtypes
from contextlib import ExitStack
import concourse.bass as bass
import concourse.mybir as mybir
from concourse.bass_utils import run_bass_kernel_spmd

F32 = mybir.dt.float32
BF16 = mybir.dt.bfloat16
AF = mybir.ActivationFunctionType
ALU = mybir.AluOpType
AXX = mybir.AxisListType.X
BFNP = ml_dtypes.bfloat16

P = 128
DM = 1024
DI = 2048
NST = 16
DFF = 4096
NE = 8
NCD = DM // P
NCI = DI // P
NCF = DFF // P
TW = 512
EPS = 1e-5
RDM = 1.0 / DM
RDI = 1.0 / DI


class Ctx:
    def __init__(self, nc, es):
        self.nc = nc
        self.es = es
        self.sem = {}
        self.cnt = {}

    def new(self, *names):
        for n in names:
            self.sem[n] = self.es.enter_context(self.nc.semaphore(n))
            self.cnt[n] = 0

    def mark(self, inst, name, k=1):
        inst.then_inc(self.sem[name], k)
        self.cnt[name] += k
        return self.cnt[name]

    def wait(self, eng, name, val=None):
        v = self.cnt[name] if val is None else val
        if v > 0:
            eng.wait_ge(self.sem[name], v)

    def dma_seq(self, q, stream, out, in_):
        self.wait(q, stream)
        self.mark(q.dma_start(out=out, in_=in_), stream, 16)


def build(n_own_t=8, n_ext_t=16, fw=1024, dbg=False, upto="full"):
    n_own = n_own_t * TW
    n_ext = n_ext_t * TW
    xrows = 3 + n_ext
    n_pad_t = n_ext_t - n_own_t
    half_t = n_ext_t // 2
    half = half_t * TW
    nwin = max(1, n_own // fw)
    fw = n_own // nwin
    nsub = fw // TW

    nc = bass.Bass("TRN2", target_bir_lowering=False)

    def din(name, shape, dtype=F32):
        return nc.dram_tensor(name, list(shape), dtype, kind="ExternalInput").ap()

    x_in = din("x", [xrows, DM])
    bx_mask = din("bx_mask", [NST, n_ext])
    m3_in = din("m3", [P, 3])
    ident_f = din("ident_f", [P, P])
    ident_b = din("ident_b", [P, P], BF16)
    w_in_l = din("w_in_l", [P, NCF, NCD * P])
    wd_l = din("wd_l", [P, NCI, NCI * P])
    w_out_l = din("w_out_l", [P, NCD * NCI * P])
    bt_l = din("bt_l", [P, NCI * NST])
    ct_l = din("ct_l", [NST, DI])
    a_log_r = din("a_log_r", [1, NST])
    d_vec_l = din("d_vec_l", [P, NCI])
    conv_w_l = din("conv_w_l", [P, NCI * 4])
    conv_b_l = din("conv_b_l", [P, NCI])
    bd_l = din("bd_l", [P, NCI])
    gamma_l = din("gamma_l", [P, NCD])
    beta_l = din("beta_l", [P, NCD])
    wr_l = din("wr_l", [P, NCD * NE])
    br_r = din("br_r", [1, NE])
    ones_r = din("ones_r", [1, P])
    consts_in = din("consts", [P, 8])
    w1_l = din("w1_l", [NE, NCF, P, NCD * P])
    w2_l = din("w2_l", [NE, NCF, P, NCD * P])
    w3_l = din("w3_l", [NE, DFF, DM])
    b1_l = din("b1_l", [NE, P, NCF])
    b2_l = din("b2_l", [NE, P, NCF])
    b3_l = din("b3_l", [NE, DM])

    y_out = nc.dram_tensor("y", [n_own, DM], F32, kind="ExternalOutput").ap()

    def scr(name, shape, dtype=F32):
        kind = "ExternalOutput" if dbg else "Internal"
        return nc.dram_tensor(name, list(shape), dtype, kind=kind).ap()

    xpxn_sp = scr("xpxn_sp", [P, NCD, n_own])
    xs_sp = scr("xs_sp", [P, NCI, n_own])
    xsbf_sp = scr("xsbf_sp", [P, NCI, n_ext], BF16)
    sg_sp = scr("sg_sp", [P, NCI, n_own])
    x1_sp = scr("x1_sp", [P, NCD, n_own])
    xn2_sp = scr("xn2_sp", [P, NCD, n_own], BF16)
    cw_sp = scr("cw_sp", [NE, n_own])
    if dbg:
        hs_dbg = scr("hs_dbg", [NST, half])
        da_dbg = scr("da_dbg", [NST, half])
        bxd_dbg = scr("bxd_dbg", [NST, half])

    es = ExitStack()
    cx = Ctx(nc, es)
    cx.new("sx0", "sx1", "spl", "swp", "pe", "sc", "dv", "gp",
           "sw0", "sw1", "sld0", "sld1", "sst", "sxp", "ssg", "scv",
           "sw20", "sw21", "sw30", "sw31", "sxw0", "sxw1", "scw0", "scw1")

    def sbuf(stack, name, shape, dtype=F32):
        return stack.enter_context(nc.sbuf_tensor(name, list(shape), dtype))

    idf = sbuf(es, "idf", [P, P])
    idb = sbuf(es, "idb", [P, P], BF16)
    m3_sb = sbuf(es, "m3_sb", [P, 3])
    dvec_sb = sbuf(es, "dvec_sb", [P, NCI])
    cwv_sb = sbuf(es, "cwv_sb", [P, NCI * 4])
    cb_sb = sbuf(es, "cb_sb", [P, NCI])
    bd_sb = sbuf(es, "bd_sb", [P, NCI])
    gam_sb = sbuf(es, "gam_sb", [P, NCD])
    bet_sb = sbuf(es, "bet_sb", [P, NCD])
    wr_sb = sbuf(es, "wr_sb", [P, NCD * NE])
    br_sb = sbuf(es, "br_sb", [1, NE])
    ones_sb = sbuf(es, "ones_sb", [1, P])
    consts_sb = sbuf(es, "consts_sb", [P, 8])
    for _ci, _cv in enumerate([EPS, RDI, -0.5, -1.0, 1.0, 0.5]):
        nc.const_aps.register(F32, _cv, consts_sb[:, _ci:_ci + 1])
    onesc_sb = sbuf(es, "onesc_sb", [P, 1], BF16)
    arow_sb = sbuf(es, "arow_sb", [1, NST])
    bt_sb = sbuf(es, "bt_sb", [P, NCI * NST], BF16)
    ct_sb = sbuf(es, "ct_sb", [NST, DI], BF16)
    hmid_sb = sbuf(es, "hmid_sb", [NST, 1])
    hsb_sb = sbuf(es, "hsb_sb", [NST, half], BF16)
    delta_sb = sbuf(es, "delta_sb", [1, TW])

    blk = es.enter_context(nc.Block())

    DMA_SEMS = ["spl", "swp", "sw0", "sw1", "sld0", "sld1", "sst", "sxp",
                "ssg", "scv", "sx0", "sx1", "sw20", "sw21", "sw30", "sw31",
                "sxw0", "sxw1", "scw0", "scw1"]

    def join():
        names = ["pe", "sc", "dv", "gp"]
        secs = {"pe": blk.tensor, "sc": blk.scalar, "dv": blk.vector,
                "gp": blk.gpsimd}
        def markself(e, n):
            if cx.cnt[n] > 0:
                e.wait_ge(cx.sem[n], cx.cnt[n])
            cx.mark(e.nop(), n)
        for n in names:
            secs[n](lambda e, n=n: markself(e, n))
        snap = {n: cx.cnt[n] for n in names}

        def waits(e, me):
            for n2, v in snap.items():
                if n2 != me and v > 0:
                    e.wait_ge(cx.sem[n2], v)
        for n in names:
            secs[n](lambda e, n=n: waits(e, n))

        def syncw(s):
            waits(s, "sync")
            for n2 in DMA_SEMS:
                cx.wait(s, n2)
        blk.sync(syncw)

        def gpw(g):
            for n2 in DMA_SEMS:
                cx.wait(g, n2)
        secs["gp"](gpw)

    with es:
        @blk.sync
        def _(s):
            for src, dst in [(ident_f, idf), (m3_in, m3_sb), (d_vec_l, dvec_sb),
                             (conv_w_l, cwv_sb), (conv_b_l, cb_sb), (bd_l, bd_sb),
                             (gamma_l, gam_sb), (beta_l, bet_sb), (wr_l, wr_sb),
                             (br_r, br_sb), (ones_r, ones_sb), (a_log_r, arow_sb),
                             (consts_in, consts_sb)]:
                cx.mark(s.dma_start(out=dst[:], in_=src), "spl", 16)

        @blk.gpsimd
        def _(g):
            for src, dst in [(ident_b, idb), (bt_l, bt_sb), (ct_l, ct_sb)]:
                cx.mark(g.dma_start(out=dst[:], in_=src), "swp", 16)
            cx.mark(g.memset(onesc_sb[:], 1.0), "gp")

        @blk.scalar
        def _(s):
            cx.wait(s, "spl")
            cx.mark(s.activation(arow_sb[:], arow_sb[:], AF.Exp), "sc")
            cx.wait(s, "sc")
            cx.mark(s.mul(arow_sb[:], arow_sb[:], -1.0), "sc")
        arow_ready = cx.cnt["sc"]

        s1 = ExitStack()
        xt_sb = [sbuf(s1, f"xt{k}", [P, 4, DM]) for k in range(2)]
        st_sb = sbuf(s1, "st_sb", [P, 4, 16])
        xn_sb = sbuf(s1, "xn_sb", [P, 4, DM])
        xq_sb = sbuf(s1, "xq_sb", [P, 4, DM])
        xnnb_sb = sbuf(s1, "xnnb_sb", [P, 4, DM], BF16)
        hfm_sb = sbuf(s1, "hfm_sb", [P, NCD, TW], BF16)
        xpfm_sb = sbuf(s1, "xpfm_sb", [P, NCD, TW])
        roll_sb = sbuf(s1, "roll_sb", [P, NCI, TW + 3], BF16)
        xca_sb = sbuf(s1, "xca_sb", [P, 4, TW], BF16)
        xcb_sb = sbuf(s1, "xcb_sb", [P, 4, TW], BF16)
        xss_sb = sbuf(s1, "xss_sb", [P, 4, TW])
        xsbfs_sb = sbuf(s1, "xsbfs_sb", [P, 4, TW], BF16)
        sgs_sb = sbuf(s1, "sgs_sb", [P, 4, TW])
        w1b_sb = [sbuf(s1, f"w1b{k}", [P, 4, NCD * P], BF16) for k in range(2)]
        ps1 = ExitStack()
        pbA = ps1.enter_context(nc.psum_tensor("pbA", [P, TW], F32))
        pbB = ps1.enter_context(nc.psum_tensor("pbB", [P, TW], F32))
        ptb0 = ps1.enter_context(nc.psum_tensor("ptb0", [P, 2 * TW], BF16))
        ptb1 = ps1.enter_context(nc.psum_tensor("ptb1", [P, 2 * TW], BF16))
        ptf0 = ps1.enter_context(nc.psum_tensor("ptf0", [P, TW], F32))
        ptf1 = ps1.enter_context(nc.psum_tensor("ptf1", [P, TW], F32))

        @blk.gpsimd
        def _(g):
            cx.wait(g, "spl")
            cx.mark(g.memset(roll_sb[:, :, 0:3], 0.0), "gp")

        xt_done = [0, 0]
        xpfm_ready = 0
        pe_win_end = 0
        pe_tr_end_prev = 0
        roll_ready = cx.cnt["gp"]
        sgspill_done = 0
        convspill_done = 0
        xp_spill_done = 0

        for i in range(n_ext_t):
            own = i >= n_pad_t
            slot = i % 2
            sxn = f"sx{slot}"
            njt = NCF if own else NCI
            nwb = (njt + 3) // 4

            @blk.sync
            def _(s):
                if i >= 2:
                    cx.wait(s, "dv", xt_done[slot])
                for sub in range(4):
                    r0 = 3 + i * TW + sub * P
                    cx.mark(s.dma_start(out=xt_sb[slot][:, sub, :],
                                        in_=x_in[r0:r0 + P, :]), sxn, 16)

            ntr = 32 + (32 if own else 0)
            pe_after_tr = cx.cnt["pe"] + ntr
            swbase = [cx.cnt["sw0"], cx.cnt["sw1"]]

            @blk.gpsimd
            def _(g):
                for wb in range(nwb):
                    ws = wb % 2
                    if wb >= 2:
                        cx.wait(g, "pe", pe_after_tr + 4 * (wb - 1))
                    elif i > 0:
                        cx.wait(g, "pe", pe_win_end)
                    cx.mark(g.dma_start(out=w1b_sb[ws][:],
                                        in_=w_in_l[:, wb * 4:(wb + 1) * 4, :]),
                            f"sw{ws}", 16)

            @blk.vector
            def _(v):
                cx.wait(v, sxn)
                if i == 0:
                    cx.wait(v, "spl")
                for sub in range(4):
                    cx.mark(v.reduce_sum(st_sb[:, sub, 0:1],
                                         xt_sb[slot][:, sub, :], axis=AXX), "dv")

            @blk.scalar
            def _(s):
                cx.wait(s, "dv")
                cx.wait(s, "pe", pe_tr_end_prev)
                for sub in range(4):
                    cx.mark(s.activation(xq_sb[:, sub, :],
                                         xt_sb[slot][:, sub, :], AF.Square), "sc")

            @blk.vector
            def _(v):
                cx.wait(v, "sc")
                for sub in range(4):
                    st = st_sb[:, sub, :]
                    cx.mark(v.reduce_sum(st[:, 1:2], xq_sb[:, sub, :], axis=AXX), "dv")
                    cx.wait(v, "dv")
                    cx.mark(v.tensor_scalar(st[:, 2:3], st[:, 0:1], RDM, None,
                                            ALU.mult), "dv")
                    cx.mark(v.tensor_scalar(st[:, 3:4], st[:, 1:2], RDM, None,
                                            ALU.mult), "dv")
                    cx.wait(v, "dv")
                    cx.mark(v.scalar_tensor_tensor(st[:, 4:5], st[:, 2:3], st[:, 2:3],
                                                   st[:, 3:4], ALU.mult, ALU.subtract),
                            "dv")

            @blk.scalar
            def _(s):
                cx.wait(s, "dv")
                for sub in range(4):
                    st = st_sb[:, sub, :]
                    cx.mark(s.activation(st[:, 5:6], st[:, 4:5], AF.Ln, bias=EPS,
                                         scale=-1.0), "sc")
                    cx.wait(s, "sc")
                    cx.mark(s.activation(st[:, 6:7], st[:, 5:6], AF.Exp, scale=-0.5),
                            "sc")

            @blk.vector
            def _(v):
                cx.wait(v, "sc")
                for sub in range(4):
                    st = st_sb[:, sub, :]
                    cx.mark(v.tensor_scalar(st[:, 7:8], st[:, 2:3], st[:, 6:7], -1.0,
                                            ALU.mult, ALU.mult), "dv")
                    cx.wait(v, "dv")
                    cx.mark(v.tensor_scalar(xn_sb[:, sub, :], xt_sb[slot][:, sub, :],
                                            st[:, 6:7], st[:, 7:8], ALU.mult, ALU.add),
                            "dv")
                    cx.wait(v, "dv")
                    cx.mark(v.reduce_sum(st[:, 8:9], xn_sb[:, sub, :], axis=AXX), "dv")

            @blk.scalar
            def _(s):
                cx.wait(s, "dv")
                for sub in range(4):
                    cx.mark(s.activation(xq_sb[:, sub, :], xn_sb[:, sub, :],
                                         AF.Square), "sc")

            @blk.vector
            def _(v):
                cx.wait(v, "sc")
                for sub in range(4):
                    st = st_sb[:, sub, :]
                    cx.mark(v.reduce_sum(st[:, 9:10], xq_sb[:, sub, :], axis=AXX), "dv")
                    cx.wait(v, "dv")
                    cx.mark(v.tensor_scalar(st[:, 10:11], st[:, 8:9], RDM, None,
                                            ALU.mult), "dv")
                    cx.mark(v.tensor_scalar(st[:, 11:12], st[:, 9:10], RDM, None,
                                            ALU.mult), "dv")
                    cx.wait(v, "dv")
                    cx.mark(v.scalar_tensor_tensor(st[:, 12:13], st[:, 10:11],
                                                   st[:, 10:11], st[:, 11:12],
                                                   ALU.mult, ALU.subtract), "dv")

            @blk.scalar
            def _(s):
                cx.wait(s, "dv")
                for sub in range(4):
                    st = st_sb[:, sub, :]
                    cx.mark(s.activation(st[:, 13:14], st[:, 12:13], AF.Ln, bias=EPS,
                                         scale=-1.0), "sc")
                    cx.wait(s, "sc")
                    cx.mark(s.activation(st[:, 14:15], st[:, 13:14], AF.Exp,
                                         scale=-0.5), "sc")

            @blk.vector
            def _(v):
                cx.wait(v, "sc")
                for sub in range(4):
                    st = st_sb[:, sub, :]
                    cx.mark(v.tensor_scalar(st[:, 15:16], st[:, 10:11], st[:, 14:15],
                                            -1.0, ALU.mult, ALU.mult), "dv")
                    cx.wait(v, "dv")
                    cx.mark(v.tensor_scalar(xnnb_sb[:, sub, :], xn_sb[:, sub, :],
                                            st[:, 14:15], st[:, 15:16], ALU.mult,
                                            ALU.add), "dv")
                    if own:
                        cx.mark(v.tensor_scalar(st[:, 2:3], st[:, 6:7], 1.0, None,
                                                ALU.add), "dv")
                        cx.wait(v, "dv")
                        cx.mark(v.tensor_scalar(xq_sb[:, sub, :],
                                                xt_sb[slot][:, sub, :], st[:, 2:3],
                                                st[:, 7:8], ALU.mult, ALU.add), "dv")
                xt_done[slot] = cx.cnt["dv"]

            dv_tr_base = cx.cnt["dv"]
            pe_tr_base = cx.cnt["pe"]

            @blk.tensor
            def _(t):
                cx.wait(t, "dv", dv_tr_base)
                cx.wait(t, "swp", 48)
                dv0 = cx.cnt["dv"]
                k = 0
                for sub in range(4):
                    for dc in range(NCD):
                        bank = [ptb0, ptb1][k % 2]
                        if k >= 2:
                            cx.wait(t, "dv", dv0 + (k - 1))
                        cx.mark(t.transpose(bank[:, 0:P],
                                            xnnb_sb[:, sub, dc * P:(dc + 1) * P],
                                            idb[:]), "pe")
                        k += 1
                if own:
                    for k2 in range(32):
                        sub, dc = k2 // NCD, k2 % NCD
                        bank = [ptf0, ptf1][k % 2]
                        if k2 >= 2:
                            cx.wait(t, "dv", dv0 + 32 + (k2 - 1))
                        cx.mark(t.transpose(bank[:, 0:P],
                                            xq_sb[:, sub, dc * P:(dc + 1) * P],
                                            idf[:]), "pe")
                        k += 1

            @blk.vector
            def _(v):
                if i >= 1:
                    cx.wait(v, "pe", pe_win_end)
                    cx.wait(v, "sxp", xp_spill_done)
                k = 0
                for sub in range(4):
                    for dc in range(NCD):
                        bank = [ptb0, ptb1][k % 2]
                        cx.wait(v, "pe", pe_tr_base + k + 1)
                        cx.mark(v.tensor_scalar(hfm_sb[:, dc, sub * P:(sub + 1) * P],
                                                bank[:, 0:P], gam_sb[:, dc:dc + 1],
                                                bet_sb[:, dc:dc + 1],
                                                ALU.mult, ALU.add), "dv")
                        k += 1
                if own:
                    for k2 in range(32):
                        sub, dc = k2 // NCD, k2 % NCD
                        bank = [ptf0, ptf1][k2 % 2]
                        cx.wait(v, "pe", pe_tr_base + 32 + k2 + 1)
                        cx.mark(v.tensor_copy(xpfm_sb[:, dc, sub * P:(sub + 1) * P],
                                              bank[:, 0:P]), "dv")
            assert cx.cnt["pe"] == pe_tr_base + ntr == pe_after_tr
            pe_tr_end_prev = cx.cnt["pe"]
            hfm_ready = cx.cnt["dv"]
            xpfm_ready = cx.cnt["dv"]

            if own:
                j = i - n_pad_t

                @blk.sync
                def _(s):
                    cx.wait(s, "dv", xpfm_ready)
                    cx.dma_seq(s, "sxp", xpxn_sp[:, :, j * TW:(j + 1) * TW],
                               xpfm_sb[:])
                xp_spill_done = cx.cnt["sxp"]

            pe_mm_base = cx.cnt["pe"]
            sc_ev_base = cx.cnt["sc"]

            @blk.tensor
            def _(t):
                cx.wait(t, "dv", hfm_ready)
                cx.wait(t, "gp", roll_ready)
                for jt in range(njt):
                    wb, ws = jt // 4, (jt // 4) % 2
                    cx.wait(t, f"sw{ws}", swbase[ws] + 16 * (wb // 2 + 1))
                    bank = [pbA, pbB][jt % 2]
                    if jt >= 2:
                        cx.wait(t, "sc", sc_ev_base + (jt - 1))
                    for dc in range(NCD):
                        ins = t.matmul(bank[:],
                                       w1b_sb[ws][:, jt % 4, dc * P:(dc + 1) * P],
                                       hfm_sb[:, dc, :],
                                       start=(dc == 0), stop=(dc == NCD - 1))
                    cx.mark(ins, "pe")
            pe_win_end = cx.cnt["pe"]
            assert pe_win_end == pe_mm_base + njt

            ngrp = njt // 4
            for grp in range(ngrp):
                gate_grp = grp >= 4

                @blk.scalar
                def _(s):
                    cx.wait(s, "gp", roll_ready)
                    if gate_grp:
                        cx.wait(s, "ssg", sgspill_done)
                    for q in range(4):
                        jt = grp * 4 + q
                        bank = [pbA, pbB][jt % 2]
                        cx.wait(s, "pe", pe_mm_base + jt + 1)
                        if jt < NCI:
                            ins = s.copy(roll_sb[:, jt, 3:3 + TW], bank[:])
                        else:
                            ins = s.activation(sgs_sb[:, q, :], bank[:], AF.Silu)
                        cx.mark(ins, "sc")

                if gate_grp and own:
                    j = i - n_pad_t

                    @blk.sync
                    def _(s):
                        cx.wait(s, "sc")
                        cx.dma_seq(s, "ssg",
                                   sg_sp[:, (grp - 4) * 4:(grp - 4) * 4 + 4,
                                         j * TW:(j + 1) * TW], sgs_sb[:])
                    sgspill_done = cx.cnt["ssg"]
            assert cx.cnt["sc"] == sc_ev_base + njt
            sc_roll_done = cx.cnt["sc"]

            for grp in range(4):
                cbase = grp * 4

                @blk.scalar
                def _(s):
                    cx.wait(s, "sc", sc_roll_done)
                    cx.wait(s, "dv")
                    for q in range(4):
                        c = cbase + q
                        cx.mark(s.mul(xca_sb[:, q, :], roll_sb[:, c, 0:TW],
                                      cwv_sb[:, c * 4:c * 4 + 1]), "sc")

                @blk.vector
                def _(v):
                    cx.wait(v, "sc")
                    for q in range(4):
                        c = cbase + q
                        for kk in (1, 2):
                            srcb, dstb = (xca_sb, xcb_sb) if kk == 1 else (xcb_sb, xca_sb)
                            cx.mark(v.scalar_tensor_tensor(
                                dstb[:, q, :], roll_sb[:, c, kk:kk + TW],
                                cwv_sb[:, c * 4 + kk:c * 4 + kk + 1],
                                srcb[:, q, :], ALU.mult, ALU.add), "dv")
                            cx.wait(v, "dv")
                        cx.mark(v.scalar_tensor_tensor(
                            xcb_sb[:, q, :], roll_sb[:, c, 3:3 + TW],
                            cwv_sb[:, c * 4 + 3:c * 4 + 4],
                            xca_sb[:, q, :], ALU.mult, ALU.add), "dv")

                @blk.scalar
                def _(s):
                    cx.wait(s, "dv")
                    cx.wait(s, "scv", convspill_done)
                    for q in range(4):
                        c = cbase + q
                        cx.mark(s.activation(xss_sb[:, q, :], xcb_sb[:, q, :],
                                             AF.Silu, bias=cb_sb[:, c:c + 1]), "sc")

                @blk.vector
                def _(v):
                    cx.wait(v, "sc")
                    for q in range(4):
                        cx.mark(v.tensor_copy(xsbfs_sb[:, q, :], xss_sb[:, q, :]),
                                "dv")

                @blk.sync
                def _(s):
                    cx.wait(s, "dv")
                    cx.dma_seq(s, "scv",
                               xsbf_sp[:, cbase:cbase + 4, i * TW:(i + 1) * TW],
                               xsbfs_sb[:])
                    if own:
                        j = i - n_pad_t
                        cx.dma_seq(s, "scv",
                                   xs_sp[:, cbase:cbase + 4, j * TW:(j + 1) * TW],
                                   xss_sb[:])
                convspill_done = cx.cnt["scv"]

            @blk.gpsimd
            def _(g):
                cx.wait(g, "dv")
                cx.wait(g, "gp")
                if i == n_pad_t - 1:
                    cx.wait(g, "spl")
                    for c in range(NCI):
                        cx.mark(g.tensor_tensor(roll_sb[:, c, 0:3],
                                                roll_sb[:, c, TW:TW + 3],
                                                m3_sb[:], ALU.mult), "gp")
                elif i < n_ext_t - 1:
                    for c in range(NCI):
                        cx.mark(g.tensor_copy(roll_sb[:, c, 0:3],
                                              roll_sb[:, c, TW:TW + 3]), "gp")
            roll_ready = cx.cnt["gp"]

        ps1.close()
        s1.close()
        join()

        s2 = ExitStack()
        wd_sb = sbuf(s2, "wd_sb", [P, NCI * NCI * P], BF16)
        xsb_sb = [sbuf(s2, f"xsb{k}", [P, NCI, TW], BF16) for k in range(2)]
        bxm_sb = [sbuf(s2, f"bxm{k}", [NST, TW]) for k in range(2)]
        spa_sb = sbuf(s2, "spa_sb", [P, TW], BF16)
        spb_sb = sbuf(s2, "spb_sb", [P, TW], BF16)
        spf_sb = [sbuf(s2, f"spf{k}", [P, TW], BF16) for k in range(2)]
        bxp_sb = sbuf(s2, "bxp_sb", [NST, TW])
        da_sb = sbuf(s2, "da_sb", [NST, half])
        bx_sb = sbuf(s2, "bx_sb", [NST, half])
        hs_sb = sbuf(s2, "hs_sb", [NST, half])
        ps2 = ExitStack()
        pb2 = [ps2.enter_context(nc.psum_tensor(f"p2{k}", [P, TW], F32))
               for k in range(5)]
        pbA2, pbB2, pbD, pbO, pbO2 = pb2

        @blk.gpsimd
        def _(g):
            cx.mark(g.dma_start(out=wd_sb[:],
                                in_=wd_l.rearrange("p a b -> p (a b)")), "swp", 16)
        wd_ready = cx.cnt["swp"]

        xsb_done = [0, 0]
        scan_a_done = 0
        sc_delta_prev = 0
        dv_bx_prev = 0

        for i in range(n_ext_t):
            slot = i % 2
            sldn = f"sld{slot}"
            hpass = i // half_t
            hoff = (i % half_t) * TW

            @blk.sync
            def _(s):
                if i >= 2:
                    cx.wait(s, "pe", xsb_done[slot])
                cx.mark(s.dma_start(out=xsb_sb[slot][:],
                                    in_=xsbf_sp[:, :, i * TW:(i + 1) * TW]), sldn, 16)
                cx.mark(s.dma_start(out=bxm_sb[slot][:],
                                    in_=bx_mask[:, i * TW:(i + 1) * TW]), sldn, 16)
            sld_now = cx.cnt[sldn]

            pe_wd_base = cx.cnt["pe"]
            sc_sp_base = cx.cnt["sc"]
            dv_sp_base = cx.cnt["dv"]

            pe_of_wd = []
            pe_of_ones = []

            @blk.tensor
            def _(t):
                cx.wait(t, sldn, sld_now)
                cx.wait(t, "swp", wd_ready)
                if hpass == 1 and i % half_t == 0:
                    cx.wait(t, "dv", scan_a_done)

                def ones_mm(jo):
                    cx.wait(t, "dv", dv_sp_base + jo + 1)
                    if jo == 0:
                        cx.wait(t, "sc", sc_delta_prev)
                    pe_of_ones.append(
                        cx.mark(t.matmul(pbD[0:1, :], onesc_sb[:], spf_sb[jo % 2][:],
                                         start=(jo == 0), stop=(jo == NCI - 1)),
                                "pe"))

                for jt in range(NCI):
                    bank = [pbA2, pbB2][jt % 2]
                    if jt >= 2:
                        cx.wait(t, "sc", sc_sp_base + 4 * (jt - 1))
                    for kc in range(NCI):
                        ins = t.matmul(bank[:],
                                       wd_sb[:, (jt * NCI + kc) * P:(jt * NCI + kc + 1) * P],
                                       xsb_sb[slot][:, kc, :],
                                       start=(kc == 0), stop=(kc == NCI - 1))
                    pe_of_wd.append(cx.mark(ins, "pe"))
                    if jt >= 2:
                        ones_mm(jt - 2)
                ones_mm(NCI - 2)
                ones_mm(NCI - 1)
            pe_ones = cx.cnt["pe"]

            @blk.scalar
            def _(s):
                for jt in range(NCI):
                    bank = [pbA2, pbB2][jt % 2]
                    cx.wait(s, "pe", pe_of_wd[jt])
                    if jt >= 1:
                        cx.wait(s, "dv", dv_sp_base + jt)
                    cx.mark(s.activation(spa_sb[:], bank[:], AF.Abs,
                                         bias=bd_sb[:, jt:jt + 1]), "sc")
                    cx.wait(s, "sc")
                    cx.mark(s.activation(spb_sb[:], spa_sb[:], AF.Exp, scale=-1.0),
                            "sc")
                    cx.wait(s, "sc")
                    cx.mark(s.activation(spa_sb[:], spb_sb[:], AF.Ln, bias=1.0), "sc")
                    cx.wait(s, "sc")
                    cx.mark(s.activation(spb_sb[:], bank[:], AF.Relu,
                                         bias=bd_sb[:, jt:jt + 1]), "sc")

            @blk.vector
            def _(v):
                for jt in range(NCI):
                    cx.wait(v, "sc", sc_sp_base + 4 * (jt + 1))
                    if jt >= 2:
                        cx.wait(v, "pe", pe_of_ones[jt - 2])
                    cx.mark(v.tensor_add(spf_sb[jt % 2][:], spa_sb[:], spb_sb[:]),
                            "dv")
            assert cx.cnt["dv"] == dv_sp_base + NCI


            @blk.scalar
            def _(s):
                cx.wait(s, "pe", pe_ones)
                cx.mark(s.activation(delta_sb[:], pbD[0:1, :], AF.Copy, scale=RDI),
                        "sc")
            sc_delta = cx.cnt["sc"]
            sc_delta_prev = sc_delta

            @blk.tensor
            def _(t):
                cx.wait(t, "sc", max(sc_delta, arow_ready))
                cx.wait(t, "dv", dv_bx_prev)
                cx.mark(t.matmul(pbO[0:16, :], arow_sb[:], delta_sb[:],
                                 start=True, stop=True), "pe")
            pe_da = cx.cnt["pe"]

            @blk.scalar
            def _(s):
                cx.wait(s, "pe", pe_da)
                cx.mark(s.activation(da_sb[:, hoff:hoff + TW], pbO[0:16, :],
                                     AF.Exp), "sc")
            sc_da = cx.cnt["sc"]

            @blk.tensor
            def _(t):
                cx.wait(t, "sc", sc_da)
                t.matmul(pbO[0:16, :], ones_sb[0:1, 0:NST], delta_sb[:],
                         start=True, stop=True)
                for kc in range(NCI):
                    ins = t.matmul(pbO2[0:16, :], bt_sb[:, kc * NST:(kc + 1) * NST],
                                   xsb_sb[slot][:, kc, :],
                                   start=(kc == 0), stop=(kc == NCI - 1))
                cx.mark(ins, "pe")
            pe_bx = cx.cnt["pe"]

            @blk.vector
            def _(v):
                cx.wait(v, "pe", pe_bx)
                cx.wait(v, sldn, sld_now)
                cx.mark(v.tensor_copy(bxp_sb[:], pbO2[0:16, :]), "dv")
                cx.wait(v, "dv")
                cx.mark(v.tensor_mul(bxp_sb[:], bxp_sb[:], pbO[0:16, :]), "dv")
                cx.wait(v, "dv")
                cx.mark(v.tensor_mul(bx_sb[:, hoff:hoff + TW], bxp_sb[:],
                                     bxm_sb[slot][:]), "dv")
            dv_bx_prev = cx.cnt["dv"]
            xsb_done[slot] = pe_bx

            if i == half_t - 1:
                @blk.vector
                def _(v):
                    cx.wait(v, "dv")
                    cx.mark(v.tensor_tensor_scan(hs_sb[:], da_sb[:], bx_sb[:], 0.0,
                                                 ALU.mult, ALU.add), "dv")
                    cx.wait(v, "dv")
                    cx.mark(v.tensor_copy(hmid_sb[:], hs_sb[:, half - 1:half]), "dv")
                scan_a_done = cx.cnt["dv"]
            elif i == n_ext_t - 1:
                @blk.vector
                def _(v):
                    cx.wait(v, "dv")
                    cx.mark(v.tensor_tensor_scan(hs_sb[:], da_sb[:], bx_sb[:],
                                                 hmid_sb[:], ALU.mult, ALU.add), "dv")
                    cx.wait(v, "dv")
                    cx.mark(v.tensor_copy(hsb_sb[:], hs_sb[:]), "dv")

        if dbg:
            @blk.sync
            def _(s):
                cx.wait(s, "dv")
                cx.dma_seq(s, "sst", hs_dbg, hs_sb[:])
                cx.dma_seq(s, "sst", da_dbg, da_sb[:])
                cx.dma_seq(s, "sst", bxd_dbg, bx_sb[:])

        if upto == "A":
            @blk.sync
            def _(s):
                cx.wait(s, "dv")
                cx.dma_seq(s, "sst", y_out[0:NST, 0:min(half, DM)],
                           hs_sb[:, 0:min(half, DM)])
                for nm in ["sst", "sxp", "ssg", "scv"]:
                    cx.wait(s, nm)
            ps2.close()
            s2.close()
            return nc

        ps2.close()
        s2.close()
        join()

        s3 = ExitStack()
        wo_sb = sbuf(s3, "wo_sb", [P, NCD * NCI * P], BF16)
        xsr_sb = [sbuf(s3, f"xsr{k}", [P, 4, TW]) for k in range(2)]
        sgr_sb = [sbuf(s3, f"sgr{k}", [P, 4, TW]) for k in range(2)]
        xpr_sb = [sbuf(s3, f"xpr{k}", [P, NCD, TW]) for k in range(2)]
        ygf_sb = sbuf(s3, "ygf_sb", [P, TW])
        ygb_sb = sbuf(s3, "ygb_sb", [P, NCI, TW], BF16)
        x1t_sb = sbuf(s3, "x1t_sb", [P, NCD, TW])
        x1b_sb = sbuf(s3, "x1b_sb", [P, NCD, TW], BF16)
        x1q_sb = sbuf(s3, "x1q_sb", [P, NCD, TW], BF16)
        xn2f_sb = sbuf(s3, "xn2f_sb", [P, NCD, TW])
        xn2b_sb = sbuf(s3, "xn2b_sb", [P, NCD, TW], BF16)
        row_sb = sbuf(s3, "row_sb", [1, 8 * TW])
        mcol_sb = sbuf(s3, "mcol_sb", [P, 8])
        eq1_sb = sbuf(s3, "eq1_sb", [P, NE])
        msk_sb = sbuf(s3, "msk_sb", [P, NE])
        cwt_sb = sbuf(s3, "cwt_sb", [P, NE])
        cwst_sb = sbuf(s3, "cwst_sb", [NE, P])
        ps3 = ExitStack()
        pcA = ps3.enter_context(nc.psum_tensor("pcA", [P, TW], F32))
        pcB = ps3.enter_context(nc.psum_tensor("pcB", [P, TW], F32))
        pcD1 = ps3.enter_context(nc.psum_tensor("pcD1", [P, TW], F32))
        pcD2 = ps3.enter_context(nc.psum_tensor("pcD2", [P, TW], F32))
        pcO = ps3.enter_context(nc.psum_tensor("pcO", [P, TW], F32))
        pcO2 = ps3.enter_context(nc.psum_tensor("pcO2", [P, TW], F32))
        pcR = ps3.enter_context(nc.psum_tensor("pcR", [P, TW], F32))
        pcT = ps3.enter_context(nc.psum_tensor("pcT", [P, TW], F32))

        @blk.gpsimd
        def _(g):
            cx.mark(g.dma_start(out=wo_sb[:], in_=w_out_l), "swp", 16)
        wo_ready = cx.cnt["swp"]

        ldslot_done = [0, 0]
        dv_rep_prev = 0
        dv_x1_prev = 0
        pe_cast_prev = 0
        dv_router_prev = 0
        dv_ygb_prev = 0

        r_mu = row_sb[0:1, 0 * TW:1 * TW]
        r_m2 = row_sb[0:1, 1 * TW:2 * TW]
        r_nv = row_sb[0:1, 2 * TW:3 * TW]
        r_rs = row_sb[0:1, 3 * TW:4 * TW]
        r_b = row_sb[0:1, 4 * TW:5 * TW]

        for j in range(n_own_t):
            slot = j % 2
            sxwn = f"sxw{slot}"

            @blk.sync
            def _(s):
                if j >= 2:
                    cx.wait(s, "dv", ldslot_done[slot])
                cx.wait(s, sxwn)
                cx.mark(s.dma_start(out=xpr_sb[slot][:],
                                    in_=xpxn_sp[:, :, j * TW:(j + 1) * TW]),
                        sxwn, 16)
            ld_xp = cx.cnt[sxwn]

            pe_ys = []
            dv_ysbase = cx.cnt["dv"]
            for grp in range(4):
                gslot = grp % 2

                @blk.sync
                def _(s):
                    if j >= 2 or grp >= 2:
                        cx.wait(s, "dv")
                    cx.wait(s, sxwn)
                    cx.mark(s.dma_start(
                        out=xsr_sb[gslot][:],
                        in_=xs_sp[:, grp * 4:(grp + 1) * 4, j * TW:(j + 1) * TW]),
                        sxwn, 16)
                    cx.mark(s.dma_start(
                        out=sgr_sb[gslot][:],
                        in_=sg_sp[:, grp * 4:(grp + 1) * 4, j * TW:(j + 1) * TW]),
                        sxwn, 16)
                ld_now = cx.cnt[sxwn]

                pe_g = []

                @blk.tensor
                def _(t):
                    cx.wait(t, "swp", wo_ready)
                    if j == 0 and grp == 0:
                        cx.wait(t, "dv")
                        cx.wait(t, "spl")
                    for q in range(4):
                        c = grp * 4 + q
                        bank = [pcA, pcB][c % 2]
                        nprev = len(pe_ys) + q
                        if nprev >= 2:
                            cx.wait(t, "dv", dv_ysbase + 2 * (nprev - 1))
                        elif j > 0 and nprev == 0:
                            cx.wait(t, "dv", dv_ygb_prev)
                        pe_g.append(cx.mark(
                            t.matmul(bank[:], ct_sb[:, c * P:(c + 1) * P],
                                     hsb_sb[:, j * TW:(j + 1) * TW],
                                     start=True, stop=True), "pe"))

                @blk.vector
                def _(v):
                    cx.wait(v, sxwn, ld_now)
                    for q in range(4):
                        c = grp * 4 + q
                        bank = [pcA, pcB][c % 2]
                        cx.wait(v, "pe", pe_g[q])
                        cx.wait(v, "dv")
                        cx.mark(v.scalar_tensor_tensor(
                            ygf_sb[:], xsr_sb[gslot][:, q, :], dvec_sb[:, c:c + 1],
                            bank[:], ALU.mult, ALU.add), "dv")
                        cx.wait(v, "dv")
                        cx.mark(v.tensor_mul(ygb_sb[:, c, :], ygf_sb[:],
                                             sgr_sb[gslot][:, q, :]), "dv")
                pe_ys += pe_g
            dv_yg = cx.cnt["dv"]
            ldslot_done[slot] = dv_yg

            pe_wo = []
            dv_wobase = cx.cnt["dv"]

            @blk.tensor
            def _(t):
                cx.wait(t, "dv", dv_yg)
                for dm in range(NCD):
                    bank = [pcA, pcB][dm % 2]
                    if dm >= 2:
                        cx.wait(t, "dv", dv_wobase + dm - 1)
                    for kc in range(NCI):
                        ins = t.matmul(bank[:],
                                       wo_sb[:, (dm * NCI + kc) * P:(dm * NCI + kc + 1) * P],
                                       ygb_sb[:, kc, :],
                                       start=(kc == 0), stop=(kc == NCI - 1))
                    pe_wo.append(cx.mark(ins, "pe"))
            dv_ygb_prev = dv_yg

            @blk.vector
            def _(v):
                cx.wait(v, sxwn, ld_xp)
                cx.wait(v, "sc", dv_x1_prev)
                for dm in range(NCD):
                    bank = [pcA, pcB][dm % 2]
                    cx.wait(v, "pe", pe_wo[dm])
                    cx.mark(v.tensor_add(x1t_sb[:, dm, :], bank[:],
                                         xpr_sb[slot][:, dm, :]), "dv")
            dv_x1 = cx.cnt["dv"]

            @blk.scalar
            def _(s):
                cx.wait(s, "dv", dv_x1)
                cx.wait(s, "pe", pe_cast_prev)
                for dm in range(NCD):
                    cx.mark(s.copy(x1b_sb[:, dm, :], x1t_sb[:, dm, :]), "sc")
                    cx.wait(s, "sc")
                    cx.mark(s.activation(x1q_sb[:, dm, :], x1t_sb[:, dm, :],
                                         AF.Square), "sc")
            sc_x1cast = cx.cnt["sc"]
            dv_x1_prev = sc_x1cast

            @blk.sync
            def _(s):
                cx.wait(s, "dv", dv_x1)
                cx.dma_seq(s, "sxp", x1_sp[:, :, j * TW:(j + 1) * TW], x1t_sb[:])

            @blk.tensor
            def _(t):
                cx.wait(t, "sc", sc_x1cast)
                cx.wait(t, "dv", dv_rep_prev)
                for dm in range(NCD):
                    ins = t.matmul(pcD1[0:1, :], onesc_sb[:], x1b_sb[:, dm, :],
                                   start=(dm == 0), stop=(dm == NCD - 1))
                cx.mark(ins, "pe")
                for dm in range(NCD):
                    ins = t.matmul(pcD2[0:1, :], onesc_sb[:], x1q_sb[:, dm, :],
                                   start=(dm == 0), stop=(dm == NCD - 1))
                cx.mark(ins, "pe")
            pe_stats = cx.cnt["pe"]
            pe_cast_prev = pe_stats

            @blk.vector
            def _(v):
                cx.wait(v, "pe", pe_stats)
                cx.mark(v.tensor_scalar(r_mu, pcD1[0:1, :], RDM, None, ALU.mult), "dv")
                cx.mark(v.tensor_scalar(r_m2, pcD2[0:1, :], RDM, None, ALU.mult), "dv")
                cx.wait(v, "dv")
                cx.mark(v.tensor_mul(r_nv, r_mu, r_mu), "dv")
                cx.wait(v, "dv")
                cx.mark(v.tensor_tensor(r_nv, r_nv, r_m2, ALU.subtract), "dv")

            @blk.scalar
            def _(s):
                cx.wait(s, "dv")
                cx.mark(s.activation(r_rs, r_nv, AF.Ln, bias=EPS, scale=-1.0), "sc")
                cx.wait(s, "sc")
                cx.mark(s.activation(r_rs, r_rs, AF.Exp, scale=-0.5), "sc")

            @blk.vector
            def _(v):
                cx.wait(v, "sc")
                cx.mark(v.tensor_mul(r_b, r_mu, r_rs), "dv")
                cx.wait(v, "dv")
                cx.mark(v.tensor_scalar(r_b, r_b, -1.0, None, ALU.mult), "dv")
            dv_rows = cx.cnt["dv"]

            @blk.tensor
            def _(t):
                cx.wait(t, "dv", dv_rows)
                cx.mark(t.matmul(pcO[:], ones_sb[:], r_rs, start=True, stop=True),
                        "pe")
                cx.mark(t.matmul(pcO2[:], ones_sb[:], r_b, start=True, stop=True),
                        "pe")
            pe_rep = cx.cnt["pe"]

            @blk.vector
            def _(v):
                cx.wait(v, "pe", pe_rep)
                cx.wait(v, "ssg")
                for dm in range(NCD):
                    cx.mark(v.tensor_mul(xn2f_sb[:, dm, :], x1t_sb[:, dm, :],
                                         pcO[:]), "dv")
                    cx.wait(v, "dv")
                    cx.mark(v.tensor_add(xn2f_sb[:, dm, :], xn2f_sb[:, dm, :],
                                         pcO2[:]), "dv")
                    cx.wait(v, "dv")
                    cx.mark(v.tensor_copy(xn2b_sb[:, dm, :], xn2f_sb[:, dm, :]),
                            "dv")
            dv_rep_prev = cx.cnt["dv"]

            @blk.sync
            def _(s):
                cx.wait(s, "dv", dv_rep_prev)
                cx.dma_seq(s, "ssg", xn2_sp[:, :, j * TW:(j + 1) * TW], xn2b_sb[:])

            for sub in range(4):
                @blk.tensor
                def _(t):
                    cx.wait(t, "dv", dv_rep_prev)
                    cx.wait(t, "dv", dv_router_prev)
                    for dc in range(NCD):
                        t.matmul(pcR[:, 0:NE],
                                 xn2f_sb[:, dc, sub * P:(sub + 1) * P],
                                 wr_sb[:, dc * NE:(dc + 1) * NE],
                                 start=(dc == 0), stop=False)
                    cx.mark(t.matmul(pcR[:, 0:NE], ones_sb[0:1, 0:P],
                                     br_sb[:], start=False, stop=True), "pe")
                pe_rt = cx.cnt["pe"]

                @blk.vector
                def _(v):
                    cx.wait(v, "pe", pe_rt)
                    cx.mark(v.reduce_max(mcol_sb[:, 0:1], pcR[:, 0:NE], axis=AXX),
                            "dv")
                    cx.wait(v, "dv")
                    cx.mark(v.tensor_scalar(eq1_sb[:], pcR[:, 0:NE],
                                            mcol_sb[:, 0:1], None, ALU.is_equal),
                            "dv")
                    cx.wait(v, "dv")
                    cx.mark(v.scalar_tensor_tensor(msk_sb[:], eq1_sb[:], -1e30,
                                                   pcR[:, 0:NE], ALU.mult, ALU.add),
                            "dv")
                    cx.wait(v, "dv")
                    cx.mark(v.reduce_max(mcol_sb[:, 1:2], msk_sb[:], axis=AXX), "dv")
                    cx.wait(v, "dv")
                    cx.mark(v.tensor_scalar(msk_sb[:], msk_sb[:], mcol_sb[:, 1:2],
                                            None, ALU.is_equal), "dv")
                    cx.wait(v, "dv")
                    cx.mark(v.tensor_sub(mcol_sb[:, 2:3], mcol_sb[:, 1:2],
                                         mcol_sb[:, 0:1]), "dv")

                @blk.scalar
                def _(s):
                    cx.wait(s, "dv")
                    cx.mark(s.activation(mcol_sb[:, 3:4], mcol_sb[:, 2:3], AF.Exp),
                            "sc")

                @blk.vector
                def _(v):
                    cx.wait(v, "sc")
                    cx.mark(v.tensor_scalar(mcol_sb[:, 4:5], mcol_sb[:, 3:4], 1.0,
                                            None, ALU.add), "dv")
                    cx.wait(v, "dv")
                    cx.mark(v.reciprocal(mcol_sb[:, 5:6], mcol_sb[:, 4:5]), "dv")
                    cx.wait(v, "dv")
                    cx.mark(v.tensor_scalar(mcol_sb[:, 6:7], mcol_sb[:, 5:6], -1.0,
                                            1.0, ALU.mult, ALU.add), "dv")
                    cx.wait(v, "dv")
                    cx.mark(v.tensor_scalar(cwt_sb[:], eq1_sb[:], mcol_sb[:, 5:6],
                                            None, ALU.mult), "dv")
                    cx.wait(v, "dv")
                    cx.mark(v.scalar_tensor_tensor(cwt_sb[:], msk_sb[:],
                                                   mcol_sb[:, 6:7], cwt_sb[:],
                                                   ALU.mult, ALU.add), "dv")
                dv_cwt = cx.cnt["dv"]

                @blk.tensor
                def _(t):
                    cx.wait(t, "dv", dv_cwt)
                    cx.mark(t.transpose(pcT[0:NE, 0:P], cwt_sb[:], idf[:]), "pe")
                pe_cwt = cx.cnt["pe"]

                @blk.vector
                def _(v):
                    cx.wait(v, "pe", pe_cwt)
                    cx.wait(v, "scw0")
                    cx.mark(v.tensor_copy(cwst_sb[:], pcT[0:NE, 0:P]), "dv")
                dv_router_prev = cx.cnt["dv"]

                @blk.sync
                def _(s):
                    cx.wait(s, "dv", dv_router_prev)
                    cx.dma_seq(s, "scw0",
                               cw_sp[:, j * TW + sub * P:j * TW + (sub + 1) * P],
                               cwst_sb[:])

        ps3.close()
        s3.close()
        join()

        if upto == "C":
            @blk.sync
            def _(s):
                for nm in DMA_SEMS:
                    cx.wait(s, nm)
                cx.dma_seq(s, "sst", y_out[0:P, 0:DM],
                           sbuf(es, "dummy_sb", [P, DM]))
            return nc

        s4 = ExitStack()
        g_sb = sbuf(s4, "g_sb", [P, NCF, fw], BF16)
        xn2w_sb = [sbuf(s4, f"xn2w{k}", [P, NCD, fw], BF16) for k in range(2)]
        yacc_sb = sbuf(s4, "yacc_sb", [P, NCD, fw])
        cwrep_sb = sbuf(s4, "cwrep_sb", [P, fw])
        cwrow_sb = [sbuf(s4, f"cwrow{k}", [1, fw]) for k in range(2)]
        cwall_sb = sbuf(s4, "cwall_sb", [NE, fw])
        b3a_sb = sbuf(s4, "b3a_sb", [NE, DM])
        w1s_sb = [sbuf(s4, f"w1s{k}", [P, NCD * P], BF16) for k in range(2)]
        w2s_sb = [sbuf(s4, f"w2s{k}", [P, NCD * P], BF16) for k in range(2)]
        w3s_sb = [sbuf(s4, f"w3s{k}", [P, NCD * P], BF16) for k in range(2)]
        b1e_sb = [sbuf(s4, f"b1e{k}", [P, NCF]) for k in range(2)]
        b2e_sb = [sbuf(s4, f"b2e{k}", [P, NCF]) for k in range(2)]
        s1t_sb = sbuf(s4, "s1t_sb", [P, 2, TW])
        ygf_c_sb = sbuf(s4, "ygf_c_sb", [P, 2, TW])
        oadd_sb = sbuf(s4, "oadd_sb", [P, NCD, P])
        ostg_sb = [sbuf(s4, f"ostg{k}", [P, DM]) for k in range(2)]
        x1g_sb = [sbuf(s4, f"x1g{k}", [P, NCD, P]) for k in range(2)]
        ps4 = ExitStack()
        pY = [ps4.enter_context(nc.psum_tensor(f"pY{k}", [P, TW], F32))
              for k in range(8)]

        @blk.sync
        def _(s):
            cx.dma_seq(s, "scw1", b3a_sb[:], b3_l)

        dv_gall_prev = 0
        dv_yacc_final = 0
        xn2w_done = [0, 0]
        pe_g_prev_end = 0
        pe_y_prev_end = 0
        dv_ostg = [0, 0]
        pe_tr_prev = 0
        NSILU = nsub * NCF
        NGOP = 2 * nsub * NCF

        for w in range(nwin):
            wslot = w % 2
            sxwn = f"sxw{wslot}"

            @blk.sync
            def _(s):
                if w >= 2:
                    cx.wait(s, "pe", xn2w_done[wslot])
                cx.wait(s, sxwn)
                cx.mark(s.dma_start(out=xn2w_sb[wslot][:],
                                    in_=xn2_sp[:, :, w * fw:(w + 1) * fw]),
                        sxwn, 16)
            ld_xn2w = cx.cnt[sxwn]

            for e in range(NE):
                eslot = e % 2

                @blk.sync
                def _(s):
                    cx.wait(s, "dv", dv_gall_prev)
                    cx.wait(s, "scw0")
                    cx.mark(s.dma_start(out=cwrow_sb[eslot][:],
                                        in_=cw_sp[e:e + 1, w * fw:(w + 1) * fw]),
                            "scw0", 16)
                    cx.mark(s.dma_start(out=b1e_sb[eslot][:], in_=b1_l[e]),
                            "scw0", 16)
                    cx.mark(s.dma_start(out=b2e_sb[eslot][:], in_=b2_l[e]),
                            "scw0", 16)
                ld_cw = cx.cnt["scw0"]

                @blk.tensor
                def _(t):
                    cx.wait(t, "scw0", ld_cw)
                    cx.wait(t, "dv", dv_gall_prev)
                    for ts in range(nsub):
                        cx.mark(t.matmul(pY[4 + (ts % 2)][:], ones_sb[:],
                                         cwrow_sb[eslot][0:1, ts * TW:(ts + 1) * TW],
                                         start=True, stop=True), "pe")
                pe_cwrep = cx.cnt["pe"]

                @blk.vector
                def _(v):
                    for ts in range(nsub):
                        cx.wait(v, "pe", pe_cwrep - nsub + ts + 1)
                        cx.mark(v.tensor_copy(cwrep_sb[:, ts * TW:(ts + 1) * TW],
                                              pY[4 + (ts % 2)][:]), "dv")
                dv_cwrep = cx.cnt["dv"]

                pe0 = cx.cnt["pe"]
                sc0 = cx.cnt["sc"]
                dv0 = cx.cnt["dv"]
                swb = {k: cx.cnt[k] for k in ("sw0", "sw1", "sw20", "sw21")}

                @blk.gpsimd
                def _(g):
                    for fblk in range(NCF):
                        fslot = fblk % 2
                        if fblk >= 2:
                            cx.wait(g, "pe", pe0 + 2 * nsub * (fblk - 1))
                        elif pe_g_prev_end > 0:
                            cx.wait(g, "pe", pe_g_prev_end)
                        cx.mark(g.dma_start(out=w1s_sb[fslot][:],
                                            in_=w1_l[e, fblk]), f"sw{fslot}", 16)
                        cx.mark(g.dma_start(out=w2s_sb[fslot][:],
                                            in_=w2_l[e, fblk]), f"sw2{fslot}", 16)

                @blk.tensor
                def _(t):
                    cx.wait(t, sxwn, ld_xn2w)
                    cx.wait(t, "dv", dv_cwrep)
                    for fblk in range(NCF):
                        fslot = fblk % 2
                        bsel = fblk % 2
                        cx.wait(t, f"sw{fslot}",
                                swb[f"sw{fslot}"] + 16 * (fblk // 2 + 1))
                        cx.wait(t, f"sw2{fslot}",
                                swb[f"sw2{fslot}"] + 16 * (fblk // 2 + 1))
                        for ts in range(nsub):
                            bank1 = pY[ts + 4 * bsel]
                            bank2 = pY[2 + ts + 4 * bsel]
                            if fblk >= 2:
                                cx.wait(t, "sc",
                                        sc0 + nsub * (fblk - 2) + ts + 1)
                                cx.wait(t, "dv",
                                        dv0 + 2 * nsub * (fblk - 2) + 2 * ts + 1)
                            for dc in range(NCD):
                                ins = t.matmul(
                                    bank1[:], w1s_sb[fslot][:, dc * P:(dc + 1) * P],
                                    xn2w_sb[wslot][:, dc, ts * TW:(ts + 1) * TW],
                                    start=(dc == 0), stop=(dc == NCD - 1))
                            cx.mark(ins, "pe")
                            for dc in range(NCD):
                                ins = t.matmul(
                                    bank2[:], w2s_sb[fslot][:, dc * P:(dc + 1) * P],
                                    xn2w_sb[wslot][:, dc, ts * TW:(ts + 1) * TW],
                                    start=(dc == 0), stop=(dc == NCD - 1))
                            cx.mark(ins, "pe")
                assert cx.cnt["pe"] == pe0 + 2 * nsub * NCF
                pe_g_prev_end = cx.cnt["pe"]
                xn2w_done[wslot] = cx.cnt["pe"]

                @blk.scalar
                def _(s):
                    for fblk in range(NCF):
                        bsel = fblk % 2
                        for ts in range(nsub):
                            cx.wait(s, "pe", pe0 + 2 * nsub * fblk + 2 * ts + 1)
                            if fblk >= 1:
                                cx.wait(s, "dv",
                                        dv0 + 2 * nsub * (fblk - 1) + 2 * ts + 1)
                            cx.mark(s.activation(
                                s1t_sb[:, ts, :], pY[ts + 4 * bsel][:], AF.Silu,
                                bias=b1e_sb[eslot][:, fblk:fblk + 1]), "sc")
                assert cx.cnt["sc"] == sc0 + NSILU

                @blk.vector
                def _(v):
                    for fblk in range(NCF):
                        bsel = fblk % 2
                        for ts in range(nsub):
                            cx.wait(v, "pe", pe0 + 2 * nsub * fblk + 2 * ts + 2)
                            cx.wait(v, "sc", sc0 + nsub * fblk + ts + 1)
                            cx.wait(v, "dv")
                            cx.mark(v.scalar_tensor_tensor(
                                ygf_c_sb[:, ts, :], pY[2 + ts + 4 * bsel][:],
                                b2e_sb[eslot][:, fblk:fblk + 1], s1t_sb[:, ts, :],
                                ALU.add, ALU.mult), "dv")
                            cx.wait(v, "dv")
                            cx.mark(v.tensor_mul(
                                g_sb[:, fblk, ts * TW:(ts + 1) * TW],
                                ygf_c_sb[:, ts, :],
                                cwrep_sb[:, ts * TW:(ts + 1) * TW]), "dv")
                assert cx.cnt["dv"] == dv0 + NGOP
                dv_gall = cx.cnt["dv"]

                for ts in range(nsub):
                    py0 = cx.cnt["pe"]
                    sw3b = {k: cx.cnt[k] for k in ("sw30", "sw31")}

                    @blk.gpsimd
                    def _(g):
                        for fblk in range(NCF):
                            fslot = fblk % 2
                            if fblk >= 2:
                                cx.wait(g, "pe", py0 + fblk - 1)
                            elif pe_y_prev_end > 0:
                                cx.wait(g, "pe", pe_y_prev_end)
                            cx.mark(g.dma_start(
                                out=w3s_sb[fslot][:],
                                in_=w3_l[e, fblk * P:(fblk + 1) * P, :]),
                                f"sw3{fslot}", 16)

                    @blk.tensor
                    def _(t):
                        cx.wait(t, "dv", dv_gall)
                        cx.wait(t, "sc")
                        if ts > 0 or e > 0:
                            cx.wait(t, "dv")
                        if e == 0 and ts == 0 and w > 0:
                            cx.wait(t, "dv", dv_yacc_final)
                        for fblk in range(NCF):
                            fslot = fblk % 2
                            cx.wait(t, f"sw3{fslot}",
                                    sw3b[f"sw3{fslot}"] + 16 * (fblk // 2 + 1))
                            for dm in range(NCD):
                                ins = t.matmul(
                                    pY[dm][:], w3s_sb[fslot][:, dm * P:(dm + 1) * P],
                                    g_sb[:, fblk, ts * TW:(ts + 1) * TW],
                                    start=(fblk == 0), stop=(fblk == NCF - 1))
                            cx.mark(ins, "pe")
                    pe_y_prev_end = cx.cnt["pe"]
                    assert pe_y_prev_end == py0 + NCF

                    @blk.vector
                    def _(v):
                        cx.wait(v, "pe", pe_y_prev_end)
                        for dm in range(NCD):
                            if e == 0:
                                cx.mark(v.tensor_copy(
                                    yacc_sb[:, dm, ts * TW:(ts + 1) * TW],
                                    pY[dm][:]), "dv")
                            else:
                                cx.mark(v.tensor_add(
                                    yacc_sb[:, dm, ts * TW:(ts + 1) * TW],
                                    yacc_sb[:, dm, ts * TW:(ts + 1) * TW],
                                    pY[dm][:]), "dv")
                dv_gall_prev = cx.cnt["dv"]

            @blk.sync
            def _(s):
                cx.wait(s, "scw1")
                cx.mark(s.dma_start(out=cwall_sb[:],
                                    in_=cw_sp[:, w * fw:(w + 1) * fw]), "scw1", 16)
            ld_cwall = cx.cnt["scw1"]

            dv_b3base = cx.cnt["dv"]
            pe_b3 = []

            @blk.tensor
            def _(t):
                cx.wait(t, "scw1", ld_cwall)
                cx.wait(t, "dv", dv_b3base)
                for ts in range(nsub):
                    for dm in range(NCD):
                        pe_b3.append(cx.mark(t.matmul(
                            pY[dm][:], b3a_sb[0:NE, dm * P:(dm + 1) * P],
                            cwall_sb[0:NE, ts * TW:(ts + 1) * TW],
                            start=True, stop=True), "pe"))

            @blk.vector
            def _(v):
                k = 0
                for ts in range(nsub):
                    for dm in range(NCD):
                        cx.wait(v, "pe", pe_b3[k])
                        cx.mark(v.tensor_add(
                            yacc_sb[:, dm, ts * TW:(ts + 1) * TW],
                            yacc_sb[:, dm, ts * TW:(ts + 1) * TW],
                            pY[dm][:]), "dv")
                        k += 1
            dv_yacc_done = cx.cnt["dv"]

            for tg in range(fw // P):
                tslot = tg % 2

                @blk.sync
                def _(s):
                    cx.wait(s, f"sld{tslot}")
                    cx.mark(s.dma_start(
                        out=x1g_sb[tslot][:],
                        in_=x1_sp[:, :, w * fw + tg * P:w * fw + (tg + 1) * P]),
                        f"sld{tslot}", 16)
                ld_x1g = cx.cnt[f"sld{tslot}"]

                @blk.vector
                def _(v):
                    cx.wait(v, f"sld{tslot}", ld_x1g)
                    cx.wait(v, "dv", dv_yacc_done)
                    cx.wait(v, "pe", pe_tr_prev)
                    for dm in range(NCD):
                        cx.mark(v.tensor_add(
                            oadd_sb[:, dm, :],
                            yacc_sb[:, dm, tg * P:(tg + 1) * P],
                            x1g_sb[tslot][:, dm, :]), "dv")
                dv_oadd = cx.cnt["dv"]

                pe_trs = []

                @blk.tensor
                def _(t):
                    for dm in range(NCD):
                        bank = pY[dm % 2]
                        cx.wait(t, "dv", dv_oadd - NCD + dm + 1)
                        if dm >= 2:
                            cx.wait(t, "dv", dv_oadd + dm - 1)
                        pe_trs.append(cx.mark(
                            t.transpose(bank[0:P, 0:P], oadd_sb[:, dm, :], idf[:]),
                            "pe"))
                pe_tr_prev = cx.cnt["pe"]

                @blk.vector
                def _(v):
                    cx.wait(v, "sst", dv_ostg[tslot])
                    for dm in range(NCD):
                        bank = pY[dm % 2]
                        cx.wait(v, "pe", pe_trs[dm])
                        cx.mark(v.tensor_copy(
                            ostg_sb[tslot][:, dm * P:(dm + 1) * P],
                            bank[0:P, 0:P]), "dv")
                dv_ostg_done = cx.cnt["dv"]

                @blk.sync
                def _(s):
                    cx.wait(s, "dv", dv_ostg_done)
                    cx.dma_seq(s, "sst",
                               y_out[w * fw + tg * P:w * fw + (tg + 1) * P, :],
                               ostg_sb[tslot][:])
                dv_ostg[tslot] = cx.cnt["sst"]
            dv_yacc_final = cx.cnt["dv"]

        ps4.close()
        s4.close()
        join()

        @blk.sync
        def _(s):
            for nm in DMA_SEMS:
                cx.wait(s, nm)

    return nc



def _layout_weights(i):
    W = {}
    W["ident_f"] = np.eye(P, dtype=np.float32)
    W["ident_b"] = np.eye(P, dtype=np.float32).astype(BFNP)
    W["w_in_l"] = np.ascontiguousarray(
        i["L0_attn_W_in"].reshape(NCD, P, NCF, P).transpose(1, 2, 0, 3)
        .reshape(P, NCF, NCD * P))
    W["wd_l"] = np.ascontiguousarray(
        i["L0_attn_Wd"].reshape(NCI, P, NCI, P).transpose(1, 2, 0, 3)
        .reshape(P, NCI, NCI * P))
    W["w_out_l"] = np.ascontiguousarray(
        i["L0_attn_W_out"].reshape(NCI, P, NCD, P).transpose(1, 2, 0, 3)
        .reshape(P, NCD * NCI * P))
    W["bt_l"] = np.ascontiguousarray(
        i["L0_attn_B_mat"].T.reshape(NCI, P, NST).transpose(1, 0, 2)
        .reshape(P, NCI * NST))
    W["ct_l"] = np.ascontiguousarray(i["L0_attn_C_mat"].T)
    W["a_log_r"] = i["L0_attn_A_log"].reshape(1, NST)
    W["d_vec_l"] = np.ascontiguousarray(i["L0_attn_D_vec"].reshape(NCI, P).T)
    W["conv_w_l"] = np.ascontiguousarray(
        i["L0_attn_conv_w"][:, 0, :].reshape(NCI, P, 4).transpose(1, 0, 2)
        .reshape(P, NCI * 4))
    W["conv_b_l"] = np.ascontiguousarray(i["L0_attn_conv_b"].reshape(NCI, P).T)
    W["bd_l"] = np.ascontiguousarray(i["L0_attn_bd"].reshape(NCI, P).T)
    W["gamma_l"] = np.ascontiguousarray(i["L0_attn_ln_gamma"].reshape(NCD, P).T)
    W["beta_l"] = np.ascontiguousarray(i["L0_attn_ln_beta"].reshape(NCD, P).T)
    W["wr_l"] = np.ascontiguousarray(
        i["L0_ffn_Wr"].reshape(NCD, P, NE).transpose(1, 0, 2).reshape(P, NCD * NE))
    W["br_r"] = i["L0_ffn_br"].reshape(1, NE)
    W["ones_r"] = np.ones((1, P), np.float32)
    W["consts"] = np.tile(np.array([[EPS, RDI, -0.5, -1.0, 1.0, 0.5, 0.0, 0.0]], np.float32), (P, 1))
    W["w1_l"] = np.ascontiguousarray(
        i["L0_ffn_W1"].reshape(NE, NCD, P, NCF, P).transpose(0, 3, 2, 1, 4)
        .reshape(NE, NCF, P, NCD * P))
    W["w2_l"] = np.ascontiguousarray(
        i["L0_ffn_W2"].reshape(NE, NCD, P, NCF, P).transpose(0, 3, 2, 1, 4)
        .reshape(NE, NCF, P, NCD * P))
    W["w3_l"] = np.ascontiguousarray(i["L0_ffn_W3"])
    W["b1_l"] = np.ascontiguousarray(
        i["L0_ffn_b1"].reshape(NE, NCF, P).transpose(0, 2, 1))
    W["b2_l"] = np.ascontiguousarray(
        i["L0_ffn_b2"].reshape(NE, NCF, P).transpose(0, 2, 1))
    W["b3_l"] = np.ascontiguousarray(i["L0_ffn_b3"])
    out = {}
    for k, v in W.items():
        v = np.asarray(v)
        out[k] = v if v.dtype == BFNP else v.astype(np.float32)
    return out


def make_in_maps(inputs, n_own=4096, n_ext=8192):
    x = np.asarray(inputs["L0_attn_x"], np.float32)
    W = _layout_weights({k: np.asarray(v, np.float32) for k, v in inputs.items()})
    in_maps = []
    for c in range(8):
        b, hf = c // 2, c % 2
        l0 = hf * n_own
        xr = np.zeros((3 + n_ext, DM), np.float32)
        lo = l0 - (n_ext - n_own) - 3
        src_lo = max(lo, 0)
        src_hi = l0 + n_own
        xr[src_lo - lo:src_hi - lo, :] = x[b, src_lo:src_hi, :]
        m = np.ones((NST, n_ext), np.float32)
        if hf == 0:
            m[:, :n_ext - n_own] = 0.0
        m3 = np.full((P, 3), 1.0 if hf == 1 else 0.0, np.float32)
        im = {"x": xr, "bx_mask": m, "m3": m3}
        im.update(W)
        in_maps.append(im)
    return in_maps


def kernel(**inputs):
    nc = build()
    in_maps = make_in_maps(inputs)
    res = run_bass_kernel_spmd(nc, in_maps, core_ids=list(range(8)))
    out = np.zeros((4, 8192, DM), np.float32)
    for c in range(8):
        b, hf = c // 2, c % 2
        out[b, hf * 4096:(hf + 1) * 4096, :] = np.asarray(res.results[c]["y"])
    return out


# version 03
# speedup vs baseline: 1.0065x; 1.0065x over previous
import numpy as np
import ml_dtypes
from contextlib import ExitStack
import concourse.bass as bass
import concourse.mybir as mybir
from concourse.bass_utils import run_bass_kernel_spmd

F32 = mybir.dt.float32
BF16 = mybir.dt.bfloat16
AF = mybir.ActivationFunctionType
ALU = mybir.AluOpType
AXX = mybir.AxisListType.X
BFNP = ml_dtypes.bfloat16

P = 128
DM = 1024
DI = 2048
NST = 16
DFF = 4096
NE = 8
NCD = DM // P
NCI = DI // P
NCF = DFF // P
TW = 512
EPS = 1e-5
RDM = 1.0 / DM
RDI = 1.0 / DI


class Ctx:
    def __init__(self, nc, es):
        self.nc = nc
        self.es = es
        self.sem = {}
        self.cnt = {}

    def new(self, *names):
        for n in names:
            self.sem[n] = self.es.enter_context(self.nc.semaphore(n))
            self.cnt[n] = 0

    def mark(self, inst, name, k=1):
        inst.then_inc(self.sem[name], k)
        self.cnt[name] += k
        return self.cnt[name]

    def wait(self, eng, name, val=None):
        v = self.cnt[name] if val is None else val
        if v > 0:
            eng.wait_ge(self.sem[name], v)

    def dma_seq(self, q, stream, out, in_):
        self.wait(q, stream)
        self.mark(q.dma_start(out=out, in_=in_), stream, 16)


def build(n_own_t=8, n_ext_t=16, fw=512, dbg=False, upto="full"):
    n_own = n_own_t * TW
    n_ext = n_ext_t * TW
    xrows = 3 + n_ext
    n_pad_t = n_ext_t - n_own_t
    half_t = n_ext_t // 2
    half = half_t * TW
    nwin = max(1, n_own // fw)
    fw = n_own // nwin
    nsub = fw // TW

    nc = bass.Bass("TRN2", target_bir_lowering=False)

    def din(name, shape, dtype=F32):
        return nc.dram_tensor(name, list(shape), dtype, kind="ExternalInput").ap()

    x_in = din("x", [xrows, DM])
    bx_mask = din("bx_mask", [NST, n_ext])
    m3_in = din("m3", [P, 3])
    ident_f = din("ident_f", [P, P])
    ident_b = din("ident_b", [P, P], BF16)
    w_in_l = din("w_in_l", [P, NCF, NCD * P])
    wd_l = din("wd_l", [P, NCI, NCI * P])
    w_out_l = din("w_out_l", [P, NCD * NCI * P])
    bt_l = din("bt_l", [P, NCI * NST])
    ct_l = din("ct_l", [NST, DI])
    a_log_r = din("a_log_r", [1, NST])
    d_vec_l = din("d_vec_l", [P, NCI])
    conv_w_l = din("conv_w_l", [P, NCI * 4])
    conv_b_l = din("conv_b_l", [P, NCI])
    bd_l = din("bd_l", [P, NCI])
    gamma_l = din("gamma_l", [P, NCD])
    beta_l = din("beta_l", [P, NCD])
    wr_l = din("wr_l", [P, NCD * NE])
    br_r = din("br_r", [1, NE])
    ones_r = din("ones_r", [1, P])
    consts_in = din("consts", [P, 8])
    w1_l = din("w1_l", [NE, NCF, P, NCD * P])
    w2_l = din("w2_l", [NE, NCF, P, NCD * P])
    w3_l = din("w3_l", [NE, DFF, DM])
    b1_l = din("b1_l", [NE, P, NCF])
    b2_l = din("b2_l", [NE, P, NCF])
    b3_l = din("b3_l", [NE, DM])

    y_out = nc.dram_tensor("y", [n_own, DM], F32, kind="ExternalOutput").ap()

    def scr(name, shape, dtype=F32):
        kind = "ExternalOutput" if dbg else "Internal"
        return nc.dram_tensor(name, list(shape), dtype, kind=kind).ap()

    xpxn_sp = scr("xpxn_sp", [P, NCD, n_own])
    xs_sp = scr("xs_sp", [P, NCI, n_own])
    xsbf_sp = scr("xsbf_sp", [P, NCI, n_ext], BF16)
    sg_sp = scr("sg_sp", [P, NCI, n_own])
    x1_sp = scr("x1_sp", [P, NCD, n_own])
    xn2_sp = scr("xn2_sp", [P, NCD, n_own], BF16)
    cw_sp = scr("cw_sp", [NE, n_own])
    w1b_sp = nc.dram_tensor("w1b_sp", [NE, NCF, P, NCD * P], BF16).ap()
    w2b_sp = nc.dram_tensor("w2b_sp", [NE, NCF, P, NCD * P], BF16).ap()
    w3b_sp = nc.dram_tensor("w3b_sp", [NE, DFF, DM], BF16).ap()
    if dbg:
        hs_dbg = scr("hs_dbg", [NST, half])
        da_dbg = scr("da_dbg", [NST, half])
        bxd_dbg = scr("bxd_dbg", [NST, half])

    es = ExitStack()
    cx = Ctx(nc, es)
    cx.new("sx0", "sx1", "spl", "swp", "pe", "sc", "dv", "gp",
           "sw0", "sw1", "sld0", "sld1", "sst", "sxp", "ssg", "scv",
           "sw20", "sw21", "sw30", "sw31", "sxw0", "sxw1", "scw0", "scw1",
           "swc", "hw10", "hw11", "hw20", "hw21", "hw30", "hw31")

    def sbuf(stack, name, shape, dtype=F32):
        return stack.enter_context(nc.sbuf_tensor(name, list(shape), dtype))

    idf = sbuf(es, "idf", [P, P])
    idb = sbuf(es, "idb", [P, P], BF16)
    m3_sb = sbuf(es, "m3_sb", [P, 3])
    dvec_sb = sbuf(es, "dvec_sb", [P, NCI])
    cwv_sb = sbuf(es, "cwv_sb", [P, NCI * 4])
    cb_sb = sbuf(es, "cb_sb", [P, NCI])
    bd_sb = sbuf(es, "bd_sb", [P, NCI])
    gam_sb = sbuf(es, "gam_sb", [P, NCD])
    bet_sb = sbuf(es, "bet_sb", [P, NCD])
    wr_sb = sbuf(es, "wr_sb", [P, NCD * NE])
    br_sb = sbuf(es, "br_sb", [1, NE])
    ones_sb = sbuf(es, "ones_sb", [1, P])
    consts_sb = sbuf(es, "consts_sb", [P, 8])
    for _ci, _cv in enumerate([EPS, RDI, -0.5, -1.0, 1.0, 0.5]):
        nc.const_aps.register(F32, _cv, consts_sb[:, _ci:_ci + 1])
    onesc_sb = sbuf(es, "onesc_sb", [P, 1], BF16)
    arow_sb = sbuf(es, "arow_sb", [1, NST])
    bt_sb = sbuf(es, "bt_sb", [P, NCI * NST], BF16)
    ct_sb = sbuf(es, "ct_sb", [NST, DI], BF16)
    hmid_sb = sbuf(es, "hmid_sb", [NST, 1])
    hsb_sb = sbuf(es, "hsb_sb", [NST, half], BF16)
    delta_sb = sbuf(es, "delta_sb", [1, TW])

    blk = es.enter_context(nc.Block())

    DMA_SEMS = ["spl", "swp", "sw0", "sw1", "sld0", "sld1", "sst", "sxp",
                "ssg", "scv", "sx0", "sx1", "sw20", "sw21", "sw30", "sw31",
                "sxw0", "sxw1", "scw0", "scw1", "swc",
                "hw10", "hw11", "hw20", "hw21", "hw30", "hw31"]

    def join():
        names = ["pe", "sc", "dv", "gp"]
        secs = {"pe": blk.tensor, "sc": blk.scalar, "dv": blk.vector,
                "gp": blk.gpsimd}
        def markself(e, n):
            if cx.cnt[n] > 0:
                e.wait_ge(cx.sem[n], cx.cnt[n])
            cx.mark(e.nop(), n)
        for n in names:
            secs[n](lambda e, n=n: markself(e, n))
        snap = {n: cx.cnt[n] for n in names}

        def waits(e, me):
            for n2, v in snap.items():
                if n2 != me and v > 0:
                    e.wait_ge(cx.sem[n2], v)
        for n in names:
            secs[n](lambda e, n=n: waits(e, n))

        def syncw(s):
            waits(s, "sync")
            for n2 in DMA_SEMS:
                cx.wait(s, n2)
        blk.sync(syncw)

        def gpw(g):
            for n2 in DMA_SEMS:
                cx.wait(g, n2)
        secs["gp"](gpw)

    with es:
        @blk.sync
        def _(s):
            for src, dst in [(ident_f, idf), (m3_in, m3_sb), (d_vec_l, dvec_sb),
                             (conv_w_l, cwv_sb), (conv_b_l, cb_sb), (bd_l, bd_sb),
                             (gamma_l, gam_sb), (beta_l, bet_sb), (wr_l, wr_sb),
                             (br_r, br_sb), (ones_r, ones_sb), (a_log_r, arow_sb),
                             (consts_in, consts_sb)]:
                cx.mark(s.dma_start(out=dst[:], in_=src), "spl", 16)

        @blk.gpsimd
        def _(g):
            for src, dst in [(ident_b, idb), (bt_l, bt_sb), (ct_l, ct_sb)]:
                cx.mark(g.dma_start(out=dst[:], in_=src), "swp", 16)
            cx.mark(g.memset(onesc_sb[:], 1.0), "gp")

        @blk.scalar
        def _(s):
            cx.wait(s, "spl")
            cx.mark(s.activation(arow_sb[:], arow_sb[:], AF.Exp), "sc")
            cx.wait(s, "sc")
            cx.mark(s.mul(arow_sb[:], arow_sb[:], -1.0), "sc")
        arow_ready = cx.cnt["sc"]

        s1 = ExitStack()
        xt_sb = [sbuf(s1, f"xt{k}", [P, 4, DM]) for k in range(2)]
        st_sb = sbuf(s1, "st_sb", [P, 4, 16])
        xn_sb = sbuf(s1, "xn_sb", [P, 4, DM])
        xq_sb = sbuf(s1, "xq_sb", [P, 4, DM])
        xnnb_sb = sbuf(s1, "xnnb_sb", [P, 4, DM], BF16)
        hfm_sb = sbuf(s1, "hfm_sb", [P, NCD, TW], BF16)
        xpfm_sb = sbuf(s1, "xpfm_sb", [P, NCD, TW])
        roll_sb = sbuf(s1, "roll_sb", [P, NCI, TW + 3], BF16)
        xca_sb = sbuf(s1, "xca_sb", [P, 4, TW], BF16)
        xcb_sb = sbuf(s1, "xcb_sb", [P, 4, TW], BF16)
        xss_sb = sbuf(s1, "xss_sb", [P, 4, TW])
        xsbfs_sb = sbuf(s1, "xsbfs_sb", [P, 4, TW], BF16)
        sgs_sb = sbuf(s1, "sgs_sb", [P, 4, TW])
        w1b_sb = [sbuf(s1, f"w1b{k}", [P, 4, NCD * P], BF16) for k in range(2)]
        ps1 = ExitStack()
        pbA = ps1.enter_context(nc.psum_tensor("pbA", [P, TW], F32))
        pbB = ps1.enter_context(nc.psum_tensor("pbB", [P, TW], F32))
        ptb0 = ps1.enter_context(nc.psum_tensor("ptb0", [P, 2 * TW], BF16))
        ptb1 = ps1.enter_context(nc.psum_tensor("ptb1", [P, 2 * TW], BF16))
        ptf0 = ps1.enter_context(nc.psum_tensor("ptf0", [P, TW], F32))
        ptf1 = ps1.enter_context(nc.psum_tensor("ptf1", [P, TW], F32))

        @blk.gpsimd
        def _(g):
            cx.wait(g, "spl")
            cx.mark(g.memset(roll_sb[:, :, 0:3], 0.0), "gp")

        xt_done = [0, 0]
        xpfm_ready = 0
        pe_win_end = 0
        pe_tr_end_prev = 0
        roll_ready = cx.cnt["gp"]
        sgspill_done = 0
        convspill_done = 0
        xp_spill_done = 0

        for i in range(n_ext_t):
            own = i >= n_pad_t
            slot = i % 2
            sxn = f"sx{slot}"
            njt = NCF if own else NCI
            nwb = (njt + 3) // 4

            @blk.sync
            def _(s):
                if i >= 2:
                    cx.wait(s, "dv", xt_done[slot])
                for sub in range(4):
                    r0 = 3 + i * TW + sub * P
                    cx.mark(s.dma_start(out=xt_sb[slot][:, sub, :],
                                        in_=x_in[r0:r0 + P, :]), sxn, 16)

            ntr = 32 + (32 if own else 0)
            pe_after_tr = cx.cnt["pe"] + ntr
            swbase = [cx.cnt["sw0"], cx.cnt["sw1"]]

            @blk.gpsimd
            def _(g):
                for wb in range(nwb):
                    ws = wb % 2
                    if wb >= 2:
                        cx.wait(g, "pe", pe_after_tr + 4 * (wb - 1))
                    elif i > 0:
                        cx.wait(g, "pe", pe_win_end)
                    cx.mark(g.dma_start(out=w1b_sb[ws][:],
                                        in_=w_in_l[:, wb * 4:(wb + 1) * 4, :]),
                            f"sw{ws}", 16)

            @blk.vector
            def _(v):
                cx.wait(v, sxn)
                if i == 0:
                    cx.wait(v, "spl")
                for sub in range(4):
                    cx.mark(v.reduce_sum(st_sb[:, sub, 0:1],
                                         xt_sb[slot][:, sub, :], axis=AXX), "dv")

            @blk.scalar
            def _(s):
                cx.wait(s, "dv")
                cx.wait(s, "pe", pe_tr_end_prev)
                for sub in range(4):
                    cx.mark(s.activation(xq_sb[:, sub, :],
                                         xt_sb[slot][:, sub, :], AF.Square), "sc")

            @blk.vector
            def _(v):
                cx.wait(v, "sc")
                for sub in range(4):
                    st = st_sb[:, sub, :]
                    cx.mark(v.reduce_sum(st[:, 1:2], xq_sb[:, sub, :], axis=AXX), "dv")
                    cx.wait(v, "dv")
                    cx.mark(v.tensor_scalar(st[:, 2:3], st[:, 0:1], RDM, None,
                                            ALU.mult), "dv")
                    cx.mark(v.tensor_scalar(st[:, 3:4], st[:, 1:2], RDM, None,
                                            ALU.mult), "dv")
                    cx.wait(v, "dv")
                    cx.mark(v.scalar_tensor_tensor(st[:, 4:5], st[:, 2:3], st[:, 2:3],
                                                   st[:, 3:4], ALU.mult, ALU.subtract),
                            "dv")

            @blk.scalar
            def _(s):
                cx.wait(s, "dv")
                for sub in range(4):
                    st = st_sb[:, sub, :]
                    cx.mark(s.activation(st[:, 5:6], st[:, 4:5], AF.Ln, bias=EPS,
                                         scale=-1.0), "sc")
                    cx.wait(s, "sc")
                    cx.mark(s.activation(st[:, 6:7], st[:, 5:6], AF.Exp, scale=-0.5),
                            "sc")

            @blk.vector
            def _(v):
                cx.wait(v, "sc")
                for sub in range(4):
                    st = st_sb[:, sub, :]
                    cx.mark(v.tensor_scalar(st[:, 7:8], st[:, 2:3], st[:, 6:7], -1.0,
                                            ALU.mult, ALU.mult), "dv")
                    cx.wait(v, "dv")
                    cx.mark(v.tensor_scalar(xn_sb[:, sub, :], xt_sb[slot][:, sub, :],
                                            st[:, 6:7], st[:, 7:8], ALU.mult, ALU.add),
                            "dv")
                    cx.wait(v, "dv")
                    cx.mark(v.reduce_sum(st[:, 8:9], xn_sb[:, sub, :], axis=AXX), "dv")

            @blk.scalar
            def _(s):
                cx.wait(s, "dv")
                for sub in range(4):
                    cx.mark(s.activation(xq_sb[:, sub, :], xn_sb[:, sub, :],
                                         AF.Square), "sc")

            @blk.vector
            def _(v):
                cx.wait(v, "sc")
                for sub in range(4):
                    st = st_sb[:, sub, :]
                    cx.mark(v.reduce_sum(st[:, 9:10], xq_sb[:, sub, :], axis=AXX), "dv")
                    cx.wait(v, "dv")
                    cx.mark(v.tensor_scalar(st[:, 10:11], st[:, 8:9], RDM, None,
                                            ALU.mult), "dv")
                    cx.mark(v.tensor_scalar(st[:, 11:12], st[:, 9:10], RDM, None,
                                            ALU.mult), "dv")
                    cx.wait(v, "dv")
                    cx.mark(v.scalar_tensor_tensor(st[:, 12:13], st[:, 10:11],
                                                   st[:, 10:11], st[:, 11:12],
                                                   ALU.mult, ALU.subtract), "dv")

            @blk.scalar
            def _(s):
                cx.wait(s, "dv")
                for sub in range(4):
                    st = st_sb[:, sub, :]
                    cx.mark(s.activation(st[:, 13:14], st[:, 12:13], AF.Ln, bias=EPS,
                                         scale=-1.0), "sc")
                    cx.wait(s, "sc")
                    cx.mark(s.activation(st[:, 14:15], st[:, 13:14], AF.Exp,
                                         scale=-0.5), "sc")

            @blk.vector
            def _(v):
                cx.wait(v, "sc")
                for sub in range(4):
                    st = st_sb[:, sub, :]
                    cx.mark(v.tensor_scalar(st[:, 15:16], st[:, 10:11], st[:, 14:15],
                                            -1.0, ALU.mult, ALU.mult), "dv")
                    cx.wait(v, "dv")
                    cx.mark(v.tensor_scalar(xnnb_sb[:, sub, :], xn_sb[:, sub, :],
                                            st[:, 14:15], st[:, 15:16], ALU.mult,
                                            ALU.add), "dv")
                    if own:
                        cx.mark(v.tensor_scalar(st[:, 2:3], st[:, 6:7], 1.0, None,
                                                ALU.add), "dv")
                        cx.wait(v, "dv")
                        cx.mark(v.tensor_scalar(xq_sb[:, sub, :],
                                                xt_sb[slot][:, sub, :], st[:, 2:3],
                                                st[:, 7:8], ALU.mult, ALU.add), "dv")
                xt_done[slot] = cx.cnt["dv"]

            dv_tr_base = cx.cnt["dv"]
            pe_tr_base = cx.cnt["pe"]

            @blk.tensor
            def _(t):
                cx.wait(t, "dv", dv_tr_base)
                cx.wait(t, "swp", 48)
                dv0 = cx.cnt["dv"]
                k = 0
                for sub in range(4):
                    for dc in range(NCD):
                        bank = [ptb0, ptb1][k % 2]
                        if k >= 2:
                            cx.wait(t, "dv", dv0 + (k - 1))
                        cx.mark(t.transpose(bank[:, 0:P],
                                            xnnb_sb[:, sub, dc * P:(dc + 1) * P],
                                            idb[:]), "pe")
                        k += 1
                if own:
                    for k2 in range(32):
                        sub, dc = k2 // NCD, k2 % NCD
                        bank = [ptf0, ptf1][k % 2]
                        if k2 >= 2:
                            cx.wait(t, "dv", dv0 + 32 + (k2 - 1))
                        cx.mark(t.transpose(bank[:, 0:P],
                                            xq_sb[:, sub, dc * P:(dc + 1) * P],
                                            idf[:]), "pe")
                        k += 1

            @blk.vector
            def _(v):
                if i >= 1:
                    cx.wait(v, "pe", pe_win_end)
                    cx.wait(v, "sxp", xp_spill_done)
                k = 0
                for sub in range(4):
                    for dc in range(NCD):
                        bank = [ptb0, ptb1][k % 2]
                        cx.wait(v, "pe", pe_tr_base + k + 1)
                        cx.mark(v.tensor_scalar(hfm_sb[:, dc, sub * P:(sub + 1) * P],
                                                bank[:, 0:P], gam_sb[:, dc:dc + 1],
                                                bet_sb[:, dc:dc + 1],
                                                ALU.mult, ALU.add), "dv")
                        k += 1
                if own:
                    for k2 in range(32):
                        sub, dc = k2 // NCD, k2 % NCD
                        bank = [ptf0, ptf1][k2 % 2]
                        cx.wait(v, "pe", pe_tr_base + 32 + k2 + 1)
                        cx.mark(v.tensor_copy(xpfm_sb[:, dc, sub * P:(sub + 1) * P],
                                              bank[:, 0:P]), "dv")
            assert cx.cnt["pe"] == pe_tr_base + ntr == pe_after_tr
            pe_tr_end_prev = cx.cnt["pe"]
            hfm_ready = cx.cnt["dv"]
            xpfm_ready = cx.cnt["dv"]

            if own:
                j = i - n_pad_t

                @blk.sync
                def _(s):
                    cx.wait(s, "dv", xpfm_ready)
                    cx.dma_seq(s, "sxp", xpxn_sp[:, :, j * TW:(j + 1) * TW],
                               xpfm_sb[:])
                xp_spill_done = cx.cnt["sxp"]

            pe_mm_base = cx.cnt["pe"]
            sc_ev_base = cx.cnt["sc"]

            @blk.tensor
            def _(t):
                cx.wait(t, "dv", hfm_ready)
                cx.wait(t, "gp", roll_ready)
                for jt in range(njt):
                    wb, ws = jt // 4, (jt // 4) % 2
                    cx.wait(t, f"sw{ws}", swbase[ws] + 16 * (wb // 2 + 1))
                    bank = [pbA, pbB][jt % 2]
                    if jt >= 2:
                        cx.wait(t, "sc", sc_ev_base + (jt - 1))
                    for dc in range(NCD):
                        ins = t.matmul(bank[:],
                                       w1b_sb[ws][:, jt % 4, dc * P:(dc + 1) * P],
                                       hfm_sb[:, dc, :],
                                       start=(dc == 0), stop=(dc == NCD - 1))
                    cx.mark(ins, "pe")
            pe_win_end = cx.cnt["pe"]
            assert pe_win_end == pe_mm_base + njt

            ngrp = njt // 4
            for grp in range(ngrp):
                gate_grp = grp >= 4

                @blk.scalar
                def _(s):
                    cx.wait(s, "gp", roll_ready)
                    if gate_grp:
                        cx.wait(s, "ssg", sgspill_done)
                    for q in range(4):
                        jt = grp * 4 + q
                        bank = [pbA, pbB][jt % 2]
                        cx.wait(s, "pe", pe_mm_base + jt + 1)
                        if jt < NCI:
                            ins = s.copy(roll_sb[:, jt, 3:3 + TW], bank[:])
                        else:
                            ins = s.activation(sgs_sb[:, q, :], bank[:], AF.Silu)
                        cx.mark(ins, "sc")

                if gate_grp and own:
                    j = i - n_pad_t

                    @blk.sync
                    def _(s):
                        cx.wait(s, "sc")
                        cx.dma_seq(s, "ssg",
                                   sg_sp[:, (grp - 4) * 4:(grp - 4) * 4 + 4,
                                         j * TW:(j + 1) * TW], sgs_sb[:])
                    sgspill_done = cx.cnt["ssg"]
            assert cx.cnt["sc"] == sc_ev_base + njt
            sc_roll_done = cx.cnt["sc"]

            for grp in range(4):
                cbase = grp * 4

                @blk.scalar
                def _(s):
                    cx.wait(s, "sc", sc_roll_done)
                    cx.wait(s, "dv")
                    for q in range(4):
                        c = cbase + q
                        cx.mark(s.mul(xca_sb[:, q, :], roll_sb[:, c, 0:TW],
                                      cwv_sb[:, c * 4:c * 4 + 1]), "sc")

                @blk.vector
                def _(v):
                    cx.wait(v, "sc")
                    for q in range(4):
                        c = cbase + q
                        for kk in (1, 2):
                            srcb, dstb = (xca_sb, xcb_sb) if kk == 1 else (xcb_sb, xca_sb)
                            cx.mark(v.scalar_tensor_tensor(
                                dstb[:, q, :], roll_sb[:, c, kk:kk + TW],
                                cwv_sb[:, c * 4 + kk:c * 4 + kk + 1],
                                srcb[:, q, :], ALU.mult, ALU.add), "dv")
                            cx.wait(v, "dv")
                        cx.mark(v.scalar_tensor_tensor(
                            xcb_sb[:, q, :], roll_sb[:, c, 3:3 + TW],
                            cwv_sb[:, c * 4 + 3:c * 4 + 4],
                            xca_sb[:, q, :], ALU.mult, ALU.add), "dv")

                @blk.scalar
                def _(s):
                    cx.wait(s, "dv")
                    cx.wait(s, "scv", convspill_done)
                    for q in range(4):
                        c = cbase + q
                        cx.mark(s.activation(xss_sb[:, q, :], xcb_sb[:, q, :],
                                             AF.Silu, bias=cb_sb[:, c:c + 1]), "sc")

                @blk.vector
                def _(v):
                    cx.wait(v, "sc")
                    for q in range(4):
                        cx.mark(v.tensor_copy(xsbfs_sb[:, q, :], xss_sb[:, q, :]),
                                "dv")

                @blk.sync
                def _(s):
                    cx.wait(s, "dv")
                    cx.dma_seq(s, "scv",
                               xsbf_sp[:, cbase:cbase + 4, i * TW:(i + 1) * TW],
                               xsbfs_sb[:])
                    if own:
                        j = i - n_pad_t
                        cx.dma_seq(s, "scv",
                                   xs_sp[:, cbase:cbase + 4, j * TW:(j + 1) * TW],
                                   xss_sb[:])
                convspill_done = cx.cnt["scv"]

            @blk.gpsimd
            def _(g):
                cx.wait(g, "dv")
                cx.wait(g, "gp")
                if i == n_pad_t - 1:
                    cx.wait(g, "spl")
                    for c in range(NCI):
                        cx.mark(g.tensor_tensor(roll_sb[:, c, 0:3],
                                                roll_sb[:, c, TW:TW + 3],
                                                m3_sb[:], ALU.mult), "gp")
                elif i < n_ext_t - 1:
                    for c in range(NCI):
                        cx.mark(g.tensor_copy(roll_sb[:, c, 0:3],
                                              roll_sb[:, c, TW:TW + 3]), "gp")
            roll_ready = cx.cnt["gp"]

        ps1.close()
        s1.close()
        join()

        s2 = ExitStack()
        wd_sb = sbuf(s2, "wd_sb", [P, NCI * NCI * P], BF16)
        xsb_sb = [sbuf(s2, f"xsb{k}", [P, NCI, TW], BF16) for k in range(2)]
        bxm_sb = [sbuf(s2, f"bxm{k}", [NST, TW]) for k in range(2)]
        spa_sb = sbuf(s2, "spa_sb", [P, TW], BF16)
        spb_sb = sbuf(s2, "spb_sb", [P, TW], BF16)
        spf_sb = [sbuf(s2, f"spf{k}", [P, TW], BF16) for k in range(2)]
        bxp_sb = sbuf(s2, "bxp_sb", [NST, TW])
        da_sb = sbuf(s2, "da_sb", [NST, half])
        bx_sb = sbuf(s2, "bx_sb", [NST, half])
        hs_sb = sbuf(s2, "hs_sb", [NST, half])
        ps2 = ExitStack()
        pb2 = [ps2.enter_context(nc.psum_tensor(f"p2{k}", [P, TW], F32))
               for k in range(5)]
        pbA2, pbB2, pbD, pbO, pbO2 = pb2

        @blk.gpsimd
        def _(g):
            cx.mark(g.dma_start(out=wd_sb[:],
                                in_=wd_l.rearrange("p a b -> p (a b)")), "swp", 16)
        wd_ready = cx.cnt["swp"]

        xsb_done = [0, 0]
        scan_a_done = 0
        sc_delta_prev = 0
        dv_bx_prev = 0

        for i in range(n_ext_t):
            slot = i % 2
            sldn = f"sld{slot}"
            hpass = i // half_t
            hoff = (i % half_t) * TW

            ee_lo = i * NE // n_ext_t
            ee_hi = (i + 1) * NE // n_ext_t
            if ee_hi > ee_lo:
                @blk.gpsimd
                def _(g):
                    for ee in range(ee_lo, ee_hi):
                        cx.dma_seq(g, "swc", w1b_sp[ee], w1_l[ee])
                        cx.dma_seq(g, "swc", w2b_sp[ee], w2_l[ee])
                        cx.dma_seq(g, "swc", w3b_sp[ee], w3_l[ee])

            @blk.sync
            def _(s):
                if i >= 2:
                    cx.wait(s, "pe", xsb_done[slot])
                cx.mark(s.dma_start(out=xsb_sb[slot][:],
                                    in_=xsbf_sp[:, :, i * TW:(i + 1) * TW]), sldn, 16)
                cx.mark(s.dma_start(out=bxm_sb[slot][:],
                                    in_=bx_mask[:, i * TW:(i + 1) * TW]), sldn, 16)
            sld_now = cx.cnt[sldn]

            pe_wd_base = cx.cnt["pe"]
            sc_sp_base = cx.cnt["sc"]
            dv_sp_base = cx.cnt["dv"]

            pe_of_wd = []
            pe_of_ones = []

            @blk.tensor
            def _(t):
                cx.wait(t, sldn, sld_now)
                cx.wait(t, "swp", wd_ready)
                if hpass == 1 and i % half_t == 0:
                    cx.wait(t, "dv", scan_a_done)

                def ones_mm(jo):
                    cx.wait(t, "dv", dv_sp_base + jo + 1)
                    if jo == 0:
                        cx.wait(t, "sc", sc_delta_prev)
                    pe_of_ones.append(
                        cx.mark(t.matmul(pbD[0:1, :], onesc_sb[:], spf_sb[jo % 2][:],
                                         start=(jo == 0), stop=(jo == NCI - 1)),
                                "pe"))

                for jt in range(NCI):
                    bank = [pbA2, pbB2][jt % 2]
                    if jt >= 2:
                        cx.wait(t, "sc", sc_sp_base + 4 * (jt - 1))
                    for kc in range(NCI):
                        ins = t.matmul(bank[:],
                                       wd_sb[:, (jt * NCI + kc) * P:(jt * NCI + kc + 1) * P],
                                       xsb_sb[slot][:, kc, :],
                                       start=(kc == 0), stop=(kc == NCI - 1))
                    pe_of_wd.append(cx.mark(ins, "pe"))
                    if jt >= 2:
                        ones_mm(jt - 2)
                ones_mm(NCI - 2)
                ones_mm(NCI - 1)
            pe_ones = cx.cnt["pe"]

            @blk.scalar
            def _(s):
                for jt in range(NCI):
                    bank = [pbA2, pbB2][jt % 2]
                    cx.wait(s, "pe", pe_of_wd[jt])
                    if jt >= 1:
                        cx.wait(s, "dv", dv_sp_base + jt)
                    cx.mark(s.activation(spa_sb[:], bank[:], AF.Abs,
                                         bias=bd_sb[:, jt:jt + 1]), "sc")
                    cx.wait(s, "sc")
                    cx.mark(s.activation(spb_sb[:], spa_sb[:], AF.Exp, scale=-1.0),
                            "sc")
                    cx.wait(s, "sc")
                    cx.mark(s.activation(spa_sb[:], spb_sb[:], AF.Ln, bias=1.0), "sc")
                    cx.wait(s, "sc")
                    cx.mark(s.activation(spb_sb[:], bank[:], AF.Relu,
                                         bias=bd_sb[:, jt:jt + 1]), "sc")

            @blk.vector
            def _(v):
                for jt in range(NCI):
                    cx.wait(v, "sc", sc_sp_base + 4 * (jt + 1))
                    if jt >= 2:
                        cx.wait(v, "pe", pe_of_ones[jt - 2])
                    cx.mark(v.tensor_add(spf_sb[jt % 2][:], spa_sb[:], spb_sb[:]),
                            "dv")
            assert cx.cnt["dv"] == dv_sp_base + NCI


            @blk.scalar
            def _(s):
                cx.wait(s, "pe", pe_ones)
                cx.mark(s.activation(delta_sb[:], pbD[0:1, :], AF.Copy, scale=RDI),
                        "sc")
            sc_delta = cx.cnt["sc"]
            sc_delta_prev = sc_delta

            @blk.tensor
            def _(t):
                cx.wait(t, "sc", max(sc_delta, arow_ready))
                cx.wait(t, "dv", dv_bx_prev)
                cx.mark(t.matmul(pbO[0:16, :], arow_sb[:], delta_sb[:],
                                 start=True, stop=True), "pe")
            pe_da = cx.cnt["pe"]

            @blk.scalar
            def _(s):
                cx.wait(s, "pe", pe_da)
                cx.mark(s.activation(da_sb[:, hoff:hoff + TW], pbO[0:16, :],
                                     AF.Exp), "sc")
            sc_da = cx.cnt["sc"]

            @blk.tensor
            def _(t):
                cx.wait(t, "sc", sc_da)
                t.matmul(pbO[0:16, :], ones_sb[0:1, 0:NST], delta_sb[:],
                         start=True, stop=True)
                for kc in range(NCI):
                    ins = t.matmul(pbO2[0:16, :], bt_sb[:, kc * NST:(kc + 1) * NST],
                                   xsb_sb[slot][:, kc, :],
                                   start=(kc == 0), stop=(kc == NCI - 1))
                cx.mark(ins, "pe")
            pe_bx = cx.cnt["pe"]

            @blk.vector
            def _(v):
                cx.wait(v, "pe", pe_bx)
                cx.wait(v, sldn, sld_now)
                cx.mark(v.tensor_copy(bxp_sb[:], pbO2[0:16, :]), "dv")
                cx.wait(v, "dv")
                cx.mark(v.tensor_mul(bxp_sb[:], bxp_sb[:], pbO[0:16, :]), "dv")
                cx.wait(v, "dv")
                cx.mark(v.tensor_mul(bx_sb[:, hoff:hoff + TW], bxp_sb[:],
                                     bxm_sb[slot][:]), "dv")
            dv_bx_prev = cx.cnt["dv"]
            xsb_done[slot] = pe_bx

            if i == half_t - 1:
                @blk.vector
                def _(v):
                    cx.wait(v, "dv")
                    cx.mark(v.tensor_tensor_scan(hs_sb[:], da_sb[:], bx_sb[:], 0.0,
                                                 ALU.mult, ALU.add), "dv")
                    cx.wait(v, "dv")
                    cx.mark(v.tensor_copy(hmid_sb[:], hs_sb[:, half - 1:half]), "dv")
                scan_a_done = cx.cnt["dv"]
            elif i == n_ext_t - 1:
                @blk.vector
                def _(v):
                    cx.wait(v, "dv")
                    cx.mark(v.tensor_tensor_scan(hs_sb[:], da_sb[:], bx_sb[:],
                                                 hmid_sb[:], ALU.mult, ALU.add), "dv")
                    cx.wait(v, "dv")
                    cx.mark(v.tensor_copy(hsb_sb[:], hs_sb[:]), "dv")

        if dbg:
            @blk.sync
            def _(s):
                cx.wait(s, "dv")
                cx.dma_seq(s, "sst", hs_dbg, hs_sb[:])
                cx.dma_seq(s, "sst", da_dbg, da_sb[:])
                cx.dma_seq(s, "sst", bxd_dbg, bx_sb[:])

        if upto == "A":
            @blk.sync
            def _(s):
                cx.wait(s, "dv")
                cx.dma_seq(s, "sst", y_out[0:NST, 0:min(half, DM)],
                           hs_sb[:, 0:min(half, DM)])
                for nm in ["sst", "sxp", "ssg", "scv"]:
                    cx.wait(s, nm)
            ps2.close()
            s2.close()
            return nc

        ps2.close()
        s2.close()
        join()

        s3 = ExitStack()
        wo_sb = sbuf(s3, "wo_sb", [P, NCD * NCI * P], BF16)
        xsr_sb = [sbuf(s3, f"xsr{k}", [P, 4, TW]) for k in range(2)]
        sgr_sb = [sbuf(s3, f"sgr{k}", [P, 4, TW]) for k in range(2)]
        xpr_sb = [sbuf(s3, f"xpr{k}", [P, NCD, TW]) for k in range(2)]
        ygf_sb = sbuf(s3, "ygf_sb", [P, TW])
        ygb_sb = sbuf(s3, "ygb_sb", [P, NCI, TW], BF16)
        x1t_sb = sbuf(s3, "x1t_sb", [P, NCD, TW])
        x1b_sb = sbuf(s3, "x1b_sb", [P, NCD, TW], BF16)
        x1q_sb = sbuf(s3, "x1q_sb", [P, NCD, TW], BF16)
        xn2f_sb = sbuf(s3, "xn2f_sb", [P, NCD, TW])
        xn2b_sb = sbuf(s3, "xn2b_sb", [P, NCD, TW], BF16)
        row_sb = sbuf(s3, "row_sb", [1, 8 * TW])
        mcol_sb = sbuf(s3, "mcol_sb", [P, 8])
        eq1_sb = sbuf(s3, "eq1_sb", [P, NE])
        msk_sb = sbuf(s3, "msk_sb", [P, NE])
        cwt_sb = sbuf(s3, "cwt_sb", [P, NE])
        cwst_sb = sbuf(s3, "cwst_sb", [NE, P])
        ps3 = ExitStack()
        pcA = ps3.enter_context(nc.psum_tensor("pcA", [P, TW], F32))
        pcB = ps3.enter_context(nc.psum_tensor("pcB", [P, TW], F32))
        pcD1 = ps3.enter_context(nc.psum_tensor("pcD1", [P, TW], F32))
        pcD2 = ps3.enter_context(nc.psum_tensor("pcD2", [P, TW], F32))
        pcO = ps3.enter_context(nc.psum_tensor("pcO", [P, TW], F32))
        pcO2 = ps3.enter_context(nc.psum_tensor("pcO2", [P, TW], F32))
        pcR = ps3.enter_context(nc.psum_tensor("pcR", [P, TW], F32))
        pcT = ps3.enter_context(nc.psum_tensor("pcT", [P, TW], F32))

        @blk.gpsimd
        def _(g):
            cx.mark(g.dma_start(out=wo_sb[:], in_=w_out_l), "swp", 16)
        wo_ready = cx.cnt["swp"]

        ldslot_done = [0, 0]
        dv_rep_prev = 0
        dv_x1_prev = 0
        pe_cast_prev = 0
        dv_router_prev = 0
        dv_ygb_prev = 0

        r_mu = row_sb[0:1, 0 * TW:1 * TW]
        r_m2 = row_sb[0:1, 1 * TW:2 * TW]
        r_nv = row_sb[0:1, 2 * TW:3 * TW]
        r_rs = row_sb[0:1, 3 * TW:4 * TW]
        r_b = row_sb[0:1, 4 * TW:5 * TW]

        for j in range(n_own_t):
            slot = j % 2
            sxwn = f"sxw{slot}"

            @blk.sync
            def _(s):
                if j >= 2:
                    cx.wait(s, "dv", ldslot_done[slot])
                cx.wait(s, sxwn)
                cx.mark(s.dma_start(out=xpr_sb[slot][:],
                                    in_=xpxn_sp[:, :, j * TW:(j + 1) * TW]),
                        sxwn, 16)
            ld_xp = cx.cnt[sxwn]

            pe_ys = []
            dv_ysbase = cx.cnt["dv"]
            for grp in range(4):
                gslot = grp % 2

                @blk.sync
                def _(s):
                    if j >= 2 or grp >= 2:
                        cx.wait(s, "dv")
                    cx.wait(s, sxwn)
                    cx.mark(s.dma_start(
                        out=xsr_sb[gslot][:],
                        in_=xs_sp[:, grp * 4:(grp + 1) * 4, j * TW:(j + 1) * TW]),
                        sxwn, 16)
                    cx.mark(s.dma_start(
                        out=sgr_sb[gslot][:],
                        in_=sg_sp[:, grp * 4:(grp + 1) * 4, j * TW:(j + 1) * TW]),
                        sxwn, 16)
                ld_now = cx.cnt[sxwn]

                pe_g = []

                @blk.tensor
                def _(t):
                    cx.wait(t, "swp", wo_ready)
                    if j == 0 and grp == 0:
                        cx.wait(t, "dv")
                        cx.wait(t, "spl")
                    for q in range(4):
                        c = grp * 4 + q
                        bank = [pcA, pcB][c % 2]
                        nprev = len(pe_ys) + q
                        if nprev >= 2:
                            cx.wait(t, "dv", dv_ysbase + 2 * (nprev - 1))
                        elif j > 0 and nprev == 0:
                            cx.wait(t, "dv", dv_ygb_prev)
                        pe_g.append(cx.mark(
                            t.matmul(bank[:], ct_sb[:, c * P:(c + 1) * P],
                                     hsb_sb[:, j * TW:(j + 1) * TW],
                                     start=True, stop=True), "pe"))

                @blk.vector
                def _(v):
                    cx.wait(v, sxwn, ld_now)
                    for q in range(4):
                        c = grp * 4 + q
                        bank = [pcA, pcB][c % 2]
                        cx.wait(v, "pe", pe_g[q])
                        cx.wait(v, "dv")
                        cx.mark(v.scalar_tensor_tensor(
                            ygf_sb[:], xsr_sb[gslot][:, q, :], dvec_sb[:, c:c + 1],
                            bank[:], ALU.mult, ALU.add), "dv")
                        cx.wait(v, "dv")
                        cx.mark(v.tensor_mul(ygb_sb[:, c, :], ygf_sb[:],
                                             sgr_sb[gslot][:, q, :]), "dv")
                pe_ys += pe_g
            dv_yg = cx.cnt["dv"]
            ldslot_done[slot] = dv_yg

            pe_wo = []
            dv_wobase = cx.cnt["dv"]

            @blk.tensor
            def _(t):
                cx.wait(t, "dv", dv_yg)
                for dm in range(NCD):
                    bank = [pcA, pcB][dm % 2]
                    if dm >= 2:
                        cx.wait(t, "dv", dv_wobase + dm - 1)
                    for kc in range(NCI):
                        ins = t.matmul(bank[:],
                                       wo_sb[:, (dm * NCI + kc) * P:(dm * NCI + kc + 1) * P],
                                       ygb_sb[:, kc, :],
                                       start=(kc == 0), stop=(kc == NCI - 1))
                    pe_wo.append(cx.mark(ins, "pe"))
            dv_ygb_prev = dv_yg

            @blk.vector
            def _(v):
                cx.wait(v, sxwn, ld_xp)
                cx.wait(v, "sc", dv_x1_prev)
                for dm in range(NCD):
                    bank = [pcA, pcB][dm % 2]
                    cx.wait(v, "pe", pe_wo[dm])
                    cx.mark(v.tensor_add(x1t_sb[:, dm, :], bank[:],
                                         xpr_sb[slot][:, dm, :]), "dv")
            dv_x1 = cx.cnt["dv"]

            @blk.scalar
            def _(s):
                cx.wait(s, "dv", dv_x1)
                cx.wait(s, "pe", pe_cast_prev)
                for dm in range(NCD):
                    cx.mark(s.copy(x1b_sb[:, dm, :], x1t_sb[:, dm, :]), "sc")
                    cx.wait(s, "sc")
                    cx.mark(s.activation(x1q_sb[:, dm, :], x1t_sb[:, dm, :],
                                         AF.Square), "sc")
            sc_x1cast = cx.cnt["sc"]
            dv_x1_prev = sc_x1cast

            @blk.sync
            def _(s):
                cx.wait(s, "dv", dv_x1)
                cx.dma_seq(s, "sxp", x1_sp[:, :, j * TW:(j + 1) * TW], x1t_sb[:])

            @blk.tensor
            def _(t):
                cx.wait(t, "sc", sc_x1cast)
                cx.wait(t, "dv", dv_rep_prev)
                for dm in range(NCD):
                    ins = t.matmul(pcD1[0:1, :], onesc_sb[:], x1b_sb[:, dm, :],
                                   start=(dm == 0), stop=(dm == NCD - 1))
                cx.mark(ins, "pe")
                for dm in range(NCD):
                    ins = t.matmul(pcD2[0:1, :], onesc_sb[:], x1q_sb[:, dm, :],
                                   start=(dm == 0), stop=(dm == NCD - 1))
                cx.mark(ins, "pe")
            pe_stats = cx.cnt["pe"]
            pe_cast_prev = pe_stats

            @blk.vector
            def _(v):
                cx.wait(v, "pe", pe_stats)
                cx.mark(v.tensor_scalar(r_mu, pcD1[0:1, :], RDM, None, ALU.mult), "dv")
                cx.mark(v.tensor_scalar(r_m2, pcD2[0:1, :], RDM, None, ALU.mult), "dv")
                cx.wait(v, "dv")
                cx.mark(v.tensor_mul(r_nv, r_mu, r_mu), "dv")
                cx.wait(v, "dv")
                cx.mark(v.tensor_tensor(r_nv, r_nv, r_m2, ALU.subtract), "dv")

            @blk.scalar
            def _(s):
                cx.wait(s, "dv")
                cx.mark(s.activation(r_rs, r_nv, AF.Ln, bias=EPS, scale=-1.0), "sc")
                cx.wait(s, "sc")
                cx.mark(s.activation(r_rs, r_rs, AF.Exp, scale=-0.5), "sc")

            @blk.vector
            def _(v):
                cx.wait(v, "sc")
                cx.mark(v.tensor_mul(r_b, r_mu, r_rs), "dv")
                cx.wait(v, "dv")
                cx.mark(v.tensor_scalar(r_b, r_b, -1.0, None, ALU.mult), "dv")
            dv_rows = cx.cnt["dv"]

            @blk.tensor
            def _(t):
                cx.wait(t, "dv", dv_rows)
                cx.mark(t.matmul(pcO[:], ones_sb[:], r_rs, start=True, stop=True),
                        "pe")
                cx.mark(t.matmul(pcO2[:], ones_sb[:], r_b, start=True, stop=True),
                        "pe")
            pe_rep = cx.cnt["pe"]

            @blk.vector
            def _(v):
                cx.wait(v, "pe", pe_rep)
                cx.wait(v, "ssg")
                for dm in range(NCD):
                    cx.mark(v.tensor_mul(xn2f_sb[:, dm, :], x1t_sb[:, dm, :],
                                         pcO[:]), "dv")
                    cx.wait(v, "dv")
                    cx.mark(v.tensor_add(xn2f_sb[:, dm, :], xn2f_sb[:, dm, :],
                                         pcO2[:]), "dv")
                    cx.wait(v, "dv")
                    cx.mark(v.tensor_copy(xn2b_sb[:, dm, :], xn2f_sb[:, dm, :]),
                            "dv")
            dv_rep_prev = cx.cnt["dv"]

            @blk.sync
            def _(s):
                cx.wait(s, "dv", dv_rep_prev)
                cx.dma_seq(s, "ssg", xn2_sp[:, :, j * TW:(j + 1) * TW], xn2b_sb[:])

            for sub in range(4):
                @blk.tensor
                def _(t):
                    cx.wait(t, "dv", dv_rep_prev)
                    cx.wait(t, "dv", dv_router_prev)
                    for dc in range(NCD):
                        t.matmul(pcR[:, 0:NE],
                                 xn2f_sb[:, dc, sub * P:(sub + 1) * P],
                                 wr_sb[:, dc * NE:(dc + 1) * NE],
                                 start=(dc == 0), stop=False)
                    cx.mark(t.matmul(pcR[:, 0:NE], ones_sb[0:1, 0:P],
                                     br_sb[:], start=False, stop=True), "pe")
                pe_rt = cx.cnt["pe"]

                @blk.vector
                def _(v):
                    cx.wait(v, "pe", pe_rt)
                    cx.mark(v.reduce_max(mcol_sb[:, 0:1], pcR[:, 0:NE], axis=AXX),
                            "dv")
                    cx.wait(v, "dv")
                    cx.mark(v.tensor_scalar(eq1_sb[:], pcR[:, 0:NE],
                                            mcol_sb[:, 0:1], None, ALU.is_equal),
                            "dv")
                    cx.wait(v, "dv")
                    cx.mark(v.scalar_tensor_tensor(msk_sb[:], eq1_sb[:], -1e30,
                                                   pcR[:, 0:NE], ALU.mult, ALU.add),
                            "dv")
                    cx.wait(v, "dv")
                    cx.mark(v.reduce_max(mcol_sb[:, 1:2], msk_sb[:], axis=AXX), "dv")
                    cx.wait(v, "dv")
                    cx.mark(v.tensor_scalar(msk_sb[:], msk_sb[:], mcol_sb[:, 1:2],
                                            None, ALU.is_equal), "dv")
                    cx.wait(v, "dv")
                    cx.mark(v.tensor_sub(mcol_sb[:, 2:3], mcol_sb[:, 1:2],
                                         mcol_sb[:, 0:1]), "dv")

                @blk.scalar
                def _(s):
                    cx.wait(s, "dv")
                    cx.mark(s.activation(mcol_sb[:, 3:4], mcol_sb[:, 2:3], AF.Exp),
                            "sc")

                @blk.vector
                def _(v):
                    cx.wait(v, "sc")
                    cx.mark(v.tensor_scalar(mcol_sb[:, 4:5], mcol_sb[:, 3:4], 1.0,
                                            None, ALU.add), "dv")
                    cx.wait(v, "dv")
                    cx.mark(v.reciprocal(mcol_sb[:, 5:6], mcol_sb[:, 4:5]), "dv")
                    cx.wait(v, "dv")
                    cx.mark(v.tensor_scalar(mcol_sb[:, 6:7], mcol_sb[:, 5:6], -1.0,
                                            1.0, ALU.mult, ALU.add), "dv")
                    cx.wait(v, "dv")
                    cx.mark(v.tensor_scalar(cwt_sb[:], eq1_sb[:], mcol_sb[:, 5:6],
                                            None, ALU.mult), "dv")
                    cx.wait(v, "dv")
                    cx.mark(v.scalar_tensor_tensor(cwt_sb[:], msk_sb[:],
                                                   mcol_sb[:, 6:7], cwt_sb[:],
                                                   ALU.mult, ALU.add), "dv")
                dv_cwt = cx.cnt["dv"]

                @blk.tensor
                def _(t):
                    cx.wait(t, "dv", dv_cwt)
                    cx.mark(t.transpose(pcT[0:NE, 0:P], cwt_sb[:], idf[:]), "pe")
                pe_cwt = cx.cnt["pe"]

                @blk.vector
                def _(v):
                    cx.wait(v, "pe", pe_cwt)
                    cx.wait(v, "scw0")
                    cx.mark(v.tensor_copy(cwst_sb[:], pcT[0:NE, 0:P]), "dv")
                dv_router_prev = cx.cnt["dv"]

                @blk.sync
                def _(s):
                    cx.wait(s, "dv", dv_router_prev)
                    cx.dma_seq(s, "scw0",
                               cw_sp[:, j * TW + sub * P:j * TW + (sub + 1) * P],
                               cwst_sb[:])

        ps3.close()
        s3.close()
        join()

        if upto == "C":
            @blk.sync
            def _(s):
                for nm in DMA_SEMS:
                    cx.wait(s, nm)
                cx.dma_seq(s, "sst", y_out[0:P, 0:DM],
                           sbuf(es, "dummy_sb", [P, DM]))
            return nc

        s4 = ExitStack()
        g_sb = sbuf(s4, "g_sb", [P, NCF, fw], BF16)
        xn2w_sb = [sbuf(s4, f"xn2w{k}", [P, NCD, fw], BF16) for k in range(2)]
        yacc_sb = sbuf(s4, "yacc_sb", [P, NCD, fw])
        cwrep_sb = sbuf(s4, "cwrep_sb", [P, fw])
        cwrow_sb = [sbuf(s4, f"cwrow{k}", [1, fw]) for k in range(2)]
        cwall_sb = sbuf(s4, "cwall_sb", [NE, fw])
        b3a_sb = sbuf(s4, "b3a_sb", [NE, DM])
        w1s_sb = [sbuf(s4, f"w1s{k}", [P, 4, NCD * P], BF16) for k in range(2)]
        w2s_sb = [sbuf(s4, f"w2s{k}", [P, 4, NCD * P], BF16) for k in range(2)]
        w3s_sb = [sbuf(s4, f"w3s{k}", [P, 4, NCD * P], BF16) for k in range(2)]
        b1e_sb = [sbuf(s4, f"b1e{k}", [P, NCF]) for k in range(2)]
        b2e_sb = [sbuf(s4, f"b2e{k}", [P, NCF]) for k in range(2)]
        s1t_sb = sbuf(s4, "s1t_sb", [P, 2, TW])
        ygf_c_sb = sbuf(s4, "ygf_c_sb", [P, 2, TW])
        oadd_sb = sbuf(s4, "oadd_sb", [P, NCD, P])
        ostg_sb = [sbuf(s4, f"ostg{k}", [P, DM]) for k in range(2)]
        x1g_sb = [sbuf(s4, f"x1g{k}", [P, NCD, P]) for k in range(2)]
        ps4 = ExitStack()
        pY = [ps4.enter_context(nc.psum_tensor(f"pY{k}", [P, TW], F32))
              for k in range(8)]

        @blk.sync
        def _(s):
            cx.dma_seq(s, "scw1", b3a_sb[:], b3_l)

        dv_gall_prev = 0
        dv_yacc_final = 0
        xn2w_done = [0, 0]
        pe_g_prev_end = 0
        pe_y_prev_end = 0
        dv_ostg = [0, 0]
        pe_tr_prev = 0
        NSILU = nsub * NCF
        NGOP = 2 * nsub * NCF

        for w in range(nwin):
            wslot = w % 2
            sxwn = f"sxw{wslot}"

            @blk.sync
            def _(s):
                if w >= 2:
                    cx.wait(s, "pe", xn2w_done[wslot])
                cx.wait(s, sxwn)
                cx.mark(s.dma_start(out=xn2w_sb[wslot][:],
                                    in_=xn2_sp[:, :, w * fw:(w + 1) * fw]),
                        sxwn, 16)
            ld_xn2w = cx.cnt[sxwn]

            for e in range(NE):
                eslot = e % 2

                @blk.sync
                def _(s):
                    cx.wait(s, "dv", dv_gall_prev)
                    cx.wait(s, "scw0")
                    cx.mark(s.dma_start(out=cwrow_sb[eslot][:],
                                        in_=cw_sp[e:e + 1, w * fw:(w + 1) * fw]),
                            "scw0", 16)
                    cx.mark(s.dma_start(out=b1e_sb[eslot][:], in_=b1_l[e]),
                            "scw0", 16)
                    cx.mark(s.dma_start(out=b2e_sb[eslot][:], in_=b2_l[e]),
                            "scw0", 16)
                ld_cw = cx.cnt["scw0"]

                @blk.tensor
                def _(t):
                    cx.wait(t, "scw0", ld_cw)
                    cx.wait(t, "dv", dv_gall_prev)
                    for ts in range(nsub):
                        cx.mark(t.matmul(pY[4 + (ts % 2)][:], ones_sb[:],
                                         cwrow_sb[eslot][0:1, ts * TW:(ts + 1) * TW],
                                         start=True, stop=True), "pe")
                pe_cwrep = cx.cnt["pe"]

                @blk.vector
                def _(v):
                    for ts in range(nsub):
                        cx.wait(v, "pe", pe_cwrep - nsub + ts + 1)
                        cx.mark(v.tensor_copy(cwrep_sb[:, ts * TW:(ts + 1) * TW],
                                              pY[4 + (ts % 2)][:]), "dv")
                dv_cwrep = cx.cnt["dv"]

                pe0 = cx.cnt["pe"]
                sc0 = cx.cnt["sc"]
                dv0 = cx.cnt["dv"]
                swb = {k: cx.cnt[k] for k in ("hw10", "hw11", "hw20", "hw21")}

                @blk.sync
                def _(s):
                    cx.wait(s, "swc", 48 * (e + 1))
                    for wb in range(NCF // 4):
                        ws = wb % 2
                        if wb >= 2:
                            cx.wait(s, "pe", pe0 + 2 * nsub * (4 * (wb - 1)))
                        elif pe_g_prev_end > 0:
                            cx.wait(s, "pe", pe_g_prev_end)
                        cx.mark(s.dma_start(
                            out=w1s_sb[ws][:],
                            in_=w1b_sp[e, wb * 4:(wb + 1) * 4]
                            .rearrange("f p d -> p f d")), "hw1%d" % ws, 16)
                        cx.mark(s.dma_start(
                            out=w2s_sb[ws][:],
                            in_=w2b_sp[e, wb * 4:(wb + 1) * 4]
                            .rearrange("f p d -> p f d")), "hw2%d" % ws, 16)

                @blk.tensor
                def _(t):
                    cx.wait(t, sxwn, ld_xn2w)
                    cx.wait(t, "dv", dv_cwrep)
                    for fblk in range(NCF):
                        wb = fblk // 4
                        ws = wb % 2
                        fi = fblk % 4
                        bsel = fblk % 2
                        if fi == 0:
                            cx.wait(t, "hw1%d" % ws,
                                    swb["hw1%d" % ws] + 16 * (wb // 2 + 1))
                            cx.wait(t, "hw2%d" % ws,
                                    swb["hw2%d" % ws] + 16 * (wb // 2 + 1))
                        for ts in range(nsub):
                            bank1 = pY[ts + 4 * bsel]
                            bank2 = pY[2 + ts + 4 * bsel]
                            if fblk >= 2:
                                cx.wait(t, "sc",
                                        sc0 + nsub * (fblk - 2) + ts + 1)
                                cx.wait(t, "dv",
                                        dv0 + 2 * nsub * (fblk - 2) + 2 * ts + 1)
                            for dc in range(NCD):
                                ins = t.matmul(
                                    bank1[:],
                                    w1s_sb[ws][:, fi, dc * P:(dc + 1) * P],
                                    xn2w_sb[wslot][:, dc, ts * TW:(ts + 1) * TW],
                                    start=(dc == 0), stop=(dc == NCD - 1))
                            cx.mark(ins, "pe")
                            for dc in range(NCD):
                                ins = t.matmul(
                                    bank2[:],
                                    w2s_sb[ws][:, fi, dc * P:(dc + 1) * P],
                                    xn2w_sb[wslot][:, dc, ts * TW:(ts + 1) * TW],
                                    start=(dc == 0), stop=(dc == NCD - 1))
                            cx.mark(ins, "pe")
                assert cx.cnt["pe"] == pe0 + 2 * nsub * NCF
                pe_g_prev_end = cx.cnt["pe"]
                xn2w_done[wslot] = cx.cnt["pe"]

                @blk.scalar
                def _(s):
                    for fblk in range(NCF):
                        bsel = fblk % 2
                        for ts in range(nsub):
                            cx.wait(s, "pe", pe0 + 2 * nsub * fblk + 2 * ts + 1)
                            if fblk >= 1:
                                cx.wait(s, "dv",
                                        dv0 + 2 * nsub * (fblk - 1) + 2 * ts + 1)
                            cx.mark(s.activation(
                                s1t_sb[:, ts, :], pY[ts + 4 * bsel][:], AF.Silu,
                                bias=b1e_sb[eslot][:, fblk:fblk + 1]), "sc")
                assert cx.cnt["sc"] == sc0 + NSILU

                @blk.vector
                def _(v):
                    for fblk in range(NCF):
                        bsel = fblk % 2
                        for ts in range(nsub):
                            cx.wait(v, "pe", pe0 + 2 * nsub * fblk + 2 * ts + 2)
                            cx.wait(v, "sc", sc0 + nsub * fblk + ts + 1)
                            cx.wait(v, "dv")
                            cx.mark(v.scalar_tensor_tensor(
                                ygf_c_sb[:, ts, :], pY[2 + ts + 4 * bsel][:],
                                b2e_sb[eslot][:, fblk:fblk + 1], s1t_sb[:, ts, :],
                                ALU.add, ALU.mult), "dv")
                            cx.wait(v, "dv")
                            cx.mark(v.tensor_mul(
                                g_sb[:, fblk, ts * TW:(ts + 1) * TW],
                                ygf_c_sb[:, ts, :],
                                cwrep_sb[:, ts * TW:(ts + 1) * TW]), "dv")
                assert cx.cnt["dv"] == dv0 + NGOP
                dv_gall = cx.cnt["dv"]

                for ts in range(nsub):
                    py0 = cx.cnt["pe"]
                    sw3b = {k: cx.cnt[k] for k in ("hw30", "hw31")}

                    @blk.scalar
                    def _(s):
                        for wb in range(NCF // 4):
                            ws = wb % 2
                            if wb >= 2:
                                cx.wait(s, "pe", py0 + 4 * (wb - 1))
                            elif pe_y_prev_end > 0:
                                cx.wait(s, "pe", pe_y_prev_end)
                            cx.mark(s.dma_start(
                                out=w3s_sb[ws][:],
                                in_=w3b_sp[e, wb * 4 * P:(wb + 1) * 4 * P, :]
                                .rearrange("(f p) d -> p f d", p=P)),
                                "hw3%d" % ws, 16)

                    @blk.tensor
                    def _(t):
                        cx.wait(t, "dv", dv_gall)
                        cx.wait(t, "sc")
                        if ts > 0 or e > 0:
                            cx.wait(t, "dv")
                        if e == 0 and ts == 0 and w > 0:
                            cx.wait(t, "dv", dv_yacc_final)
                        for fblk in range(NCF):
                            wb = fblk // 4
                            ws = wb % 2
                            fi = fblk % 4
                            if fi == 0:
                                cx.wait(t, "hw3%d" % ws,
                                        sw3b["hw3%d" % ws] + 16 * (wb // 2 + 1))
                            for dm in range(NCD):
                                ins = t.matmul(
                                    pY[dm][:],
                                    w3s_sb[ws][:, fi, dm * P:(dm + 1) * P],
                                    g_sb[:, fblk, ts * TW:(ts + 1) * TW],
                                    start=(fblk == 0), stop=(fblk == NCF - 1))
                            cx.mark(ins, "pe")
                    pe_y_prev_end = cx.cnt["pe"]
                    assert pe_y_prev_end == py0 + NCF

                    @blk.vector
                    def _(v):
                        cx.wait(v, "pe", pe_y_prev_end)
                        for dm in range(NCD):
                            if e == 0:
                                cx.mark(v.tensor_copy(
                                    yacc_sb[:, dm, ts * TW:(ts + 1) * TW],
                                    pY[dm][:]), "dv")
                            else:
                                cx.mark(v.tensor_add(
                                    yacc_sb[:, dm, ts * TW:(ts + 1) * TW],
                                    yacc_sb[:, dm, ts * TW:(ts + 1) * TW],
                                    pY[dm][:]), "dv")
                dv_gall_prev = cx.cnt["dv"]

            @blk.sync
            def _(s):
                cx.wait(s, "scw1")
                cx.mark(s.dma_start(out=cwall_sb[:],
                                    in_=cw_sp[:, w * fw:(w + 1) * fw]), "scw1", 16)
            ld_cwall = cx.cnt["scw1"]

            dv_b3base = cx.cnt["dv"]
            pe_b3 = []

            @blk.tensor
            def _(t):
                cx.wait(t, "scw1", ld_cwall)
                cx.wait(t, "dv", dv_b3base)
                for ts in range(nsub):
                    for dm in range(NCD):
                        pe_b3.append(cx.mark(t.matmul(
                            pY[dm][:], b3a_sb[0:NE, dm * P:(dm + 1) * P],
                            cwall_sb[0:NE, ts * TW:(ts + 1) * TW],
                            start=True, stop=True), "pe"))

            @blk.vector
            def _(v):
                k = 0
                for ts in range(nsub):
                    for dm in range(NCD):
                        cx.wait(v, "pe", pe_b3[k])
                        cx.mark(v.tensor_add(
                            yacc_sb[:, dm, ts * TW:(ts + 1) * TW],
                            yacc_sb[:, dm, ts * TW:(ts + 1) * TW],
                            pY[dm][:]), "dv")
                        k += 1
            dv_yacc_done = cx.cnt["dv"]

            for tg in range(fw // P):
                tslot = tg % 2

                @blk.sync
                def _(s):
                    cx.wait(s, f"sld{tslot}")
                    cx.mark(s.dma_start(
                        out=x1g_sb[tslot][:],
                        in_=x1_sp[:, :, w * fw + tg * P:w * fw + (tg + 1) * P]),
                        f"sld{tslot}", 16)
                ld_x1g = cx.cnt[f"sld{tslot}"]

                @blk.vector
                def _(v):
                    cx.wait(v, f"sld{tslot}", ld_x1g)
                    cx.wait(v, "dv", dv_yacc_done)
                    cx.wait(v, "pe", pe_tr_prev)
                    for dm in range(NCD):
                        cx.mark(v.tensor_add(
                            oadd_sb[:, dm, :],
                            yacc_sb[:, dm, tg * P:(tg + 1) * P],
                            x1g_sb[tslot][:, dm, :]), "dv")
                dv_oadd = cx.cnt["dv"]

                pe_trs = []

                @blk.tensor
                def _(t):
                    for dm in range(NCD):
                        bank = pY[dm % 2]
                        cx.wait(t, "dv", dv_oadd - NCD + dm + 1)
                        if dm >= 2:
                            cx.wait(t, "dv", dv_oadd + dm - 1)
                        pe_trs.append(cx.mark(
                            t.transpose(bank[0:P, 0:P], oadd_sb[:, dm, :], idf[:]),
                            "pe"))
                pe_tr_prev = cx.cnt["pe"]

                @blk.vector
                def _(v):
                    cx.wait(v, "sst", dv_ostg[tslot])
                    for dm in range(NCD):
                        bank = pY[dm % 2]
                        cx.wait(v, "pe", pe_trs[dm])
                        cx.mark(v.tensor_copy(
                            ostg_sb[tslot][:, dm * P:(dm + 1) * P],
                            bank[0:P, 0:P]), "dv")
                dv_ostg_done = cx.cnt["dv"]

                @blk.sync
                def _(s):
                    cx.wait(s, "dv", dv_ostg_done)
                    cx.dma_seq(s, "sst",
                               y_out[w * fw + tg * P:w * fw + (tg + 1) * P, :],
                               ostg_sb[tslot][:])
                dv_ostg[tslot] = cx.cnt["sst"]
            dv_yacc_final = cx.cnt["dv"]

        ps4.close()
        s4.close()
        join()

        @blk.sync
        def _(s):
            for nm in DMA_SEMS:
                cx.wait(s, nm)

    return nc



def _layout_weights(i):
    W = {}
    W["ident_f"] = np.eye(P, dtype=np.float32)
    W["ident_b"] = np.eye(P, dtype=np.float32).astype(BFNP)
    W["w_in_l"] = np.ascontiguousarray(
        i["L0_attn_W_in"].reshape(NCD, P, NCF, P).transpose(1, 2, 0, 3)
        .reshape(P, NCF, NCD * P))
    W["wd_l"] = np.ascontiguousarray(
        i["L0_attn_Wd"].reshape(NCI, P, NCI, P).transpose(1, 2, 0, 3)
        .reshape(P, NCI, NCI * P))
    W["w_out_l"] = np.ascontiguousarray(
        i["L0_attn_W_out"].reshape(NCI, P, NCD, P).transpose(1, 2, 0, 3)
        .reshape(P, NCD * NCI * P))
    W["bt_l"] = np.ascontiguousarray(
        i["L0_attn_B_mat"].T.reshape(NCI, P, NST).transpose(1, 0, 2)
        .reshape(P, NCI * NST))
    W["ct_l"] = np.ascontiguousarray(i["L0_attn_C_mat"].T)
    W["a_log_r"] = i["L0_attn_A_log"].reshape(1, NST)
    W["d_vec_l"] = np.ascontiguousarray(i["L0_attn_D_vec"].reshape(NCI, P).T)
    W["conv_w_l"] = np.ascontiguousarray(
        i["L0_attn_conv_w"][:, 0, :].reshape(NCI, P, 4).transpose(1, 0, 2)
        .reshape(P, NCI * 4))
    W["conv_b_l"] = np.ascontiguousarray(i["L0_attn_conv_b"].reshape(NCI, P).T)
    W["bd_l"] = np.ascontiguousarray(i["L0_attn_bd"].reshape(NCI, P).T)
    W["gamma_l"] = np.ascontiguousarray(i["L0_attn_ln_gamma"].reshape(NCD, P).T)
    W["beta_l"] = np.ascontiguousarray(i["L0_attn_ln_beta"].reshape(NCD, P).T)
    W["wr_l"] = np.ascontiguousarray(
        i["L0_ffn_Wr"].reshape(NCD, P, NE).transpose(1, 0, 2).reshape(P, NCD * NE))
    W["br_r"] = i["L0_ffn_br"].reshape(1, NE)
    W["ones_r"] = np.ones((1, P), np.float32)
    W["consts"] = np.tile(np.array([[EPS, RDI, -0.5, -1.0, 1.0, 0.5, 0.0, 0.0]], np.float32), (P, 1))
    W["w1_l"] = np.ascontiguousarray(
        i["L0_ffn_W1"].reshape(NE, NCD, P, NCF, P).transpose(0, 3, 2, 1, 4)
        .reshape(NE, NCF, P, NCD * P))
    W["w2_l"] = np.ascontiguousarray(
        i["L0_ffn_W2"].reshape(NE, NCD, P, NCF, P).transpose(0, 3, 2, 1, 4)
        .reshape(NE, NCF, P, NCD * P))
    W["w3_l"] = np.ascontiguousarray(i["L0_ffn_W3"])
    W["b1_l"] = np.ascontiguousarray(
        i["L0_ffn_b1"].reshape(NE, NCF, P).transpose(0, 2, 1))
    W["b2_l"] = np.ascontiguousarray(
        i["L0_ffn_b2"].reshape(NE, NCF, P).transpose(0, 2, 1))
    W["b3_l"] = np.ascontiguousarray(i["L0_ffn_b3"])
    out = {}
    for k, v in W.items():
        v = np.asarray(v)
        out[k] = v if v.dtype == BFNP else v.astype(np.float32)
    return out


def make_in_maps(inputs, n_own=4096, n_ext=8192):
    x = np.asarray(inputs["L0_attn_x"], np.float32)
    W = _layout_weights({k: np.asarray(v, np.float32) for k, v in inputs.items()})
    in_maps = []
    for c in range(8):
        b, hf = c // 2, c % 2
        l0 = hf * n_own
        xr = np.zeros((3 + n_ext, DM), np.float32)
        lo = l0 - (n_ext - n_own) - 3
        src_lo = max(lo, 0)
        src_hi = l0 + n_own
        xr[src_lo - lo:src_hi - lo, :] = x[b, src_lo:src_hi, :]
        m = np.ones((NST, n_ext), np.float32)
        if hf == 0:
            m[:, :n_ext - n_own] = 0.0
        m3 = np.full((P, 3), 1.0 if hf == 1 else 0.0, np.float32)
        im = {"x": xr, "bx_mask": m, "m3": m3}
        im.update(W)
        in_maps.append(im)
    return in_maps


def kernel(**inputs):
    nc = build()
    in_maps = make_in_maps(inputs)
    res = run_bass_kernel_spmd(nc, in_maps, core_ids=list(range(8)))
    out = np.zeros((4, 8192, DM), np.float32)
    for c in range(8):
        b, hf = c // 2, c % 2
        out[b, hf * 4096:(hf + 1) * 4096, :] = np.asarray(res.results[c]["y"])
    return out


# version 05
# speedup vs baseline: 1.3022x; 1.2937x over previous
import numpy as np
import ml_dtypes
from contextlib import ExitStack
import concourse.bass as bass
import concourse.mybir as mybir
from concourse.bass_utils import run_bass_kernel_spmd

F32 = mybir.dt.float32
BF16 = mybir.dt.bfloat16
AF = mybir.ActivationFunctionType
ALU = mybir.AluOpType
AXX = mybir.AxisListType.X
BFNP = ml_dtypes.bfloat16

P = 128
DM = 1024
DI = 2048
NST = 16
DFF = 4096
NE = 8
NCD = DM // P
NCI = DI // P
NCF = DFF // P
TW = 512
EPS = 1e-5
RDM = 1.0 / DM
RDI = 1.0 / DI


class Ctx:
    def __init__(self, nc, es):
        self.nc = nc
        self.es = es
        self.sem = {}
        self.cnt = {}

    def new(self, *names):
        for n in names:
            self.sem[n] = self.es.enter_context(self.nc.semaphore(n))
            self.cnt[n] = 0

    def mark(self, inst, name, k=1):
        inst.then_inc(self.sem[name], k)
        self.cnt[name] += k
        return self.cnt[name]

    def wait(self, eng, name, val=None):
        v = self.cnt[name] if val is None else val
        if v > 0:
            eng.wait_ge(self.sem[name], v)

    def dma_seq(self, q, stream, out, in_):
        self.wait(q, stream)
        self.mark(q.dma_start(out=out, in_=in_), stream, 16)


def build(n_own_t=8, n_ext_t=16, fw=512, dbg=False, upto="full"):
    n_own = n_own_t * TW
    n_ext = n_ext_t * TW
    xrows = 3 + n_ext
    n_pad_t = n_ext_t - n_own_t
    half_t = n_ext_t // 2
    half = half_t * TW
    nwin = max(1, n_own // fw)
    fw = n_own // nwin
    nsub = fw // TW

    nc = bass.Bass("TRN2", target_bir_lowering=False)

    def din(name, shape, dtype=F32):
        return nc.dram_tensor(name, list(shape), dtype, kind="ExternalInput").ap()

    x_in = din("x", [xrows, DM])
    bx_mask = din("bx_mask", [NST, n_ext])
    m3_in = din("m3", [P, 3])
    ident_f = din("ident_f", [P, P])
    ident_b = din("ident_b", [P, P], BF16)
    w_in_l = din("w_in_l", [P, NCF, NCD * P])
    wd_l = din("wd_l", [P, NCI, NCI * P])
    w_out_l = din("w_out_l", [P, NCD * NCI * P])
    bt_l = din("bt_l", [P, NCI * NST])
    ct_l = din("ct_l", [NST, DI])
    a_log_r = din("a_log_r", [1, NST])
    d_vec_l = din("d_vec_l", [P, NCI])
    conv_w_l = din("conv_w_l", [P, NCI * 4])
    conv_diag_l = din("conv_diag_l", [P, NCI * 4 * P])
    m3b_in = din("m3b", [P, 3], BF16)
    conv_b_l = din("conv_b_l", [P, NCI])
    bd_l = din("bd_l", [P, NCI])
    gamma_l = din("gamma_l", [P, NCD])
    beta_l = din("beta_l", [P, NCD])
    wr_l = din("wr_l", [P, NCD * NE])
    br_r = din("br_r", [1, NE])
    ones_r = din("ones_r", [1, P])
    consts_in = din("consts", [P, 8])
    w1_l = din("w1_l", [NE, NCF, P, NCD * P])
    w2_l = din("w2_l", [NE, NCF, P, NCD * P])
    w3_l = din("w3_l", [NE, DFF, DM])
    b1_l = din("b1_l", [NE, P, NCF])
    b2_l = din("b2_l", [NE, P, NCF])
    b3_l = din("b3_l", [NE, DM])

    y_out = nc.dram_tensor("y", [n_own, DM], F32, kind="ExternalOutput").ap()

    def scr(name, shape, dtype=F32):
        kind = "ExternalOutput" if dbg else "Internal"
        return nc.dram_tensor(name, list(shape), dtype, kind=kind).ap()

    xpxn_sp = scr("xpxn_sp", [P, NCD, n_own])
    xsbf_sp = scr("xsbf_sp", [P, NCI, n_ext], BF16)
    sg_sp = scr("sg_sp", [P, NCI, n_own], BF16)
    x1_sp = scr("x1_sp", [P, NCD, n_own])
    xn2_sp = scr("xn2_sp", [P, NCD, n_own], BF16)
    cw_sp = scr("cw_sp", [NE, n_own])
    w1b_sp = nc.dram_tensor("w1b_sp", [NE, NCF, P, NCD * P], BF16).ap()
    w2b_sp = nc.dram_tensor("w2b_sp", [NE, NCF, P, NCD * P], BF16).ap()
    w3b_sp = nc.dram_tensor("w3b_sp", [NE, DFF, DM], BF16).ap()
    if dbg:
        hs_dbg = scr("hs_dbg", [NST, half])
        da_dbg = scr("da_dbg", [NST, half])
        bxd_dbg = scr("bxd_dbg", [NST, half])

    es = ExitStack()
    cx = Ctx(nc, es)
    cx.new("sx0", "sx1", "spl", "swp", "pe", "sc", "dv", "gp",
           "sw0", "sw1", "sld0", "sld1", "sst", "sxp", "ssg", "scv",
           "sw20", "sw21", "sw30", "sw31", "sxw0", "sxw1", "scw0", "scw1",
           "swc", "hw10", "hw11", "hw20", "hw21", "hw30", "hw31")

    def sbuf(stack, name, shape, dtype=F32):
        return stack.enter_context(nc.sbuf_tensor(name, list(shape), dtype))

    idf = sbuf(es, "idf", [P, P])
    idb = sbuf(es, "idb", [P, P], BF16)
    m3_sb = sbuf(es, "m3_sb", [P, 3])
    m3b_sb = sbuf(es, "m3b_sb", [P, 3], BF16)
    dvec_sb = sbuf(es, "dvec_sb", [P, NCI])
    cwv_sb = sbuf(es, "cwv_sb", [P, NCI * 4])
    cb_sb = sbuf(es, "cb_sb", [P, NCI])
    bd_sb = sbuf(es, "bd_sb", [P, NCI])
    gam_sb = sbuf(es, "gam_sb", [P, NCD])
    bet_sb = sbuf(es, "bet_sb", [P, NCD])
    wr_sb = sbuf(es, "wr_sb", [P, NCD * NE])
    br_sb = sbuf(es, "br_sb", [1, NE])
    ones_sb = sbuf(es, "ones_sb", [1, P])
    consts_sb = sbuf(es, "consts_sb", [P, 8])
    for _ci, _cv in enumerate([EPS, RDI, -0.5, -1.0, 1.0, 0.5]):
        nc.const_aps.register(F32, _cv, consts_sb[:, _ci:_ci + 1])
    onesc_sb = sbuf(es, "onesc_sb", [P, 1], BF16)
    arow_sb = sbuf(es, "arow_sb", [1, NST])
    bt_sb = sbuf(es, "bt_sb", [P, NCI * NST], BF16)
    ct_sb = sbuf(es, "ct_sb", [NST, DI], BF16)
    hmid_sb = sbuf(es, "hmid_sb", [NST, 1])
    hsb_sb = sbuf(es, "hsb_sb", [NST, half], BF16)
    delta_sb = sbuf(es, "delta_sb", [1, TW])

    blk = es.enter_context(nc.Block())

    DMA_SEMS = ["spl", "swp", "sw0", "sw1", "sld0", "sld1", "sst", "sxp",
                "ssg", "scv", "sx0", "sx1", "sw20", "sw21", "sw30", "sw31",
                "sxw0", "sxw1", "scw0", "scw1", "swc",
                "hw10", "hw11", "hw20", "hw21", "hw30", "hw31"]

    def join():
        names = ["pe", "sc", "dv", "gp"]
        secs = {"pe": blk.tensor, "sc": blk.scalar, "dv": blk.vector,
                "gp": blk.gpsimd}
        def markself(e, n):
            if cx.cnt[n] > 0:
                e.wait_ge(cx.sem[n], cx.cnt[n])
            cx.mark(e.nop(), n)
        for n in names:
            secs[n](lambda e, n=n: markself(e, n))
        snap = {n: cx.cnt[n] for n in names}

        def waits(e, me):
            for n2, v in snap.items():
                if n2 != me and v > 0:
                    e.wait_ge(cx.sem[n2], v)
        for n in names:
            secs[n](lambda e, n=n: waits(e, n))

        def syncw(s):
            waits(s, "sync")
            for n2 in DMA_SEMS:
                cx.wait(s, n2)
        blk.sync(syncw)

        def gpw(g):
            for n2 in DMA_SEMS:
                cx.wait(g, n2)
        secs["gp"](gpw)

    with es:
        @blk.sync
        def _(s):
            for src, dst in [(ident_f, idf), (m3_in, m3_sb), (d_vec_l, dvec_sb),
                             (conv_w_l, cwv_sb), (conv_b_l, cb_sb), (bd_l, bd_sb),
                             (gamma_l, gam_sb), (beta_l, bet_sb), (wr_l, wr_sb),
                             (br_r, br_sb), (ones_r, ones_sb), (a_log_r, arow_sb),
                             (consts_in, consts_sb)]:
                cx.mark(s.dma_start(out=dst[:], in_=src), "spl", 16)

        @blk.gpsimd
        def _(g):
            for src, dst in [(ident_b, idb), (bt_l, bt_sb), (ct_l, ct_sb),
                             (m3b_in, m3b_sb)]:
                cx.mark(g.dma_start(out=dst[:], in_=src), "swp", 16)
            cx.mark(g.memset(onesc_sb[:], 1.0), "gp")

        @blk.scalar
        def _(s):
            cx.wait(s, "spl")
            cx.mark(s.activation(arow_sb[:], arow_sb[:], AF.Exp), "sc")
            cx.wait(s, "sc")
            cx.mark(s.mul(arow_sb[:], arow_sb[:], -1.0), "sc")
        arow_ready = cx.cnt["sc"]

        s1 = ExitStack()
        xt_sb = [sbuf(s1, f"xt{k}", [P, 4, DM]) for k in range(2)]
        st_sb = sbuf(s1, "st_sb", [P, 4, 16])
        xq_sb = sbuf(s1, "xq_sb", [P, 4, DM])
        xnnb_sb = sbuf(s1, "xnnb_sb", [P, 4, DM], BF16)
        hfm_sb = sbuf(s1, "hfm_sb", [P, NCD, TW], BF16)
        xpfm_sb = sbuf(s1, "xpfm_sb", [P, NCD, TW])
        roll_sb = sbuf(s1, "roll_sb", [P, NCI, TW + 3], BF16)
        cdg_sb = sbuf(s1, "cdg_sb", [P, NCI * 4 * P], BF16)
        xsg_sb = sbuf(s1, "xsg_sb", [P, NCI, TW], BF16)
        sgs_sb = sbuf(s1, "sgs_sb", [P, NCI, TW], BF16)
        w1b_sb = [sbuf(s1, f"w1b{k}", [P, 4, NCD * P], BF16) for k in range(2)]
        ps1 = ExitStack()
        pbA = ps1.enter_context(nc.psum_tensor("pbA", [P, TW], F32))
        pbB = ps1.enter_context(nc.psum_tensor("pbB", [P, TW], F32))
        ptb0 = ps1.enter_context(nc.psum_tensor("ptb0", [P, 2 * TW], BF16))
        ptb1 = ps1.enter_context(nc.psum_tensor("ptb1", [P, 2 * TW], BF16))
        ptf0 = ps1.enter_context(nc.psum_tensor("ptf0", [P, TW], F32))
        ptf1 = ps1.enter_context(nc.psum_tensor("ptf1", [P, TW], F32))

        @blk.gpsimd
        def _(g):
            cx.wait(g, "spl")
            cx.mark(g.memset(roll_sb[:, :, 0:3], 0.0), "gp")
            cx.mark(g.dma_start(out=cdg_sb[:], in_=conv_diag_l), "swp", 16)

        xt_done = [0, 0]
        pe_win_end = 0
        pe_tr_end_prev = 0
        roll_ready = cx.cnt["gp"]
        spill_done = 0
        xp_spill_done = 0
        xpfm_ready = 0

        for pre_i in range(min(2, n_ext_t)):
            @blk.sync
            def _(s):
                for sub in range(4):
                    r0 = 3 + pre_i * TW + sub * P
                    cx.mark(s.dma_start(out=xt_sb[pre_i][:, sub, :],
                                        in_=x_in[r0:r0 + P, :]), f"sx{pre_i}", 16)

        for i in range(n_ext_t):
            own = i >= n_pad_t
            slot = i % 2
            sxn = f"sx{slot}"
            njt = NCF if own else NCI
            nwb = (njt + 3) // 4
            sx_tile = cx.cnt[sxn]

            ntr = 32 + (32 if own else 0)
            pe_after_tr = cx.cnt["pe"] + ntr
            swbase = [cx.cnt["sw0"], cx.cnt["sw1"]]

            @blk.gpsimd
            def _(g):
                for wb in range(nwb):
                    ws = wb % 2
                    if wb >= 2:
                        cx.wait(g, "pe", pe_after_tr + 4 * (wb - 1))
                    elif i > 0:
                        cx.wait(g, "pe", pe_win_end)
                    cx.mark(g.dma_start(out=w1b_sb[ws][:],
                                        in_=w_in_l[:, wb * 4:(wb + 1) * 4, :]),
                            f"sw{ws}", 16)

            sc_ln0 = cx.cnt["sc"]
            dv_ln0 = cx.cnt["dv"]

            @blk.scalar
            def _(s):
                cx.wait(s, sxn, sx_tile)
                cx.wait(s, "pe", pe_tr_end_prev)
                if i == 0:
                    cx.wait(s, "spl")
                for sub in range(4):
                    st = st_sb[:, sub, :]
                    cx.mark(s.activation(xq_sb[:, sub, :], xt_sb[slot][:, sub, :],
                                         AF.Identity, accum_out=st[:, 0:1]), "sc")
                    cx.wait(s, "sc")
                    cx.mark(s.activation(xq_sb[:, sub, :], xt_sb[slot][:, sub, :],
                                         AF.Square, accum_out=st[:, 1:2]), "sc")
                    cx.wait(s, "sc")
                for sub in range(4):
                    st = st_sb[:, sub, :]
                    cx.wait(s, "dv", dv_ln0 + 3 * (sub + 1))
                    cx.mark(s.activation(st[:, 5:6], st[:, 4:5], AF.Ln, bias=EPS,
                                         scale=-1.0), "sc")
                    cx.wait(s, "sc")
                    cx.mark(s.activation(st[:, 6:7], st[:, 5:6], AF.Exp,
                                         scale=-0.5), "sc")
                    cx.wait(s, "sc")
                for sub in range(4):
                    st = st_sb[:, sub, :]
                    cx.wait(s, "dv", dv_ln0 + 12 + 2 * (sub + 1))
                    cx.mark(s.activation(st[:, 9:10], st[:, 8:9], AF.Ln, bias=EPS),
                            "sc")
                    cx.wait(s, "sc")
                    cx.mark(s.activation(st[:, 10:11], st[:, 9:10], AF.Exp,
                                         scale=-0.5), "sc")
                    cx.wait(s, "sc")
            sc_ln_end = cx.cnt["sc"]
            assert sc_ln_end == sc_ln0 + 24

            @blk.vector
            def _(v):
                for sub in range(4):
                    st = st_sb[:, sub, :]
                    cx.wait(v, "sc", sc_ln0 + 2 * sub + 2)
                    cx.wait(v, "dv")
                    cx.mark(v.tensor_scalar(st[:, 2:3], st[:, 0:1], RDM, None,
                                            ALU.mult), "dv")
                    cx.mark(v.tensor_scalar(st[:, 3:4], st[:, 1:2], RDM, None,
                                            ALU.mult), "dv")
                    cx.wait(v, "dv")
                    cx.mark(v.scalar_tensor_tensor(st[:, 4:5], st[:, 2:3],
                                                   st[:, 2:3], st[:, 3:4],
                                                   ALU.mult, ALU.subtract), "dv")
                for sub in range(4):
                    st = st_sb[:, sub, :]
                    cx.wait(v, "sc", sc_ln0 + 8 + 2 * sub + 2)
                    cx.wait(v, "dv")
                    cx.mark(v.tensor_scalar(st[:, 8:9], st[:, 4:5], st[:, 6:7],
                                            -1.0, ALU.mult, ALU.mult), "dv")
                    cx.wait(v, "dv")
                    cx.mark(v.tensor_scalar(st[:, 8:9], st[:, 8:9], st[:, 6:7],
                                            None, ALU.mult), "dv")
                for sub in range(4):
                    st = st_sb[:, sub, :]
                    cx.wait(v, "sc", sc_ln0 + 16 + 2 * sub + 2)
                    cx.wait(v, "dv")
                    cx.mark(v.tensor_scalar(st[:, 11:12], st[:, 6:7], st[:, 10:11],
                                            None, ALU.mult), "dv")
                    cx.mark(v.tensor_scalar(st[:, 7:8], st[:, 2:3], st[:, 6:7],
                                            -1.0, ALU.mult, ALU.mult), "dv")
                    cx.mark(v.tensor_scalar(st[:, 13:14], st[:, 6:7], 1.0, None,
                                            ALU.add), "dv")
                    cx.wait(v, "dv")
                    cx.mark(v.tensor_scalar(st[:, 12:13], st[:, 2:3], st[:, 11:12],
                                            -1.0, ALU.mult, ALU.mult), "dv")
                    cx.wait(v, "dv")
                    cx.mark(v.tensor_scalar(xnnb_sb[:, sub, :],
                                            xt_sb[slot][:, sub, :], st[:, 11:12],
                                            st[:, 12:13], ALU.mult, ALU.add), "dv")
                    if own:
                        cx.wait(v, "dv")
                        cx.mark(v.tensor_scalar(xq_sb[:, sub, :],
                                                xt_sb[slot][:, sub, :],
                                                st[:, 13:14], st[:, 7:8],
                                                ALU.mult, ALU.add), "dv")
                xt_done[slot] = cx.cnt["dv"]

            if i + 2 < n_ext_t:
                @blk.sync
                def _(s):
                    cx.wait(s, "dv", xt_done[slot])
                    for sub in range(4):
                        r0 = 3 + (i + 2) * TW + sub * P
                        cx.mark(s.dma_start(out=xt_sb[slot][:, sub, :],
                                            in_=x_in[r0:r0 + P, :]), sxn, 16)

            dv_tr_base = cx.cnt["dv"]
            pe_tr_base = cx.cnt["pe"]

            @blk.tensor
            def _(t):
                cx.wait(t, "dv", dv_tr_base)
                cx.wait(t, "swp")
                dv0 = cx.cnt["dv"]
                k = 0
                for sub in range(4):
                    for dc in range(NCD):
                        bank = [ptb0, ptb1][k % 2]
                        if k >= 2:
                            cx.wait(t, "dv", dv0 + (k - 1))
                        cx.mark(t.transpose(bank[:, 0:P],
                                            xnnb_sb[:, sub, dc * P:(dc + 1) * P],
                                            idb[:]), "pe")
                        k += 1
                if own:
                    for k2 in range(32):
                        sub, dc = k2 // NCD, k2 % NCD
                        bank = [ptf0, ptf1][k2 % 2]
                        if k2 >= 2:
                            cx.wait(t, "dv", dv0 + 32 + (k2 - 1))
                        cx.mark(t.transpose(bank[:, 0:P],
                                            xq_sb[:, sub, dc * P:(dc + 1) * P],
                                            idf[:]), "pe")
                        k += 1

            @blk.vector
            def _(v):
                if i >= 1:
                    cx.wait(v, "pe", pe_win_end)
                    cx.wait(v, "sxp", xp_spill_done)
                k = 0
                for sub in range(4):
                    for dc in range(NCD):
                        bank = [ptb0, ptb1][k % 2]
                        cx.wait(v, "pe", pe_tr_base + k + 1)
                        cx.mark(v.tensor_scalar(hfm_sb[:, dc, sub * P:(sub + 1) * P],
                                                bank[:, 0:P], gam_sb[:, dc:dc + 1],
                                                bet_sb[:, dc:dc + 1],
                                                ALU.mult, ALU.add), "dv")
                        k += 1
                if own:
                    for k2 in range(32):
                        sub, dc = k2 // NCD, k2 % NCD
                        bank = [ptf0, ptf1][k2 % 2]
                        cx.wait(v, "pe", pe_tr_base + 32 + k2 + 1)
                        cx.mark(v.tensor_copy(xpfm_sb[:, dc, sub * P:(sub + 1) * P],
                                              bank[:, 0:P]), "dv")
            assert cx.cnt["pe"] == pe_tr_base + ntr == pe_after_tr
            pe_tr_end_prev = cx.cnt["pe"]
            hfm_ready = cx.cnt["dv"]
            xpfm_ready = cx.cnt["dv"]

            if own:
                j = i - n_pad_t

                @blk.sync
                def _(s):
                    cx.wait(s, "dv", xpfm_ready)
                    cx.dma_seq(s, "sxp", xpxn_sp[:, :, j * TW:(j + 1) * TW],
                               xpfm_sb[:])
                xp_spill_done = cx.cnt["sxp"]

            pe_mm_base = cx.cnt["pe"]
            sc_ev_base = cx.cnt["sc"]

            @blk.tensor
            def _(t):
                cx.wait(t, "dv", hfm_ready)
                cx.wait(t, "gp", roll_ready)
                for jt in range(njt):
                    wb, ws = jt // 4, (jt // 4) % 2
                    if jt % 4 == 0:
                        cx.wait(t, f"sw{ws}", swbase[ws] + 16 * (wb // 2 + 1))
                    bank = [pbA, pbB][jt % 2]
                    if jt >= 2:
                        cx.wait(t, "sc", sc_ev_base + (jt - 1))
                    for dc in range(NCD):
                        ins = t.matmul(bank[:],
                                       w1b_sb[ws][:, jt % 4, dc * P:(dc + 1) * P],
                                       hfm_sb[:, dc, :],
                                       start=(dc == 0), stop=(dc == NCD - 1))
                    cx.mark(ins, "pe")
                for c in range(NCI):
                    bank = [pbA, pbB][(njt + c) % 2]
                    cx.wait(t, "sc", sc_ev_base + (njt + c - 1))
                    for kk in range(4):
                        ins = t.matmul(bank[:],
                                       cdg_sb[:, (c * 4 + kk) * P:(c * 4 + kk + 1) * P],
                                       roll_sb[:, c, kk:kk + TW],
                                       start=(kk == 0), stop=(kk == 3))
                    cx.mark(ins, "pe")
            pe_win_end = pe_mm_base + njt
            pe_conv_end = cx.cnt["pe"]
            assert pe_conv_end == pe_mm_base + njt + NCI

            @blk.scalar
            def _(s):
                cx.wait(s, "gp", roll_ready)
                cx.wait(s, "scv", spill_done)
                for jt in range(njt):
                    bank = [pbA, pbB][jt % 2]
                    cx.wait(s, "pe", pe_mm_base + jt + 1)
                    if jt < NCI:
                        ins = s.copy(roll_sb[:, jt, 3:3 + TW], bank[:])
                    else:
                        ins = s.activation(sgs_sb[:, jt - NCI, :], bank[:], AF.Silu)
                    cx.mark(ins, "sc")
                for c in range(NCI):
                    bank = [pbA, pbB][(njt + c) % 2]
                    cx.wait(s, "pe", pe_mm_base + njt + c + 1)
                    cx.mark(s.activation(xsg_sb[:, c, :], bank[:], AF.Silu,
                                         bias=cb_sb[:, c:c + 1]), "sc")
            assert cx.cnt["sc"] == sc_ev_base + njt + NCI

            @blk.sync
            def _(s):
                cx.wait(s, "sc")
                cx.dma_seq(s, "scv",
                           xsbf_sp[:, :, i * TW:(i + 1) * TW], xsg_sb[:])
                if own:
                    j = i - n_pad_t
                    cx.dma_seq(s, "scv",
                               sg_sp[:, :, j * TW:(j + 1) * TW], sgs_sb[:])
            spill_done = cx.cnt["scv"]

            @blk.gpsimd
            def _(g):
                cx.wait(g, "pe", pe_conv_end)
                cx.wait(g, "sc")
                if i == n_pad_t - 1:
                    for c in range(NCI):
                        cx.mark(g.tensor_tensor(roll_sb[:, c, 0:3],
                                                roll_sb[:, c, TW:TW + 3],
                                                m3b_sb[:], ALU.mult), "gp")
                elif i < n_ext_t - 1:
                    for c in range(NCI):
                        cx.mark(g.tensor_copy(roll_sb[:, c, 0:3],
                                              roll_sb[:, c, TW:TW + 3]), "gp")
            roll_ready = cx.cnt["gp"]

        ps1.close()
        s1.close()
        join()

        s2 = ExitStack()
        wd_sb = sbuf(s2, "wd_sb", [P, NCI * NCI * P], BF16)
        xsb_sb = [sbuf(s2, f"xsb{k}", [P, NCI, TW], BF16) for k in range(2)]
        bxm_sb = [sbuf(s2, f"bxm{k}", [NST, TW]) for k in range(2)]
        spa_sb = sbuf(s2, "spa_sb", [P, TW], BF16)
        spb_sb = sbuf(s2, "spb_sb", [P, TW], BF16)
        spf_sb = [sbuf(s2, f"spf{k}", [P, TW], BF16) for k in range(2)]
        bxp_sb = sbuf(s2, "bxp_sb", [NST, TW])
        da_sb = sbuf(s2, "da_sb", [NST, half])
        bx_sb = sbuf(s2, "bx_sb", [NST, half])
        hs_sb = sbuf(s2, "hs_sb", [NST, half])
        ps2 = ExitStack()
        pb2 = [ps2.enter_context(nc.psum_tensor(f"p2{k}", [P, TW], F32))
               for k in range(5)]
        pbA2, pbB2, pbD, pbO, pbO2 = pb2

        @blk.gpsimd
        def _(g):
            cx.mark(g.dma_start(out=wd_sb[:],
                                in_=wd_l.rearrange("p a b -> p (a b)")), "swp", 16)
        wd_ready = cx.cnt["swp"]

        xsb_done = [0, 0]
        scan_a_done = 0
        sc_delta_prev = 0
        dv_bx_prev = 0

        for i in range(n_ext_t):
            slot = i % 2
            sldn = f"sld{slot}"
            hpass = i // half_t
            hoff = (i % half_t) * TW

            ee_lo = i * NE // n_ext_t
            ee_hi = (i + 1) * NE // n_ext_t
            if ee_hi > ee_lo:
                @blk.gpsimd
                def _(g):
                    for ee in range(ee_lo, ee_hi):
                        cx.dma_seq(g, "swc", w1b_sp[ee], w1_l[ee])
                        cx.dma_seq(g, "swc", w2b_sp[ee], w2_l[ee])
                        cx.dma_seq(g, "swc", w3b_sp[ee], w3_l[ee])

            @blk.sync
            def _(s):
                if i >= 2:
                    cx.wait(s, "pe", xsb_done[slot])
                cx.mark(s.dma_start(out=xsb_sb[slot][:],
                                    in_=xsbf_sp[:, :, i * TW:(i + 1) * TW]), sldn, 16)
                cx.mark(s.dma_start(out=bxm_sb[slot][:],
                                    in_=bx_mask[:, i * TW:(i + 1) * TW]), sldn, 16)
            sld_now = cx.cnt[sldn]

            pe_wd_base = cx.cnt["pe"]
            sc_sp_base = cx.cnt["sc"]
            dv_sp_base = cx.cnt["dv"]

            pe_of_wd = []
            pe_of_ones = []

            @blk.tensor
            def _(t):
                cx.wait(t, sldn, sld_now)
                cx.wait(t, "swp", wd_ready)
                if hpass == 1 and i % half_t == 0:
                    cx.wait(t, "dv", scan_a_done)

                def ones_mm(jo):
                    cx.wait(t, "dv", dv_sp_base + jo + 1)
                    if jo == 0:
                        cx.wait(t, "sc", sc_delta_prev)
                    pe_of_ones.append(
                        cx.mark(t.matmul(pbD[0:1, :], onesc_sb[:], spf_sb[jo % 2][:],
                                         start=(jo == 0), stop=(jo == NCI - 1)),
                                "pe"))

                for jt in range(NCI):
                    bank = [pbA2, pbB2][jt % 2]
                    if jt >= 2:
                        cx.wait(t, "sc", sc_sp_base + 4 * (jt - 1))
                    for kc in range(NCI):
                        ins = t.matmul(bank[:],
                                       wd_sb[:, (jt * NCI + kc) * P:(jt * NCI + kc + 1) * P],
                                       xsb_sb[slot][:, kc, :],
                                       start=(kc == 0), stop=(kc == NCI - 1))
                    pe_of_wd.append(cx.mark(ins, "pe"))
                    if jt >= 2:
                        ones_mm(jt - 2)
                ones_mm(NCI - 2)
                ones_mm(NCI - 1)
            pe_ones = cx.cnt["pe"]

            @blk.scalar
            def _(s):
                for jt in range(NCI):
                    bank = [pbA2, pbB2][jt % 2]
                    cx.wait(s, "pe", pe_of_wd[jt])
                    if jt >= 1:
                        cx.wait(s, "dv", dv_sp_base + jt)
                    cx.mark(s.activation(spa_sb[:], bank[:], AF.Abs,
                                         bias=bd_sb[:, jt:jt + 1]), "sc")
                    cx.wait(s, "sc")
                    cx.mark(s.activation(spb_sb[:], spa_sb[:], AF.Exp, scale=-1.0),
                            "sc")
                    cx.wait(s, "sc")
                    cx.mark(s.activation(spa_sb[:], spb_sb[:], AF.Ln, bias=1.0), "sc")
                    cx.wait(s, "sc")
                    cx.mark(s.activation(spb_sb[:], bank[:], AF.Relu,
                                         bias=bd_sb[:, jt:jt + 1]), "sc")

            @blk.vector
            def _(v):
                for jt in range(NCI):
                    cx.wait(v, "sc", sc_sp_base + 4 * (jt + 1))
                    if jt >= 2:
                        cx.wait(v, "pe", pe_of_ones[jt - 2])
                    cx.mark(v.tensor_add(spf_sb[jt % 2][:], spa_sb[:], spb_sb[:]),
                            "dv")
            assert cx.cnt["dv"] == dv_sp_base + NCI


            @blk.scalar
            def _(s):
                cx.wait(s, "pe", pe_ones)
                cx.mark(s.activation(delta_sb[:], pbD[0:1, :], AF.Copy, scale=RDI),
                        "sc")
            sc_delta = cx.cnt["sc"]
            sc_delta_prev = sc_delta

            @blk.tensor
            def _(t):
                cx.wait(t, "sc", max(sc_delta, arow_ready))
                cx.wait(t, "dv", dv_bx_prev)
                cx.mark(t.matmul(pbO[0:16, :], arow_sb[:], delta_sb[:],
                                 start=True, stop=True), "pe")
            pe_da = cx.cnt["pe"]

            @blk.scalar
            def _(s):
                cx.wait(s, "pe", pe_da)
                cx.mark(s.activation(da_sb[:, hoff:hoff + TW], pbO[0:16, :],
                                     AF.Exp), "sc")
            sc_da = cx.cnt["sc"]

            @blk.tensor
            def _(t):
                cx.wait(t, "sc", sc_da)
                t.matmul(pbO[0:16, :], ones_sb[0:1, 0:NST], delta_sb[:],
                         start=True, stop=True)
                for kc in range(NCI):
                    ins = t.matmul(pbO2[0:16, :], bt_sb[:, kc * NST:(kc + 1) * NST],
                                   xsb_sb[slot][:, kc, :],
                                   start=(kc == 0), stop=(kc == NCI - 1))
                cx.mark(ins, "pe")
            pe_bx = cx.cnt["pe"]

            @blk.vector
            def _(v):
                cx.wait(v, "pe", pe_bx)
                cx.wait(v, sldn, sld_now)
                cx.mark(v.tensor_copy(bxp_sb[:], pbO2[0:16, :]), "dv")
                cx.wait(v, "dv")
                cx.mark(v.tensor_mul(bxp_sb[:], bxp_sb[:], pbO[0:16, :]), "dv")
                cx.wait(v, "dv")
                cx.mark(v.tensor_mul(bx_sb[:, hoff:hoff + TW], bxp_sb[:],
                                     bxm_sb[slot][:]), "dv")
            dv_bx_prev = cx.cnt["dv"]
            xsb_done[slot] = pe_bx

            if i == half_t - 1:
                @blk.vector
                def _(v):
                    cx.wait(v, "dv")
                    cx.mark(v.tensor_tensor_scan(hs_sb[:], da_sb[:], bx_sb[:], 0.0,
                                                 ALU.mult, ALU.add), "dv")
                    cx.wait(v, "dv")
                    cx.mark(v.tensor_copy(hmid_sb[:], hs_sb[:, half - 1:half]), "dv")
                scan_a_done = cx.cnt["dv"]
            elif i == n_ext_t - 1:
                @blk.vector
                def _(v):
                    cx.wait(v, "dv")
                    cx.mark(v.tensor_tensor_scan(hs_sb[:], da_sb[:], bx_sb[:],
                                                 hmid_sb[:], ALU.mult, ALU.add), "dv")
                    cx.wait(v, "dv")
                    cx.mark(v.tensor_copy(hsb_sb[:], hs_sb[:]), "dv")

        if dbg:
            @blk.sync
            def _(s):
                cx.wait(s, "dv")
                cx.dma_seq(s, "sst", hs_dbg, hs_sb[:])
                cx.dma_seq(s, "sst", da_dbg, da_sb[:])
                cx.dma_seq(s, "sst", bxd_dbg, bx_sb[:])

        if upto == "A":
            @blk.sync
            def _(s):
                cx.wait(s, "dv")
                cx.dma_seq(s, "sst", y_out[0:NST, 0:min(half, DM)],
                           hs_sb[:, 0:min(half, DM)])
                for nm in ["sst", "sxp", "ssg", "scv"]:
                    cx.wait(s, nm)
            ps2.close()
            s2.close()
            return nc

        ps2.close()
        s2.close()
        join()

        s3 = ExitStack()
        wo_sb = sbuf(s3, "wo_sb", [P, NCD * NCI * P], BF16)
        xsr_sb = [sbuf(s3, f"xsr{k}", [P, 4, TW], BF16) for k in range(2)]
        sgr_sb = [sbuf(s3, f"sgr{k}", [P, 4, TW], BF16) for k in range(2)]
        xpr_sb = [sbuf(s3, f"xpr{k}", [P, NCD, TW]) for k in range(2)]
        ygf_sb = sbuf(s3, "ygf_sb", [P, TW])
        ygb_sb = sbuf(s3, "ygb_sb", [P, NCI, TW], BF16)
        x1t_sb = sbuf(s3, "x1t_sb", [P, NCD, TW])
        x1b_sb = sbuf(s3, "x1b_sb", [P, NCD, TW], BF16)
        x1q_sb = sbuf(s3, "x1q_sb", [P, NCD, TW], BF16)
        xn2f_sb = sbuf(s3, "xn2f_sb", [P, NCD, TW])
        xn2b_sb = sbuf(s3, "xn2b_sb", [P, NCD, TW], BF16)
        row_sb = sbuf(s3, "row_sb", [1, 8 * TW])
        mcol_sb = sbuf(s3, "mcol_sb", [P, 8])
        eq1_sb = sbuf(s3, "eq1_sb", [P, NE])
        msk_sb = sbuf(s3, "msk_sb", [P, NE])
        cwt_sb = sbuf(s3, "cwt_sb", [P, NE])
        cwst_sb = sbuf(s3, "cwst_sb", [NE, P])
        ps3 = ExitStack()
        pcA = ps3.enter_context(nc.psum_tensor("pcA", [P, TW], F32))
        pcB = ps3.enter_context(nc.psum_tensor("pcB", [P, TW], F32))
        pcD1 = ps3.enter_context(nc.psum_tensor("pcD1", [P, TW], F32))
        pcD2 = ps3.enter_context(nc.psum_tensor("pcD2", [P, TW], F32))
        pcO = ps3.enter_context(nc.psum_tensor("pcO", [P, TW], F32))
        pcO2 = ps3.enter_context(nc.psum_tensor("pcO2", [P, TW], F32))
        pcR = ps3.enter_context(nc.psum_tensor("pcR", [P, TW], F32))
        pcT = ps3.enter_context(nc.psum_tensor("pcT", [P, TW], F32))

        @blk.gpsimd
        def _(g):
            cx.mark(g.dma_start(out=wo_sb[:], in_=w_out_l), "swp", 16)
        wo_ready = cx.cnt["swp"]

        ldslot_done = [0, 0]
        dv_rep_prev = 0
        dv_x1_prev = 0
        pe_cast_prev = 0
        dv_router_prev = 0
        dv_ygb_prev = 0

        r_mu = row_sb[0:1, 0 * TW:1 * TW]
        r_m2 = row_sb[0:1, 1 * TW:2 * TW]
        r_nv = row_sb[0:1, 2 * TW:3 * TW]
        r_rs = row_sb[0:1, 3 * TW:4 * TW]
        r_b = row_sb[0:1, 4 * TW:5 * TW]

        for j in range(n_own_t):
            slot = j % 2
            sxwn = f"sxw{slot}"

            @blk.sync
            def _(s):
                if j >= 2:
                    cx.wait(s, "dv", ldslot_done[slot])
                cx.wait(s, sxwn)
                cx.mark(s.dma_start(out=xpr_sb[slot][:],
                                    in_=xpxn_sp[:, :, j * TW:(j + 1) * TW]),
                        sxwn, 16)
            ld_xp = cx.cnt[sxwn]

            pe_ys = []
            dv_ysbase = cx.cnt["dv"]
            for grp in range(4):
                gslot = grp % 2

                @blk.sync
                def _(s):
                    if j >= 2 or grp >= 2:
                        cx.wait(s, "dv")
                    cx.wait(s, sxwn)
                    cx.mark(s.dma_start(
                        out=xsr_sb[gslot][:],
                        in_=xsbf_sp[:, grp * 4:(grp + 1) * 4,
                                    (n_pad_t + j) * TW:(n_pad_t + j + 1) * TW]),
                        sxwn, 16)
                    cx.mark(s.dma_start(
                        out=sgr_sb[gslot][:],
                        in_=sg_sp[:, grp * 4:(grp + 1) * 4, j * TW:(j + 1) * TW]),
                        sxwn, 16)
                ld_now = cx.cnt[sxwn]

                pe_g = []

                @blk.tensor
                def _(t):
                    cx.wait(t, "swp", wo_ready)
                    if j == 0 and grp == 0:
                        cx.wait(t, "dv")
                        cx.wait(t, "spl")
                    for q in range(4):
                        c = grp * 4 + q
                        bank = [pcA, pcB][c % 2]
                        nprev = len(pe_ys) + q
                        if nprev >= 2:
                            cx.wait(t, "dv", dv_ysbase + 2 * (nprev - 1))
                        elif j > 0 and nprev == 0:
                            cx.wait(t, "dv", dv_ygb_prev)
                        pe_g.append(cx.mark(
                            t.matmul(bank[:], ct_sb[:, c * P:(c + 1) * P],
                                     hsb_sb[:, j * TW:(j + 1) * TW],
                                     start=True, stop=True), "pe"))

                @blk.vector
                def _(v):
                    cx.wait(v, sxwn, ld_now)
                    for q in range(4):
                        c = grp * 4 + q
                        bank = [pcA, pcB][c % 2]
                        cx.wait(v, "pe", pe_g[q])
                        cx.wait(v, "dv")
                        cx.mark(v.scalar_tensor_tensor(
                            ygf_sb[:], xsr_sb[gslot][:, q, :], dvec_sb[:, c:c + 1],
                            bank[:], ALU.mult, ALU.add), "dv")
                        cx.wait(v, "dv")
                        cx.mark(v.tensor_mul(ygb_sb[:, c, :], ygf_sb[:],
                                             sgr_sb[gslot][:, q, :]), "dv")
                pe_ys += pe_g
            dv_yg = cx.cnt["dv"]
            ldslot_done[slot] = dv_yg

            pe_wo = []
            dv_wobase = cx.cnt["dv"]

            @blk.tensor
            def _(t):
                cx.wait(t, "dv", dv_yg)
                for dm in range(NCD):
                    bank = [pcA, pcB][dm % 2]
                    if dm >= 2:
                        cx.wait(t, "dv", dv_wobase + dm - 1)
                    for kc in range(NCI):
                        ins = t.matmul(bank[:],
                                       wo_sb[:, (dm * NCI + kc) * P:(dm * NCI + kc + 1) * P],
                                       ygb_sb[:, kc, :],
                                       start=(kc == 0), stop=(kc == NCI - 1))
                    pe_wo.append(cx.mark(ins, "pe"))
            dv_ygb_prev = dv_yg

            @blk.vector
            def _(v):
                cx.wait(v, sxwn, ld_xp)
                cx.wait(v, "sc", dv_x1_prev)
                for dm in range(NCD):
                    bank = [pcA, pcB][dm % 2]
                    cx.wait(v, "pe", pe_wo[dm])
                    cx.mark(v.tensor_add(x1t_sb[:, dm, :], bank[:],
                                         xpr_sb[slot][:, dm, :]), "dv")
            dv_x1 = cx.cnt["dv"]

            @blk.scalar
            def _(s):
                cx.wait(s, "dv", dv_x1)
                cx.wait(s, "pe", pe_cast_prev)
                for dm in range(NCD):
                    cx.mark(s.copy(x1b_sb[:, dm, :], x1t_sb[:, dm, :]), "sc")
                    cx.wait(s, "sc")
                    cx.mark(s.activation(x1q_sb[:, dm, :], x1t_sb[:, dm, :],
                                         AF.Square), "sc")
            sc_x1cast = cx.cnt["sc"]
            dv_x1_prev = sc_x1cast

            @blk.sync
            def _(s):
                cx.wait(s, "dv", dv_x1)
                cx.dma_seq(s, "sxp", x1_sp[:, :, j * TW:(j + 1) * TW], x1t_sb[:])

            @blk.tensor
            def _(t):
                cx.wait(t, "sc", sc_x1cast)
                cx.wait(t, "dv", dv_rep_prev)
                for dm in range(NCD):
                    ins = t.matmul(pcD1[0:1, :], onesc_sb[:], x1b_sb[:, dm, :],
                                   start=(dm == 0), stop=(dm == NCD - 1))
                cx.mark(ins, "pe")
                for dm in range(NCD):
                    ins = t.matmul(pcD2[0:1, :], onesc_sb[:], x1q_sb[:, dm, :],
                                   start=(dm == 0), stop=(dm == NCD - 1))
                cx.mark(ins, "pe")
            pe_stats = cx.cnt["pe"]
            pe_cast_prev = pe_stats

            @blk.vector
            def _(v):
                cx.wait(v, "pe", pe_stats)
                cx.mark(v.tensor_scalar(r_mu, pcD1[0:1, :], RDM, None, ALU.mult), "dv")
                cx.mark(v.tensor_scalar(r_m2, pcD2[0:1, :], RDM, None, ALU.mult), "dv")
                cx.wait(v, "dv")
                cx.mark(v.tensor_mul(r_nv, r_mu, r_mu), "dv")
                cx.wait(v, "dv")
                cx.mark(v.tensor_tensor(r_nv, r_nv, r_m2, ALU.subtract), "dv")

            @blk.scalar
            def _(s):
                cx.wait(s, "dv")
                cx.mark(s.activation(r_rs, r_nv, AF.Ln, bias=EPS, scale=-1.0), "sc")
                cx.wait(s, "sc")
                cx.mark(s.activation(r_rs, r_rs, AF.Exp, scale=-0.5), "sc")

            @blk.vector
            def _(v):
                cx.wait(v, "sc")
                cx.mark(v.tensor_mul(r_b, r_mu, r_rs), "dv")
                cx.wait(v, "dv")
                cx.mark(v.tensor_scalar(r_b, r_b, -1.0, None, ALU.mult), "dv")
            dv_rows = cx.cnt["dv"]

            @blk.tensor
            def _(t):
                cx.wait(t, "dv", dv_rows)
                cx.mark(t.matmul(pcO[:], ones_sb[:], r_rs, start=True, stop=True),
                        "pe")
                cx.mark(t.matmul(pcO2[:], ones_sb[:], r_b, start=True, stop=True),
                        "pe")
            pe_rep = cx.cnt["pe"]

            @blk.vector
            def _(v):
                cx.wait(v, "pe", pe_rep)
                cx.wait(v, "ssg")
                for dm in range(NCD):
                    cx.mark(v.tensor_mul(xn2f_sb[:, dm, :], x1t_sb[:, dm, :],
                                         pcO[:]), "dv")
                    cx.wait(v, "dv")
                    cx.mark(v.tensor_add(xn2f_sb[:, dm, :], xn2f_sb[:, dm, :],
                                         pcO2[:]), "dv")
                    cx.wait(v, "dv")
                    cx.mark(v.tensor_copy(xn2b_sb[:, dm, :], xn2f_sb[:, dm, :]),
                            "dv")
            dv_rep_prev = cx.cnt["dv"]

            @blk.sync
            def _(s):
                cx.wait(s, "dv", dv_rep_prev)
                cx.dma_seq(s, "ssg", xn2_sp[:, :, j * TW:(j + 1) * TW], xn2b_sb[:])

            for sub in range(4):
                @blk.tensor
                def _(t):
                    cx.wait(t, "dv", dv_rep_prev)
                    cx.wait(t, "dv", dv_router_prev)
                    for dc in range(NCD):
                        t.matmul(pcR[:, 0:NE],
                                 xn2f_sb[:, dc, sub * P:(sub + 1) * P],
                                 wr_sb[:, dc * NE:(dc + 1) * NE],
                                 start=(dc == 0), stop=False)
                    cx.mark(t.matmul(pcR[:, 0:NE], ones_sb[0:1, 0:P],
                                     br_sb[:], start=False, stop=True), "pe")
                pe_rt = cx.cnt["pe"]

                @blk.vector
                def _(v):
                    cx.wait(v, "pe", pe_rt)
                    cx.mark(v.reduce_max(mcol_sb[:, 0:1], pcR[:, 0:NE], axis=AXX),
                            "dv")
                    cx.wait(v, "dv")
                    cx.mark(v.tensor_scalar(eq1_sb[:], pcR[:, 0:NE],
                                            mcol_sb[:, 0:1], None, ALU.is_equal),
                            "dv")
                    cx.wait(v, "dv")
                    cx.mark(v.scalar_tensor_tensor(msk_sb[:], eq1_sb[:], -1e30,
                                                   pcR[:, 0:NE], ALU.mult, ALU.add),
                            "dv")
                    cx.wait(v, "dv")
                    cx.mark(v.reduce_max(mcol_sb[:, 1:2], msk_sb[:], axis=AXX), "dv")
                    cx.wait(v, "dv")
                    cx.mark(v.tensor_scalar(msk_sb[:], msk_sb[:], mcol_sb[:, 1:2],
                                            None, ALU.is_equal), "dv")
                    cx.wait(v, "dv")
                    cx.mark(v.tensor_sub(mcol_sb[:, 2:3], mcol_sb[:, 1:2],
                                         mcol_sb[:, 0:1]), "dv")

                @blk.scalar
                def _(s):
                    cx.wait(s, "dv")
                    cx.mark(s.activation(mcol_sb[:, 3:4], mcol_sb[:, 2:3], AF.Exp),
                            "sc")

                @blk.vector
                def _(v):
                    cx.wait(v, "sc")
                    cx.mark(v.tensor_scalar(mcol_sb[:, 4:5], mcol_sb[:, 3:4], 1.0,
                                            None, ALU.add), "dv")
                    cx.wait(v, "dv")
                    cx.mark(v.reciprocal(mcol_sb[:, 5:6], mcol_sb[:, 4:5]), "dv")
                    cx.wait(v, "dv")
                    cx.mark(v.tensor_scalar(mcol_sb[:, 6:7], mcol_sb[:, 5:6], -1.0,
                                            1.0, ALU.mult, ALU.add), "dv")
                    cx.wait(v, "dv")
                    cx.mark(v.tensor_scalar(cwt_sb[:], eq1_sb[:], mcol_sb[:, 5:6],
                                            None, ALU.mult), "dv")
                    cx.wait(v, "dv")
                    cx.mark(v.scalar_tensor_tensor(cwt_sb[:], msk_sb[:],
                                                   mcol_sb[:, 6:7], cwt_sb[:],
                                                   ALU.mult, ALU.add), "dv")
                dv_cwt = cx.cnt["dv"]

                @blk.tensor
                def _(t):
                    cx.wait(t, "dv", dv_cwt)
                    cx.mark(t.transpose(pcT[0:NE, 0:P], cwt_sb[:], idf[:]), "pe")
                pe_cwt = cx.cnt["pe"]

                @blk.vector
                def _(v):
                    cx.wait(v, "pe", pe_cwt)
                    cx.wait(v, "scw0")
                    cx.mark(v.tensor_copy(cwst_sb[:], pcT[0:NE, 0:P]), "dv")
                dv_router_prev = cx.cnt["dv"]

                @blk.sync
                def _(s):
                    cx.wait(s, "dv", dv_router_prev)
                    cx.dma_seq(s, "scw0",
                               cw_sp[:, j * TW + sub * P:j * TW + (sub + 1) * P],
                               cwst_sb[:])

        ps3.close()
        s3.close()
        join()

        if upto == "C":
            @blk.sync
            def _(s):
                for nm in DMA_SEMS:
                    cx.wait(s, nm)
                cx.dma_seq(s, "sst", y_out[0:P, 0:DM],
                           sbuf(es, "dummy_sb", [P, DM]))
            return nc

        s4 = ExitStack()
        g_sb = sbuf(s4, "g_sb", [P, NCF, fw], BF16)
        xn2w_sb = [sbuf(s4, f"xn2w{k}", [P, NCD, fw], BF16) for k in range(2)]
        yacc_sb = sbuf(s4, "yacc_sb", [P, NCD, fw])
        cwrep_sb = sbuf(s4, "cwrep_sb", [P, fw])
        cwrow_sb = [sbuf(s4, f"cwrow{k}", [1, fw]) for k in range(2)]
        cwall_sb = sbuf(s4, "cwall_sb", [NE, fw])
        b3a_sb = sbuf(s4, "b3a_sb", [NE, DM])
        w1s_sb = [sbuf(s4, f"w1s{k}", [P, 4, NCD * P], BF16) for k in range(2)]
        w2s_sb = [sbuf(s4, f"w2s{k}", [P, 4, NCD * P], BF16) for k in range(2)]
        w3s_sb = [sbuf(s4, f"w3s{k}", [P, 4, NCD * P], BF16) for k in range(2)]
        b1e_sb = [sbuf(s4, f"b1e{k}", [P, NCF]) for k in range(2)]
        b2e_sb = [sbuf(s4, f"b2e{k}", [P, NCF]) for k in range(2)]
        s1t_sb = sbuf(s4, "s1t_sb", [P, 2, TW])
        ygf_c_sb = sbuf(s4, "ygf_c_sb", [P, 2, TW])
        oadd_sb = sbuf(s4, "oadd_sb", [P, NCD, P])
        ostg_sb = [sbuf(s4, f"ostg{k}", [P, DM]) for k in range(2)]
        x1g_sb = [sbuf(s4, f"x1g{k}", [P, NCD, P]) for k in range(2)]
        ps4 = ExitStack()
        pY = [ps4.enter_context(nc.psum_tensor(f"pY{k}", [P, TW], F32))
              for k in range(8)]

        @blk.sync
        def _(s):
            cx.dma_seq(s, "scw1", b3a_sb[:], b3_l)

        dv_gall_prev = 0
        dv_yacc_final = 0
        xn2w_done = [0, 0]
        pe_g_prev_end = 0
        pe_y_prev_end = 0
        dv_ostg = [0, 0]
        pe_tr_prev = 0
        NSILU = nsub * NCF
        NGOP = 2 * nsub * NCF

        for w in range(nwin):
            wslot = w % 2
            sxwn = f"sxw{wslot}"

            @blk.sync
            def _(s):
                if w >= 2:
                    cx.wait(s, "pe", xn2w_done[wslot])
                cx.wait(s, sxwn)
                cx.mark(s.dma_start(out=xn2w_sb[wslot][:],
                                    in_=xn2_sp[:, :, w * fw:(w + 1) * fw]),
                        sxwn, 16)
            ld_xn2w = cx.cnt[sxwn]

            for e in range(NE):
                eslot = e % 2

                @blk.sync
                def _(s):
                    cx.wait(s, "dv", dv_gall_prev)
                    cx.wait(s, "scw0")
                    cx.mark(s.dma_start(out=cwrow_sb[eslot][:],
                                        in_=cw_sp[e:e + 1, w * fw:(w + 1) * fw]),
                            "scw0", 16)
                    cx.mark(s.dma_start(out=b1e_sb[eslot][:], in_=b1_l[e]),
                            "scw0", 16)
                    cx.mark(s.dma_start(out=b2e_sb[eslot][:], in_=b2_l[e]),
                            "scw0", 16)
                ld_cw = cx.cnt["scw0"]

                @blk.tensor
                def _(t):
                    cx.wait(t, "scw0", ld_cw)
                    cx.wait(t, "dv", dv_gall_prev)
                    for ts in range(nsub):
                        cx.mark(t.matmul(pY[4 + (ts % 2)][:], ones_sb[:],
                                         cwrow_sb[eslot][0:1, ts * TW:(ts + 1) * TW],
                                         start=True, stop=True), "pe")
                pe_cwrep = cx.cnt["pe"]

                @blk.vector
                def _(v):
                    for ts in range(nsub):
                        cx.wait(v, "pe", pe_cwrep - nsub + ts + 1)
                        cx.mark(v.tensor_copy(cwrep_sb[:, ts * TW:(ts + 1) * TW],
                                              pY[4 + (ts % 2)][:]), "dv")
                dv_cwrep = cx.cnt["dv"]

                pe0 = cx.cnt["pe"]
                sc0 = cx.cnt["sc"]
                dv0 = cx.cnt["dv"]
                swb = {k: cx.cnt[k] for k in ("hw10", "hw11", "hw20", "hw21")}

                @blk.sync
                def _(s):
                    cx.wait(s, "swc", 48 * (e + 1))
                    for wb in range(NCF // 4):
                        ws = wb % 2
                        if wb >= 2:
                            cx.wait(s, "pe", pe0 + 2 * nsub * (4 * (wb - 1)))
                        elif pe_g_prev_end > 0:
                            cx.wait(s, "pe", pe_g_prev_end)
                        cx.mark(s.dma_start(
                            out=w1s_sb[ws][:],
                            in_=w1b_sp[e, wb * 4:(wb + 1) * 4]
                            .rearrange("f p d -> p f d")), "hw1%d" % ws, 16)
                        cx.mark(s.dma_start(
                            out=w2s_sb[ws][:],
                            in_=w2b_sp[e, wb * 4:(wb + 1) * 4]
                            .rearrange("f p d -> p f d")), "hw2%d" % ws, 16)

                @blk.tensor
                def _(t):
                    cx.wait(t, sxwn, ld_xn2w)
                    cx.wait(t, "dv", dv_cwrep)
                    for fblk in range(NCF):
                        wb = fblk // 4
                        ws = wb % 2
                        fi = fblk % 4
                        bsel = fblk % 2
                        if fi == 0:
                            cx.wait(t, "hw1%d" % ws,
                                    swb["hw1%d" % ws] + 16 * (wb // 2 + 1))
                            cx.wait(t, "hw2%d" % ws,
                                    swb["hw2%d" % ws] + 16 * (wb // 2 + 1))
                        for ts in range(nsub):
                            bank1 = pY[ts + 4 * bsel]
                            bank2 = pY[2 + ts + 4 * bsel]
                            if fblk >= 2:
                                cx.wait(t, "sc",
                                        sc0 + nsub * (fblk - 2) + ts + 1)
                                cx.wait(t, "dv",
                                        dv0 + 2 * nsub * (fblk - 2) + 2 * ts + 1)
                            for dc in range(NCD):
                                ins = t.matmul(
                                    bank1[:],
                                    w1s_sb[ws][:, fi, dc * P:(dc + 1) * P],
                                    xn2w_sb[wslot][:, dc, ts * TW:(ts + 1) * TW],
                                    start=(dc == 0), stop=(dc == NCD - 1))
                            cx.mark(ins, "pe")
                            for dc in range(NCD):
                                ins = t.matmul(
                                    bank2[:],
                                    w2s_sb[ws][:, fi, dc * P:(dc + 1) * P],
                                    xn2w_sb[wslot][:, dc, ts * TW:(ts + 1) * TW],
                                    start=(dc == 0), stop=(dc == NCD - 1))
                            cx.mark(ins, "pe")
                assert cx.cnt["pe"] == pe0 + 2 * nsub * NCF
                pe_g_prev_end = cx.cnt["pe"]
                xn2w_done[wslot] = cx.cnt["pe"]

                @blk.scalar
                def _(s):
                    for fblk in range(NCF):
                        bsel = fblk % 2
                        for ts in range(nsub):
                            cx.wait(s, "pe", pe0 + 2 * nsub * fblk + 2 * ts + 1)
                            if fblk >= 1:
                                cx.wait(s, "dv",
                                        dv0 + 2 * nsub * (fblk - 1) + 2 * ts + 1)
                            cx.mark(s.activation(
                                s1t_sb[:, ts, :], pY[ts + 4 * bsel][:], AF.Silu,
                                bias=b1e_sb[eslot][:, fblk:fblk + 1]), "sc")
                assert cx.cnt["sc"] == sc0 + NSILU

                @blk.vector
                def _(v):
                    for fblk in range(NCF):
                        bsel = fblk % 2
                        for ts in range(nsub):
                            cx.wait(v, "pe", pe0 + 2 * nsub * fblk + 2 * ts + 2)
                            cx.wait(v, "sc", sc0 + nsub * fblk + ts + 1)
                            cx.wait(v, "dv")
                            cx.mark(v.scalar_tensor_tensor(
                                ygf_c_sb[:, ts, :], pY[2 + ts + 4 * bsel][:],
                                b2e_sb[eslot][:, fblk:fblk + 1], s1t_sb[:, ts, :],
                                ALU.add, ALU.mult), "dv")
                            cx.wait(v, "dv")
                            cx.mark(v.tensor_mul(
                                g_sb[:, fblk, ts * TW:(ts + 1) * TW],
                                ygf_c_sb[:, ts, :],
                                cwrep_sb[:, ts * TW:(ts + 1) * TW]), "dv")
                assert cx.cnt["dv"] == dv0 + NGOP
                dv_gall = cx.cnt["dv"]

                for ts in range(nsub):
                    py0 = cx.cnt["pe"]
                    sw3b = {k: cx.cnt[k] for k in ("hw30", "hw31")}

                    @blk.scalar
                    def _(s):
                        for wb in range(NCF // 4):
                            ws = wb % 2
                            if wb >= 2:
                                cx.wait(s, "pe", py0 + 4 * (wb - 1))
                            elif pe_y_prev_end > 0:
                                cx.wait(s, "pe", pe_y_prev_end)
                            cx.mark(s.dma_start(
                                out=w3s_sb[ws][:],
                                in_=w3b_sp[e, wb * 4 * P:(wb + 1) * 4 * P, :]
                                .rearrange("(f p) d -> p f d", p=P)),
                                "hw3%d" % ws, 16)

                    @blk.tensor
                    def _(t):
                        cx.wait(t, "dv", dv_gall)
                        cx.wait(t, "sc")
                        if ts > 0 or e > 0:
                            cx.wait(t, "dv")
                        if e == 0 and ts == 0 and w > 0:
                            cx.wait(t, "dv", dv_yacc_final)
                        for fblk in range(NCF):
                            wb = fblk // 4
                            ws = wb % 2
                            fi = fblk % 4
                            if fi == 0:
                                cx.wait(t, "hw3%d" % ws,
                                        sw3b["hw3%d" % ws] + 16 * (wb // 2 + 1))
                            for dm in range(NCD):
                                ins = t.matmul(
                                    pY[dm][:],
                                    w3s_sb[ws][:, fi, dm * P:(dm + 1) * P],
                                    g_sb[:, fblk, ts * TW:(ts + 1) * TW],
                                    start=(fblk == 0), stop=(fblk == NCF - 1))
                            cx.mark(ins, "pe")
                    pe_y_prev_end = cx.cnt["pe"]
                    assert pe_y_prev_end == py0 + NCF

                    @blk.vector
                    def _(v):
                        cx.wait(v, "pe", pe_y_prev_end)
                        for dm in range(NCD):
                            if e == 0:
                                cx.mark(v.tensor_copy(
                                    yacc_sb[:, dm, ts * TW:(ts + 1) * TW],
                                    pY[dm][:]), "dv")
                            else:
                                cx.mark(v.tensor_add(
                                    yacc_sb[:, dm, ts * TW:(ts + 1) * TW],
                                    yacc_sb[:, dm, ts * TW:(ts + 1) * TW],
                                    pY[dm][:]), "dv")
                dv_gall_prev = cx.cnt["dv"]

            @blk.sync
            def _(s):
                cx.wait(s, "scw1")
                cx.mark(s.dma_start(out=cwall_sb[:],
                                    in_=cw_sp[:, w * fw:(w + 1) * fw]), "scw1", 16)
            ld_cwall = cx.cnt["scw1"]

            dv_b3base = cx.cnt["dv"]
            pe_b3 = []

            @blk.tensor
            def _(t):
                cx.wait(t, "scw1", ld_cwall)
                cx.wait(t, "dv", dv_b3base)
                for ts in range(nsub):
                    for dm in range(NCD):
                        pe_b3.append(cx.mark(t.matmul(
                            pY[dm][:], b3a_sb[0:NE, dm * P:(dm + 1) * P],
                            cwall_sb[0:NE, ts * TW:(ts + 1) * TW],
                            start=True, stop=True), "pe"))

            @blk.vector
            def _(v):
                k = 0
                for ts in range(nsub):
                    for dm in range(NCD):
                        cx.wait(v, "pe", pe_b3[k])
                        cx.mark(v.tensor_add(
                            yacc_sb[:, dm, ts * TW:(ts + 1) * TW],
                            yacc_sb[:, dm, ts * TW:(ts + 1) * TW],
                            pY[dm][:]), "dv")
                        k += 1
            dv_yacc_done = cx.cnt["dv"]

            for tg in range(fw // P):
                tslot = tg % 2

                @blk.sync
                def _(s):
                    cx.wait(s, f"sld{tslot}")
                    cx.mark(s.dma_start(
                        out=x1g_sb[tslot][:],
                        in_=x1_sp[:, :, w * fw + tg * P:w * fw + (tg + 1) * P]),
                        f"sld{tslot}", 16)
                ld_x1g = cx.cnt[f"sld{tslot}"]

                @blk.vector
                def _(v):
                    cx.wait(v, f"sld{tslot}", ld_x1g)
                    cx.wait(v, "dv", dv_yacc_done)
                    cx.wait(v, "pe", pe_tr_prev)
                    for dm in range(NCD):
                        cx.mark(v.tensor_add(
                            oadd_sb[:, dm, :],
                            yacc_sb[:, dm, tg * P:(tg + 1) * P],
                            x1g_sb[tslot][:, dm, :]), "dv")
                dv_oadd = cx.cnt["dv"]

                pe_trs = []

                @blk.tensor
                def _(t):
                    for dm in range(NCD):
                        bank = pY[dm % 2]
                        cx.wait(t, "dv", dv_oadd - NCD + dm + 1)
                        if dm >= 2:
                            cx.wait(t, "dv", dv_oadd + dm - 1)
                        pe_trs.append(cx.mark(
                            t.transpose(bank[0:P, 0:P], oadd_sb[:, dm, :], idf[:]),
                            "pe"))
                pe_tr_prev = cx.cnt["pe"]

                @blk.vector
                def _(v):
                    cx.wait(v, "sst", dv_ostg[tslot])
                    for dm in range(NCD):
                        bank = pY[dm % 2]
                        cx.wait(v, "pe", pe_trs[dm])
                        cx.mark(v.tensor_copy(
                            ostg_sb[tslot][:, dm * P:(dm + 1) * P],
                            bank[0:P, 0:P]), "dv")
                dv_ostg_done = cx.cnt["dv"]

                @blk.sync
                def _(s):
                    cx.wait(s, "dv", dv_ostg_done)
                    cx.dma_seq(s, "sst",
                               y_out[w * fw + tg * P:w * fw + (tg + 1) * P, :],
                               ostg_sb[tslot][:])
                dv_ostg[tslot] = cx.cnt["sst"]
            dv_yacc_final = cx.cnt["dv"]

        ps4.close()
        s4.close()
        join()

        @blk.sync
        def _(s):
            for nm in DMA_SEMS:
                cx.wait(s, nm)

    return nc



def _layout_weights(i):
    W = {}
    W["ident_f"] = np.eye(P, dtype=np.float32)
    W["ident_b"] = np.eye(P, dtype=np.float32).astype(BFNP)
    W["w_in_l"] = np.ascontiguousarray(
        i["L0_attn_W_in"].reshape(NCD, P, NCF, P).transpose(1, 2, 0, 3)
        .reshape(P, NCF, NCD * P))
    W["wd_l"] = np.ascontiguousarray(
        i["L0_attn_Wd"].reshape(NCI, P, NCI, P).transpose(1, 2, 0, 3)
        .reshape(P, NCI, NCI * P))
    W["w_out_l"] = np.ascontiguousarray(
        i["L0_attn_W_out"].reshape(NCI, P, NCD, P).transpose(1, 2, 0, 3)
        .reshape(P, NCD * NCI * P))
    W["bt_l"] = np.ascontiguousarray(
        i["L0_attn_B_mat"].T.reshape(NCI, P, NST).transpose(1, 0, 2)
        .reshape(P, NCI * NST))
    W["ct_l"] = np.ascontiguousarray(i["L0_attn_C_mat"].T)
    W["a_log_r"] = i["L0_attn_A_log"].reshape(1, NST)
    W["d_vec_l"] = np.ascontiguousarray(i["L0_attn_D_vec"].reshape(NCI, P).T)
    W["conv_w_l"] = np.ascontiguousarray(
        i["L0_attn_conv_w"][:, 0, :].reshape(NCI, P, 4).transpose(1, 0, 2)
        .reshape(P, NCI * 4))
    cd = np.zeros((NCI, 4, P, P), np.float32)
    cw = i["L0_attn_conv_w"][:, 0, :]
    idx = np.arange(P)
    for c in range(NCI):
        for kk in range(4):
            cd[c, kk, idx, idx] = cw[c * P + idx, kk]
    W["conv_diag_l"] = np.ascontiguousarray(
        cd.transpose(2, 0, 1, 3).reshape(P, NCI * 4 * P))
    W["conv_b_l"] = np.ascontiguousarray(i["L0_attn_conv_b"].reshape(NCI, P).T)
    W["bd_l"] = np.ascontiguousarray(i["L0_attn_bd"].reshape(NCI, P).T)
    W["gamma_l"] = np.ascontiguousarray(i["L0_attn_ln_gamma"].reshape(NCD, P).T)
    W["beta_l"] = np.ascontiguousarray(i["L0_attn_ln_beta"].reshape(NCD, P).T)
    W["wr_l"] = np.ascontiguousarray(
        i["L0_ffn_Wr"].reshape(NCD, P, NE).transpose(1, 0, 2).reshape(P, NCD * NE))
    W["br_r"] = i["L0_ffn_br"].reshape(1, NE)
    W["ones_r"] = np.ones((1, P), np.float32)
    W["consts"] = np.tile(np.array([[EPS, RDI, -0.5, -1.0, 1.0, 0.5, 0.0, 0.0]], np.float32), (P, 1))
    W["w1_l"] = np.ascontiguousarray(
        i["L0_ffn_W1"].reshape(NE, NCD, P, NCF, P).transpose(0, 3, 2, 1, 4)
        .reshape(NE, NCF, P, NCD * P))
    W["w2_l"] = np.ascontiguousarray(
        i["L0_ffn_W2"].reshape(NE, NCD, P, NCF, P).transpose(0, 3, 2, 1, 4)
        .reshape(NE, NCF, P, NCD * P))
    W["w3_l"] = np.ascontiguousarray(i["L0_ffn_W3"])
    W["b1_l"] = np.ascontiguousarray(
        i["L0_ffn_b1"].reshape(NE, NCF, P).transpose(0, 2, 1))
    W["b2_l"] = np.ascontiguousarray(
        i["L0_ffn_b2"].reshape(NE, NCF, P).transpose(0, 2, 1))
    W["b3_l"] = np.ascontiguousarray(i["L0_ffn_b3"])
    out = {}
    for k, v in W.items():
        v = np.asarray(v)
        out[k] = v if v.dtype == BFNP else v.astype(np.float32)
    return out


def make_in_maps(inputs, n_own=4096, n_ext=8192):
    x = np.asarray(inputs["L0_attn_x"], np.float32)
    W = _layout_weights({k: np.asarray(v, np.float32) for k, v in inputs.items()})
    in_maps = []
    for c in range(8):
        b, hf = c // 2, c % 2
        l0 = hf * n_own
        xr = np.zeros((3 + n_ext, DM), np.float32)
        lo = l0 - (n_ext - n_own) - 3
        src_lo = max(lo, 0)
        src_hi = l0 + n_own
        xr[src_lo - lo:src_hi - lo, :] = x[b, src_lo:src_hi, :]
        m = np.ones((NST, n_ext), np.float32)
        if hf == 0:
            m[:, :n_ext - n_own] = 0.0
        m3 = np.full((P, 3), 1.0 if hf == 1 else 0.0, np.float32)
        im = {"x": xr, "bx_mask": m, "m3": m3, "m3b": m3.astype(BFNP)}
        im.update(W)
        in_maps.append(im)
    return in_maps


def kernel(**inputs):
    nc = build()
    in_maps = make_in_maps(inputs)
    res = run_bass_kernel_spmd(nc, in_maps, core_ids=list(range(8)))
    out = np.zeros((4, 8192, DM), np.float32)
    for c in range(8):
        b, hf = c // 2, c % 2
        out[b, hf * 4096:(hf + 1) * 4096, :] = np.asarray(res.results[c]["y"])
    return out
